# version 01
# baseline (speedup 1.0000x reference)
import numpy as np
import ml_dtypes
from contextlib import ExitStack

import concourse.bass as bass
import concourse.mybir as mybir
from concourse.bass_core_rust import IndirectOffsetOnAxis

F32 = mybir.dt.float32
BF16 = mybir.dt.bfloat16
U32 = mybir.dt.uint32
AF = mybir.ActivationFunctionType
OP = mybir.AluOpType
AX = mybir.AxisListType

S = 16384
D = 1024
NCORES = 8
TOK = S // NCORES
NT = TOK // 128
NH = 16
DH = 64
STRIDES = [1, 1, 2, 2, 2, 2, 4, 4, 4, 4, 8, 8, 8, 8, 16, 16]
NE = 16
CAP = 1280
EPC = 2
SLOTC = EPC * CAP
NSLOT = NE * CAP
DFF = 4096
LN_EPS = 1e-5
EXP_SCALE = 0.125
EXP_SHIFT = -16.0
BIGIDX = 99999.0

KV_OFF = []
_o = 0
for _st in STRIDES:
    KV_OFF.append(_o)
    _o += TOK // _st
KV_TOT = _o
VGROUPS = []
_h = 0
while _h < NH:
    _st = STRIDES[_h]
    _n = 1
    while _h + _n < NH and STRIDES[_h + _n] == _st:
        _n += 1
    VGROUPS.append((_h, _n, _st))
    _h += _n


class Build:
    ENGS = ("sync", "scalar", "vector", "tensor", "gpsimd")
    ESEM = {"scalar": "ss", "vector": "sv", "tensor": "st"}

    def __init__(self, nc, es):
        self.nc = nc
        self.es = es
        self.q = {e: [] for e in self.ENGS}
        self.sem = {}
        self.cnt = {}
        self.unord = {}
        self.inc = {}
        for name in ("ss", "sv", "st"):
            self._mksem(name, 1, False)
        self.lastw = {}
        self.readers = {}
        self.wm = {e: {} for e in self.ENGS}
        self._guards = {}
        self._free_deps = []

    def _mksem(self, name, inc, unord):
        self.sem[name] = self.es.enter_context(self.nc.semaphore(name))
        self.cnt[name] = 0
        self.inc[name] = inc
        self.unord[name] = unord

    def stream(self, name):
        if name not in self.sem:
            self._mksem(name, 16, True)
        return name

    def ccsem(self, name):
        if name not in self.sem:
            self._mksem(name, 1, False)
        return name

    def _wait(self, eng, sem, val):
        if val <= 0 or self.wm[eng].get(sem, 0) >= val:
            return
        self.wm[eng][sem] = val
        self.q[eng].append(lambda e, s=self.sem[sem], v=val: e.wait_ge(s, v))

    def _wait_dep(self, eng, dep):
        sem, val = dep
        if self.unord[sem]:
            val = self.cnt[sem]
        self._wait(eng, sem, val)

    def op(self, eng, fn, reads=(), writes=(), stream=None, sem=None):
        for r in reads:
            lw = self.lastw.get(r)
            if lw is not None:
                self._wait_dep(eng, lw)
        for w in writes:
            lw = self.lastw.get(w)
            if lw is not None:
                self._wait_dep(eng, lw)
            for dep in self.readers.get(w, ()):
                self._wait_dep(eng, dep)
        if stream is not None:
            semname = self.stream(stream)
        elif sem is not None:
            semname = sem
        else:
            semname = self.ESEM[eng]
        amt = self.inc[semname]
        self.cnt[semname] += amt
        val = self.cnt[semname]
        self.q[eng].append(
            lambda e, f=fn, s=self.sem[semname], a=amt: f(e).then_inc(s, a))
        for r in reads:
            self.readers.setdefault(r, []).append((semname, val))
        for w in writes:
            self.lastw[w] = (semname, val)
            self.readers[w] = []
        return val

    def raw_wait(self, eng, semname, val):
        self._wait(eng, semname, val)

    def raw(self, eng, fn):
        self.q[eng].append(fn)

    def sbuf(self, name, shape, dtype):
        g = self.nc.sbuf_tensor(name, shape, dtype)
        h = g.__enter__()
        self._guards[name] = g
        self._alloc_order = getattr(self, "_alloc_order", [])
        self._alloc_order.append(name)
        pool = list(getattr(self, "_free_deps", []))
        if pool:
            self.readers.setdefault(name, []).extend(pool)
        return h

    def free(self, *names):
        order = {n: i for i, n in enumerate(getattr(self, "_alloc_order", []))}
        names = sorted(names, key=lambda n: -order.get(n, 0))
        for name in names:
            lw = self.lastw.get(name)
            if lw is not None:
                self._free_deps.append(lw)
            self._free_deps.extend(self.readers.get(name, ()))
            g = self._guards.pop(name, None)
            if g is not None:
                g.__exit__(None, None, None)

    def close(self):
        nc = self.nc
        with nc.Block() as block:
            @block.sync
            def _(e):
                for f in self.q["sync"]:
                    f(e)

            @block.scalar
            def _(e):
                for f in self.q["scalar"]:
                    f(e)

            @block.vector
            def _(e):
                for f in self.q["vector"]:
                    f(e)

            @block.tensor
            def _(e):
                for f in self.q["tensor"]:
                    f(e)

            @block.gpsimd
            def _(e):
                for f in self.q["gpsimd"]:
                    f(e)
        return nc


def build_nc(debug_upto=0, upto_phase=99, nt_limit=NT, p1sec=6):
    nc = bass.Bass("TRN2", target_bir_lowering=False)
    es = ExitStack()
    B = Build(nc, es)

    def din(name, shape, dtype=F32):
        return nc.dram_tensor(name, shape, dtype, kind="ExternalInput").ap()

    def dint(name, shape, dtype):
        return nc.dram_tensor(name, shape, dtype).ap()

    def dout(name, shape, dtype=F32):
        return nc.dram_tensor(name, shape, dtype, kind="ExternalOutput").ap()

    x_in = din("x", [TOK, D])
    wqkv_in = din("wqkv", [D, 3 * D])
    bq64_in = din("bq64", [64, 48])
    wout_in = din("wout", [D, D])
    w1_in = din("w1", [D, DFF])
    b1_in = din("b1i", [128, 32])
    w2_in = din("w2", [DFF, D])
    gwT_in = din("gwT", [D, NE])
    we1_in = din("we1", [EPC, D, DFF])
    we2_in = din("we2", [EPC, DFF, D])
    identb_in = din("identb", [128, 128], BF16)
    identf_in = din("identf", [128, 128], F32)
    ltinc_in = din("ltinc", [128, 128])
    oneshift_in = din("oneshift", [128, 255])
    cpfx_in = din("cpfx", [128, NT * 128])
    iota16_in = din("iota16", [128, 16])
    iotarev_in = din("iotarev", [128, 16])
    prevmask_in = din("prevmask", [8, 1])
    slotb_in = din("slotb", [128, 1])

    y_out = dout("y", [TOK, D])

    agin_cnt = dint("agin_cnt", [128, NE], F32)
    agout_cnt = dint("agout_cnt", [8 * 128, NE], F32)
    agin_xn = dint("agin_xn", [TOK, D], BF16)
    agout_xn = dint("agout_xn", [S, D], BF16)
    agin_kt = dint("agin_kt", [64, KV_TOT], BF16)
    agout_kt = dint("agout_kt", [8 * 64, KV_TOT], BF16)
    agin_v = dint("agin_v", [KV_TOT, 64], BF16)
    agout_v = dint("agout_v", [8 * KV_TOT, 64], BF16)
    agin_gx = dint("agin_gx", [NT, 128], F32)
    agout_gx = dint("agout_gx", [8 * NT, 128], F32)
    agin_eo = dint("agin_eo", [SLOTC, D], BF16)
    agout_eo = dint("agout_eo", [NSLOT, D], BF16)

    qt_spill = dint("qt_spill", [64, NH * TOK], BF16)
    attn_spill = dint("attn_spill", [64, NH * TOK], BF16)
    h_spill = dint("h_spill", [TOK, D], F32)
    br_spill = dint("br_spill", [TOK, D], F32)
    disp_dram = dint("disp_dram", [SLOTC, D], BF16)

    dbg = {}
    if debug_upto >= 1:
        dbg["xn"] = dout("dbg_xn", [TOK, D])
        dbg["logit"] = dout("dbg_logit", [128, NT * 16])
        dbg["mask"] = dout("dbg_mask", [128, NT * 16])
        dbg["cnt"] = dout("dbg_cnt", [128, 16])
    if debug_upto >= 2:
        dbg["gidx"] = dout("dbg_gidx", [128, NT])
        dbg["w"] = dout("dbg_w", [128, NT])
        dbg["kt"] = dout("dbg_kt", [64, KV_TOT])
        dbg["v"] = dout("dbg_v", [KV_TOT, 64])
        dbg["qt"] = dout("dbg_qt", [64, NH * 128])
        dbg["base"] = dout("dbg_base", [1, NE])
        dbg["cnt8"] = dout("dbg_cnt8", [8, NE])
        dbg["rank0"] = dout("dbg_rank0", [128, 16])
        dbg["tot"] = dout("dbg_tot", [1, NE])
    if debug_upto >= 3:
        dbg["attn"] = dout("dbg_attn", [64, NH * 128])
    if debug_upto >= 4:
        dbg["h"] = dout("dbg_h", [TOK, D])
        dbg["br"] = dout("dbg_br", [TOK, D])
    if debug_upto >= 5:
        dbg["idx"] = dout("dbg_idx", [128, 128])
        dbg["disp0"] = dout("dbg_disp0", [128, D])
        dbg["eo"] = dout("dbg_eo", [128, 2 * D])

    sbA = nc.alloc_sbuf_tensor

    identb = sbA("identb_s", [128, 128], BF16)
    identf = sbA("identf_s", [128, 128], F32)
    ltinc = sbA("ltinc_s", [128, 128], F32)
    oneshift = sbA("oneshift_s", [128, 255], F32)
    cpfx = sbA("cpfx_s", [128, NT * 128], F32)
    iota16 = sbA("iota16_s", [128, 16], F32)
    iotarev = sbA("iotarev_s", [128, 16], F32)
    prevmask = sbA("prevmask_s", [8, 1], F32)
    slotb = sbA("slotb_s", [128, 1], F32)
    onescol = sbA("onescol", [128, 1], F32)
    onesrow = sbA("onesrow", [1, 128], F32)
    ones64 = sbA("ones64", [128, 64], F32)
    c9 = sbA("c9", [128, 1], F32)
    epsc = sbA("epsc", [128, 1], F32)
    bq64 = sbA("bq64_s", [64, 48], F32)
    b1sb = sbA("b1sb_s", [128, 32], F32)
    gwT = sbA("gwT_s", [128, 8, 16], F32)
    rstd1 = sbA("rstd1", [128, NT], F32)
    negmr1 = sbA("negmr1", [128, NT], F32)
    logit_sb = sbA("logit_sb", [128, NT * 16], F32)
    mask_sb = sbA("mask_sb", [128, NT * 16], F32)
    e_sb = sbA("e_sb", [128, NT], F32)
    wpre_sb = sbA("wpre_sb", [128, NT], F32)
    counts_sb = sbA("counts_sb", [128, 16], F32)
    total_sb = sbA("total_sb", [1, 16], F32)
    cnt8_sb = sbA("cnt8_sb", [8, NE], F32)
    base_sb = sbA("base_sb", [1, NE], F32)
    gidx_sb = sbA("gidx_sb", [128, NT], F32)
    w_sb = sbA("w_sb", [128, NT], F32)
    gidxu = sbA("gidxu", [128, NT], U32)
    gxT_sb = sbA("gxT_sb", [NT, 128], F32)
    idxall_f = sbA("idxall_f", [128, 128], F32)
    idxall_u = sbA("idxall_u", [128, 128], U32)
    zero_bf = sbA("zero_bf", [128, D], BF16)
    t16a = sbA("t16a", [128, 16], F32)
    t16b = sbA("t16b", [128, 16], F32)
    t1a = sbA("t1a", [128, 1], F32)
    t1b = sbA("t1b", [128, 1], F32)
    t1c = sbA("t1c", [128, 1], F32)
    t1d = sbA("t1d", [128, 1], F32)
    t1e = sbA("t1e", [128, 1], F32)
    sum1 = sbA("sum1", [128, 1], F32)
    sumsq = sbA("sumsq", [128, 1], F32)
    rstd2 = sbA("rstd2", [128, 1], F32)
    negmr2 = sbA("negmr2", [128, 1], F32)

    pb = [nc.alloc_psum_tensor(f"pb{i}", [128, 512], F32) for i in range(6)]
    psmall = nc.alloc_psum_tensor("psmall", [128, 512], F32)
    ptrb = nc.alloc_psum_tensor("ptrb", [128, 1024], BF16)

    for (dst, src, nm) in [
        (identb, identb_in, "identb"), (identf, identf_in, "identf"),
        (ltinc, ltinc_in, "ltinc"), (oneshift, oneshift_in, "oneshift"),
        (cpfx, cpfx_in, "cpfx"), (iota16, iota16_in, "iota16"),
        (iotarev, iotarev_in, "iotarev"), (prevmask, prevmask_in, "prevmask"),
        (slotb, slotb_in, "slotb"), (bq64, bq64_in, "bq64"), (b1sb, b1_in, "b1sb"),
    ]:
        B.op("sync", lambda e, d=dst, s=src: e.dma_start(out=d.ap(), in_=s),
             writes=[nm], stream="d_const")
    B.op("sync", lambda e: e.dma_start(
        out=gwT.ap(), in_=gwT_in.rearrange("(c p) e -> p c e", p=128)),
        writes=["gwT"], stream="d_const")
    B.op("vector", lambda e: e.memset(onescol.ap(), 1.0), writes=["onescol"])
    B.op("vector", lambda e: e.memset(onesrow.ap(), 1.0), writes=["onesrow"])
    B.op("vector", lambda e: e.memset(ones64.ap(), 1.0), writes=["ones64"])
    B.op("vector", lambda e: e.memset(c9.ap(), BIGIDX), writes=["c9"])
    B.op("vector", lambda e: e.memset(epsc.ap(), LN_EPS), writes=["epsc"])
    B.op("vector", lambda e: e.memset(zero_bf.ap(), 0.0), writes=["zero_bf"])

    if upto_phase <= 0:
        return finish(B, nc, dict(locals()))

    xbuf = [B.sbuf(f"xbuf{i}", [128, D], F32) for i in range(2)]
    xsq = B.sbuf("xsq", [128, D], F32)
    xn32 = B.sbuf("xn32", [128, D], F32)
    xnbf = B.sbuf("xnbf", [128, D], BF16)
    xnT32 = B.sbuf("xnT32", [128, D], F32)
    xnT_sb = B.sbuf("xnT_sb", [128, 8 * TOK], BF16)

    for t in range(nt_limit):
        xb, xbn = xbuf[t % 2], f"xbuf{t % 2}"
        B.op("sync", lambda e, t=t, xb=xb: e.dma_start(
            out=xb.ap(), in_=x_in[128 * t:128 * (t + 1), :]), writes=[xbn], stream="d_x")
        B.op("vector", lambda e, xb=xb: e.reduce_sum(sum1.ap(), xb.ap(), axis=AX.X),
             reads=[xbn], writes=["sum1"])
        B.op("scalar", lambda e, xb=xb: e.activation(
            out=xsq.ap(), in_=xb.ap(), func=AF.Square, accum_out=sumsq.ap()),
            reads=[xbn], writes=["xsq", "sumsq"])
        B.op("vector", lambda e: e.tensor_scalar(t1a.ap(), sum1.ap(), 1.0 / D, None, OP.mult),
             reads=["sum1"], writes=["t1a"])
        B.op("vector", lambda e: e.tensor_scalar(t1b.ap(), sumsq.ap(), 1.0 / D, None, OP.mult),
             reads=["sumsq"], writes=["t1b"])
        B.op("vector", lambda e: e.tensor_scalar(t1c.ap(), t1a.ap(), t1a.ap(), -1.0, OP.mult, OP.mult),
             reads=["t1a"], writes=["t1c"])
        B.op("vector", lambda e: e.tensor_add(t1d.ap(), t1b.ap(), t1c.ap()),
             reads=["t1b", "t1c"], writes=["t1d"])
        B.op("scalar", lambda e: e.activation(
            out=t1e.ap(), in_=t1d.ap(), func=AF.Sqrt, bias=epsc.ap()),
            reads=["t1d", "epsc"], writes=["t1e"])
        B.op("vector", lambda e, t=t: e.reciprocal(rstd1.ap()[:, t:t + 1], t1e.ap()),
             reads=["t1e"], writes=["rstd1"])
        B.op("vector", lambda e, t=t: e.tensor_scalar(
            negmr1.ap()[:, t:t + 1], t1a.ap(), rstd1.ap()[:, t:t + 1], -1.0,
            OP.mult, OP.mult), reads=["t1a", "rstd1"], writes=["negmr1"])
        B.op("vector", lambda e, xb=xb, t=t: e.tensor_scalar(
            xn32.ap(), xb.ap(), rstd1.ap()[:, t:t + 1], negmr1.ap()[:, t:t + 1],
            OP.mult, OP.add), reads=[xbn, "rstd1", "negmr1"], writes=["xn32"])
        B.op("scalar", lambda e, xb=xb, t=t: e.activation(
            out=xnbf.ap(), in_=xb.ap(), func=AF.Identity,
            bias=negmr1.ap()[:, t:t + 1], scale=rstd1.ap()[:, t:t + 1]),
            reads=[xbn, "rstd1", "negmr1"], writes=["xnbf"])
        B.op("sync", lambda e, t=t: e.dma_start(
            out=agin_xn[128 * t:128 * (t + 1), :], in_=xnbf.ap()),
            reads=["xnbf"], writes=["agin_xn"], stream="d_xn")
        if debug_upto >= 1:
            B.op("gpsimd", lambda e, t=t: e.dma_start(
                out=dbg["xn"][128 * t:128 * (t + 1), :], in_=xnbf.ap()),
                reads=["xnbf"], stream="d_dbg")
        if p1sec < 2:
            continue
        for c in range(8):
            slot = c % 2
            pn = f"pb{slot}"
            B.op("tensor", lambda e, c=c, slot=slot: e.transpose(
                pb[slot].ap()[:, 0:128],
                xn32.ap()[:, 128 * c:128 * (c + 1)], identf.ap()),
                reads=["xn32", "identf"], writes=[pn])
            if p1sec >= 22 or p1sec < 20:
                B.op("vector", lambda e, c=c, slot=slot: e.tensor_copy(
                    xnT32.ap()[:, 128 * c:128 * (c + 1)],
                    pb[slot].ap()[:, 0:128]),
                    reads=[pn], writes=["xnT32"])
            if p1sec >= 21 or p1sec < 20:
                B.op("scalar", lambda e, c=c, t=t, slot=slot: e.activation(
                    out=xnT_sb.ap()[:, c * TOK + 128 * t: c * TOK + 128 * (t + 1)],
                    in_=(xnT32.ap()[:, 128 * c:128 * (c + 1)] if (p1sec >= 22 or p1sec < 20)
                         else pb[slot].ap()[:, 0:128]), func=AF.Copy),
                    reads=["xnT32" if (p1sec >= 22 or p1sec < 20) else pn], writes=["xnT_sb"])
        if (p1sec if p1sec < 20 else 2) < 3:
            continue
        for c in range(8):
            B.op("tensor", lambda e, c=c: e.matmul(
                psmall.ap()[:, 0:16], xnT32.ap()[:, 128 * c:128 * (c + 1)],
                gwT.ap()[:, c, :], start=(c == 0), stop=(c == 7)),
                reads=["xnT32", "gwT"] + (["psmall"] if c == 0 else []),
                writes=["psmall"])
        B.op("vector", lambda e, t=t: e.tensor_copy(
            logit_sb.ap()[:, 16 * t:16 * (t + 1)], psmall.ap()[:, 0:16]),
            reads=["psmall"], writes=["logit_sb"])
        if (p1sec if p1sec < 20 else 2) < 4:
            continue
        B.op("vector", lambda e, t=t: e.reduce_max(
            t1a.ap(), logit_sb.ap()[:, 16 * t:16 * (t + 1)], axis=AX.X),
            reads=["logit_sb"], writes=["t1a"])
        B.op("vector", lambda e, t=t: e.tensor_reduce(
            t1b.ap(), logit_sb.ap()[:, 16 * t:16 * (t + 1)], axis=AX.X,
            op=OP.max, negate=True), reads=["logit_sb"], writes=["t1b"])
        B.op("scalar", lambda e, t=t: e.activation(
            out=t16a.ap(), in_=logit_sb.ap()[:, 16 * t:16 * (t + 1)], func=AF.Exp,
            bias=t1b.ap(), accum_out=t1c.ap()),
            reads=["logit_sb", "t1b"], writes=["t16a", "t1c"])
        B.op("vector", lambda e, t=t: e.reciprocal(wpre_sb.ap()[:, t:t + 1], t1c.ap()),
             reads=["t1c"], writes=["wpre_sb"])
        B.op("vector", lambda e, t=t: e.tensor_scalar(
            t16b.ap(), logit_sb.ap()[:, 16 * t:16 * (t + 1)], t1a.ap(), None,
            OP.is_equal), reads=["logit_sb", "t1a"], writes=["t16b"])
        B.op("vector", lambda e: e.tensor_mul(t16b.ap(), t16b.ap(), iotarev.ap()),
             reads=["t16b", "iotarev"], writes=["t16b"])
        B.op("vector", lambda e: e.reduce_max(t1d.ap(), t16b.ap(), axis=AX.X),
             reads=["t16b"], writes=["t1d"])
        B.op("vector", lambda e, t=t: e.tensor_scalar(
            e_sb.ap()[:, t:t + 1], t1d.ap(), -1.0, 16.0, OP.mult, OP.add),
            reads=["t1d"], writes=["e_sb"])
        B.op("vector", lambda e, t=t: e.tensor_scalar(
            mask_sb.ap()[:, 16 * t:16 * (t + 1)], iota16.ap(), e_sb.ap()[:, t:t + 1],
            None, OP.is_equal), reads=["iota16", "e_sb"], writes=["mask_sb"])
        B.op("tensor", lambda e, t=t: e.matmul(
            pb[4].ap()[:, 0:16], oneshift.ap()[:, 127 - t:255 - t],
            mask_sb.ap()[:, 16 * t:16 * (t + 1)], start=(t == 0), stop=(t == nt_limit - 1)),
            reads=["mask_sb", "oneshift"] + (["pb4"] if t == 0 else []),
            writes=["pb4"])
        B.op("tensor", lambda e, t=t: e.matmul(
            pb[3].ap()[0:1, 0:16], onescol.ap(),
            mask_sb.ap()[:, 16 * t:16 * (t + 1)], start=(t == 0), stop=(t == nt_limit - 1)),
            reads=["mask_sb", "onescol"] + (["pb3"] if t == 0 else []),
            writes=["pb3"])

    if p1sec >= 5 and nt_limit == NT:
        B.op("vector", lambda e: e.tensor_copy(counts_sb.ap(), pb[4].ap()[:, 0:16]),
             reads=["pb4"], writes=["counts_sb"])
        B.op("vector", lambda e: e.tensor_copy(total_sb.ap(), pb[3].ap()[0:1, 0:16]),
             reads=["pb3"], writes=["total_sb"])
        B.op("vector", lambda e: e.memset(t16a.ap(), 7777.0), reads=[], writes=["t16a"])
        B.op("sync", lambda e: e.dma_start(out=agin_cnt[1:128, :], in_=t16a.ap()[1:128, :]),
             reads=["t16a"], writes=["agin_cnt"], stream="d_cnt")
        B.op("sync", lambda e: e.dma_start(out=agin_cnt[0:1, :], in_=total_sb.ap()),
             reads=["total_sb"], writes=["agin_cnt"], stream="d_cnt")
    if debug_upto >= 1 and p1sec >= 5 and nt_limit == NT:
        B.op("sync", lambda e: e.dma_start(out=dbg["cnt"], in_=counts_sb.ap()),
             reads=["counts_sb"], stream="d_dbg2")
        B.op("sync", lambda e: e.dma_start(out=dbg["logit"], in_=logit_sb.ap()),
             reads=["logit_sb"], stream="d_dbg2")
        B.op("sync", lambda e: e.dma_start(out=dbg["mask"], in_=mask_sb.ap()),
             reads=["mask_sb"], stream="d_dbg2")
    if p1sec >= 6 and nt_limit == NT:
        B.op("gpsimd", lambda e: e.collective_compute(
            "AllGather", OP.bypass, replica_groups=[list(range(NCORES))],
            ins=[agin_cnt.opt()], outs=[agout_cnt.opt()]),
            reads=["agin_cnt"], writes=["agout_cnt"], sem=B.ccsem("scc0"))

    if upto_phase <= 1:
        return finish(B, nc, dict(locals()))

    wqsg = [B.sbuf(f"wqsg{i}", [128, 8, 512], BF16) for i in range(2)]
    qth = [B.sbuf(f"qth{i}", [64, TOK], BF16) for i in range(2)]
    wkh = [B.sbuf(f"wkh{i}", [128, 8, 64], BF16) for i in range(2)]
    kevac = [B.sbuf(f"kevac{i}", [64, 512], BF16) for i in range(2)]
    wvg = [B.sbuf(f"wvg{i}", [128, 8, 256], BF16) for i in range(2)]
    vevac = [B.sbuf(f"vevac{i}", [128, 256], BF16) for i in range(2)]

    for sgi in range(2):
        wq, wqn = wqsg[sgi % 2], f"wqsg{sgi % 2}"
        for c in range(8):
            B.op("gpsimd", lambda e, sgi=sgi, c=c, wq=wq: e.dma_start(
                out=wq.ap()[:, c, :],
                in_=wqkv_in[128 * c:128 * (c + 1), 512 * sgi:512 * (sgi + 1)]),
                writes=[wqn], stream="d_wq")
        for hh in range(8):
            h = 8 * sgi + hh
            qb, qbn = qth[h % 2], f"qth{h % 2}"
            for n in range(4):
                pbank = n % 2
                pn = f"pb{pbank}"
                for c in range(8):
                    B.op("tensor", lambda e, c=c, hh=hh, n=n, wq=wq, pbank=pbank: e.matmul(
                        pb[pbank].ap()[0:64, :],
                        wq.ap()[:, c, 64 * hh:64 * (hh + 1)],
                        xnT_sb.ap()[:, c * TOK + 512 * n:c * TOK + 512 * (n + 1)],
                        start=(c == 0), stop=(c == 7)),
                        reads=[wqn, "xnT_sb"] + ([pn] if c == 0 else []),
                        writes=[pn])
                B.op("scalar", lambda e, h=h, n=n, qb=qb, pbank=pbank: e.activation(
                    out=qb.ap()[:, 512 * n:512 * (n + 1)],
                    in_=pb[pbank].ap()[0:64, :], func=AF.Identity,
                    bias=bq64.ap()[:, h:h + 1]),
                    reads=[pn, "bq64"], writes=[qbn])
            B.op("sync", lambda e, h=h, qb=qb: e.dma_start(
                out=qt_spill[:, TOK * h:TOK * (h + 1)], in_=qb.ap()),
                reads=[qbn], writes=["qt_spill"], stream="d_qt")
            if debug_upto >= 2:
                B.op("gpsimd", lambda e, h=h, qb=qb: e.dma_start(
                    out=dbg["qt"][:, 128 * h:128 * (h + 1)], in_=qb.ap()[:, 0:128]),
                    reads=[qbn], stream="d_dbg")

    for h in range(NH):
        stv = STRIDES[h]
        L = TOK // stv
        wk, wkn = wkh[h % 2], f"wkh{h % 2}"
        for c in range(8):
            B.op("gpsimd", lambda e, h=h, c=c, wk=wk: e.dma_start(
                out=wk.ap()[:, c, :],
                in_=wqkv_in[128 * c:128 * (c + 1), D + 64 * h:D + 64 * (h + 1)]),
                writes=[wkn], stream="d_wk")
        nt_ = (L + 511) // 512
        for n in range(nt_):
            cnt_ = min(512, L - 512 * n)
            kb, kbn = kevac[n % 2], f"kevac{n % 2}"
            pn = "pb0"
            for c in range(8):
                B.op("tensor", lambda e, c=c, n=n, cnt_=cnt_, stv=stv, wk=wk: e.matmul(
                    pb[0].ap()[0:64, 0:cnt_], wk.ap()[:, c, :],
                    bass.AP(xnT_sb, c * TOK + 512 * n * stv, [[8 * TOK, 128], [stv, cnt_]]),
                    start=(c == 0), stop=(c == 7)),
                    reads=[wkn, "xnT_sb"] + ([pn] if c == 0 else []),
                    writes=[pn])
            B.op("scalar", lambda e, cnt_=cnt_, kb=kb: e.activation(
                out=kb.ap()[:, 0:cnt_], in_=pb[0].ap()[0:64, 0:cnt_], func=AF.Copy),
                reads=[pn], writes=[kbn])
            B.op("sync", lambda e, h=h, n=n, cnt_=cnt_, kb=kb: e.dma_start(
                out=agin_kt[:, KV_OFF[h] + 512 * n:KV_OFF[h] + 512 * n + cnt_],
                in_=kb.ap()[:, 0:cnt_]),
                reads=[kbn], writes=["agin_kt"], stream="d_kt")
            if debug_upto >= 2:
                B.op("gpsimd", lambda e, h=h, n=n, cnt_=cnt_, kb=kb: e.dma_start(
                    out=dbg["kt"][:, KV_OFF[h] + 512 * n:KV_OFF[h] + 512 * n + cnt_],
                    in_=kb.ap()[:, 0:cnt_]), reads=[kbn], stream="d_dbg")

    for gi, (h0, gh, stv) in enumerate(VGROUPS):
        L = TOK // stv
        wv, wvn = wvg[gi % 2], f"wvg{gi % 2}"
        gn = 64 * gh
        for c in range(8):
            B.op("gpsimd", lambda e, h0=h0, gn=gn, c=c, wv=wv: e.dma_start(
                out=wv.ap()[:, c, 0:gn],
                in_=wqkv_in[128 * c:128 * (c + 1), 2 * D + 64 * h0:2 * D + 64 * h0 + gn]),
                writes=[wvn], stream="d_wv")
        for tl in range(L // 128):
            vb, vbn = vevac[tl % 2], f"vevac{tl % 2}"
            pn = "pb1"
            for c in range(8):
                B.op("tensor", lambda e, c=c, tl=tl, stv=stv, gn=gn, wv=wv: e.matmul(
                    pb[1].ap()[:, 0:gn],
                    bass.AP(xnT_sb, c * TOK + 128 * tl * stv, [[8 * TOK, 128], [stv, 128]]),
                    wv.ap()[:, c, 0:gn], start=(c == 0), stop=(c == 7)),
                    reads=[wvn, "xnT_sb"] + ([pn] if c == 0 else []),
                    writes=[pn])
            B.op("scalar", lambda e, gn=gn, vb=vb: e.activation(
                out=vb.ap()[:, 0:gn], in_=pb[1].ap()[:, 0:gn], func=AF.Copy),
                reads=[pn], writes=[vbn])
            for hh in range(gh):
                h = h0 + hh
                B.op("sync", lambda e, h=h, hh=hh, tl=tl, vb=vb: e.dma_start(
                    out=agin_v[KV_OFF[h] + 128 * tl:KV_OFF[h] + 128 * (tl + 1), :],
                    in_=vb.ap()[:, 64 * hh:64 * (hh + 1)]),
                    reads=[vbn], writes=["agin_v"], stream="d_v")
                if debug_upto >= 2:
                    B.op("gpsimd", lambda e, h=h, hh=hh, tl=tl, vb=vb: e.dma_start(
                        out=dbg["v"][KV_OFF[h] + 128 * tl:KV_OFF[h] + 128 * (tl + 1), :],
                        in_=vb.ap()[:, 64 * hh:64 * (hh + 1)]), reads=[vbn], stream="d_dbg")

    B.op("sync", lambda e: e.dma_start(out=cnt8_sb.ap(), in_=bass.AP(agout_cnt.tensor, 0, [[128 * NE, 8], [1, NE]])),
         reads=["agout_cnt"], writes=["cnt8_sb"], stream="d_cnt")
    B.op("tensor", lambda e: e.matmul(
        psmall.ap()[0:1, 0:16], prevmask.ap(), cnt8_sb.ap(), start=True, stop=True),
        reads=["prevmask", "cnt8_sb", "psmall"], writes=["psmall"])
    B.op("vector", lambda e: e.tensor_copy(base_sb.ap(), psmall.ap()[0:1, 0:16]),
         reads=["psmall"], writes=["base_sb"])
    if debug_upto >= 2:
        B.op("sync", lambda e: e.dma_start(out=dbg["tot"], in_=total_sb.ap()),
             reads=["total_sb"], stream="d_dbg2")
        B.op("sync", lambda e: e.dma_start(out=dbg["base"], in_=base_sb.ap()),
             reads=["base_sb"], stream="d_dbg2")
        B.op("sync", lambda e: e.dma_start(out=dbg["cnt8"], in_=cnt8_sb.ap()),
             reads=["cnt8_sb"], stream="d_dbg2")
    for t in range(NT):
        pn = "pb2"
        B.op("tensor", lambda e, t=t: e.matmul(
            pb[2].ap()[:, 0:16], ltinc.ap(), mask_sb.ap()[:, 16 * t:16 * (t + 1)],
            start=True, stop=False),
            reads=["ltinc", "mask_sb", pn], writes=[pn])
        B.op("tensor", lambda e, t=t: e.matmul(
            pb[2].ap()[:, 0:16], cpfx.ap()[:, 128 * t:128 * (t + 1)], counts_sb.ap(),
            start=False, stop=False), reads=["cpfx", "counts_sb"], writes=[pn])
        B.op("tensor", lambda e: e.matmul(
            pb[2].ap()[:, 0:16], onesrow.ap(), base_sb.ap(),
            start=False, stop=True), reads=["onesrow", "base_sb"], writes=[pn])
        if debug_upto >= 2 and t == 0:
            B.op("vector", lambda e: e.tensor_copy(t16b.ap(), pb[2].ap()[:, 0:16]),
                 reads=[pn], writes=["t16b"])
            B.op("sync", lambda e: e.dma_start(out=dbg["rank0"], in_=t16b.ap()),
                 reads=["t16b"], stream="d_dbg2")
        B.op("vector", lambda e: e.tensor_scalar(
            t16a.ap(), pb[2].ap()[:, 0:16], CAP + 0.5, None, OP.is_le),
            reads=[pn], writes=["t16a"])
        B.op("vector", lambda e, t=t: e.tensor_mul(
            t16b.ap(), mask_sb.ap()[:, 16 * t:16 * (t + 1)], t16a.ap()),
            reads=["mask_sb", "t16a"], writes=["t16b"])
        B.op("vector", lambda e: e.scalar_tensor_tensor(
            t16a.ap(), t16b.ap(), 1.0, pb[2].ap()[:, 0:16], OP.mult, OP.mult,
            accum_out=t1a.ap()),
            reads=["t16b", pn], writes=["t16a", "t1a"])
        B.op("vector", lambda e: e.reduce_sum(t1b.ap(), t16b.ap(), axis=AX.X),
             reads=["t16b"], writes=["t1b"])
        B.op("vector", lambda e: e.tensor_sub(t1c.ap(), t1a.ap(), t1b.ap()),
             reads=["t1a", "t1b"], writes=["t1c"])
        B.op("vector", lambda e, t=t: e.tensor_scalar(
            t1d.ap(), e_sb.ap()[:, t:t + 1], CAP * 1.0, None, OP.mult),
            reads=["e_sb"], writes=["t1d"])
        B.op("vector", lambda e: e.tensor_add(t1d.ap(), t1d.ap(), t1c.ap()),
             reads=["t1c"], writes=["t1d"])
        B.op("vector", lambda e: e.tensor_scalar(
            t1e.ap(), t1d.ap(), BIGIDX, None, OP.subtract),
            reads=["t1d"], writes=["t1e"])
        B.op("vector", lambda e, t=t: e.scalar_tensor_tensor(
            gidx_sb.ap()[:, t:t + 1], t1e.ap(), t1b.ap(), c9.ap(), OP.mult, OP.add),
            reads=["t1e", "t1b", "c9"], writes=["gidx_sb"])
        B.op("vector", lambda e, t=t: e.tensor_mul(
            w_sb.ap()[:, t:t + 1], wpre_sb.ap()[:, t:t + 1], t1b.ap()),
            reads=["wpre_sb", "t1b"], writes=["w_sb"])
    B.op("tensor", lambda e: e.transpose(
        pb[2].ap()[0:16, 0:128], gidx_sb.ap(), identf.ap()),
        reads=["gidx_sb", "identf", "pb2"], writes=["pb2"])
    B.op("vector", lambda e: e.tensor_copy(gxT_sb.ap(), pb[2].ap()[0:16, 0:128]),
         reads=["pb2"], writes=["gxT_sb"])
    B.op("sync", lambda e: e.dma_start(out=agin_gx, in_=gxT_sb.ap()),
         reads=["gxT_sb"], writes=["agin_gx"], stream="d_gx")
    if debug_upto >= 2:
        B.op("sync", lambda e: e.dma_start(out=dbg["gidx"], in_=gidx_sb.ap()),
             reads=["gidx_sb"], stream="d_dbg2")
        B.op("sync", lambda e: e.dma_start(out=dbg["w"], in_=w_sb.ap()),
             reads=["w_sb"], stream="d_dbg2")
    for i, (ain, aout, ainn, aoutn) in enumerate([
            (agin_kt, agout_kt, "agin_kt", "agout_kt"),
            (agin_v, agout_v, "agin_v", "agout_v"),
            (agin_gx, agout_gx, "agin_gx", "agout_gx"),
            (agin_xn, agout_xn, "agin_xn", "agout_xn")]):
        B.op("gpsimd", lambda e, ain=ain, aout=aout: e.collective_compute(
            "AllGather", OP.bypass, replica_groups=[list(range(NCORES))],
            ins=[ain.opt()], outs=[aout.opt()]),
            reads=[ainn], writes=[aoutn], sem=B.ccsem(f"scc{i + 1}"))

    if upto_phase <= 2:
        return finish(B, nc, dict(locals()))

    B.free("xbuf0", "xbuf1", "xsq", "xn32", "xnbf", "xnT32", "xnT_sb",
           "wqsg0", "wqsg1", "qth0", "qth1", "wkh0", "wkh1", "kevac0", "kevac1",
           "wvg0", "wvg1", "vevac0", "vevac1")
    for nm in ["xbuf0", "xbuf1", "xsq", "xn32", "xnbf", "xnT32", "xnT_sb",
               "wqsg0", "wqsg1", "qth0", "qth1", "wkh0", "wkh1", "kevac0",
               "kevac1", "wvg0", "wvg1", "vevac0", "vevac1"]:
        pass

    ktb = [B.sbuf(f"ktb{i}", [64, S], BF16) for i in range(2)]
    vaugb = [B.sbuf(f"vaugb{i}", [128, 128 * 65], BF16) for i in range(2)]
    qtb = [B.sbuf(f"qtb{i}", [64, TOK], BF16) for i in range(2)]
    ptb = [B.sbuf(f"ptb{i}", [128, 512], BF16) for i in range(2)]
    atmp = B.sbuf("atmp", [64, 512], BF16)
    atb = B.sbuf("atb", [64, TOK], BF16)
    rsum_sb = B.sbuf("rsum_sb", [128, TOK], F32)
    rsum0_sb = B.sbuf("rsum0_sb", [1, TOK], F32)
    recips = B.sbuf("recips", [1, TOK], F32)
    eshift = B.sbuf("eshift", [128, 1], F32)
    B.op("vector", lambda e: e.memset(eshift.ap(), EXP_SHIFT), writes=["eshift"])
    for i in range(2):
        B.op("vector", lambda e, i=i: e.memset(
            bass.AP(vaugb[i], 64, [[128 * 65, 128], [65, 128]]), 1.0),
            writes=[f"vaugb{i}_ones"])

    for h in range(NH):
        stv = STRIDES[h]
        L = S // stv
        Lloc = TOK // stv
        kt, ktn = ktb[h % 2], f"ktb{h % 2}"
        va, van = vaugb[h % 2], f"vaugb{h % 2}"
        qb, qbn = qtb[h % 2], f"qtb{h % 2}"
        B.op("sync", lambda e, h=h, kt=kt, Lloc=Lloc: e.dma_start(
            out=kt.ap()[:, 0:8 * Lloc],
            in_=bass.AP(agout_kt.tensor, KV_OFF[h],
                        [[KV_TOT, 64], [64 * KV_TOT, 8], [1, Lloc]])),
            reads=["agout_kt"], writes=[ktn], stream="d_kld")
        nck = Lloc // 128
        for rblk in range(8):
            B.op("sync", lambda e, h=h, va=va, nck=nck, rblk=rblk: e.dma_start(
                out=bass.AP(va, rblk * nck * 65, [[128 * 65, 128], [65, nck], [1, 64]]),
                in_=bass.AP(agout_v.tensor, 64 * (KV_OFF[h] + KV_TOT * rblk),
                            [[64, 128], [64 * 128, nck], [1, 64]])),
                reads=["agout_v", f"vaugb{h % 2}_ones"], writes=[van], stream="d_vld")
        B.op("sync", lambda e, h=h, qb=qb: e.dma_start(
            out=qb.ap(), in_=qt_spill[:, TOK * h:TOK * (h + 1)]),
            reads=["qt_spill"], writes=[qbn], stream="d_qld")
        NKC = L // 128
        for qt in range(4):
            abank = 2 + (qt % 4)
            abn = f"pb{abank}"
            prev = None
            for kc in range(NKC):
                sbank = kc % 2
                sbn = f"pb{sbank}"
                B.op("tensor", lambda e, kc=kc, sbank=sbank, kt=kt, qb=qb, qt=qt: e.matmul(
                    pb[sbank].ap()[:, 0:512], kt.ap()[:, 128 * kc:128 * (kc + 1)],
                    qb.ap()[:, 512 * qt:512 * (qt + 1)], start=True, stop=True),
                    reads=[ktn, qbn, sbn], writes=[sbn])
                B.op("scalar", lambda e, kc=kc, sbank=sbank: e.activation(
                    out=ptb[kc % 2].ap(), in_=pb[sbank].ap()[:, 0:512], func=AF.Exp,
                    bias=eshift.ap(), scale=EXP_SCALE),
                    reads=[sbn, "eshift"], writes=[f"ptb{kc % 2}"])
                B.op("tensor", lambda e, kc=kc, va=va, abank=abank, NKC=NKC: e.matmul(
                    pb[abank].ap()[0:65, 0:512], va.ap()[:, 65 * kc:65 * (kc + 1)],
                    ptb[kc % 2].ap(), start=(kc == 0), stop=(kc == NKC - 1)),
                    reads=[van, f"ptb{kc % 2}"] + ([abn] if kc == 0 else []),
                    writes=[abn])
            B.op("vector", lambda e, qt=qt, abank=abank: e.tensor_copy(
                rsum_sb.ap()[64:65, 512 * qt:512 * (qt + 1)],
                pb[abank].ap()[64:65, 0:512]),
                reads=[abn], writes=["rsum_sb"])
        B.op("sync", lambda e: e.dma_start(
            out=rsum0_sb.ap(), in_=rsum_sb.ap()[64:65, :]),
            reads=["rsum_sb"], writes=["rsum0_sb"], stream="d_rs")
        B.op("vector", lambda e: e.reciprocal(recips.ap(), rsum0_sb.ap()),
             reads=["rsum0_sb"], writes=["recips"])
        for qt in range(4):
            abank = 2 + (qt % 4)
            abn = f"pb{abank}"
            B.op("tensor", lambda e, qt=qt: e.matmul(
                psmall.ap()[0:64, 0:512], onesrow.ap()[:, 0:64],
                recips.ap()[:, 512 * qt:512 * (qt + 1)], start=True, stop=True),
                reads=["recips", "onesrow", "psmall"], writes=["psmall"])
            B.op("vector", lambda e, qt=qt, abank=abank: e.tensor_copy(
                atmp.ap(), pb[abank].ap()[0:64, 0:512]),
                reads=[abn], writes=["atmp"])
            B.op("vector", lambda e, qt=qt: e.tensor_mul(
                atb.ap()[:, 512 * qt:512 * (qt + 1)], atmp.ap(), psmall.ap()[0:64, 0:512]),
                reads=["atmp", "psmall"], writes=["atb"])
        B.op("sync", lambda e, h=h: e.dma_start(
            out=attn_spill[:, TOK * h:TOK * (h + 1)], in_=atb.ap()),
            reads=["atb"], writes=["attn_spill"], stream="d_attnst")
        if debug_upto >= 3:
            B.op("gpsimd", lambda e, h=h: e.dma_start(
                out=dbg["attn"][:, 128 * h:128 * (h + 1)], in_=atb.ap()[:, 0:128]),
                reads=["atb"], stream="d_dbg")

    if upto_phase <= 3:
        return finish(B, nc, dict(locals()))

    B.free("ktb0", "ktb1", "vaugb0", "vaugb1", "qtb0", "qtb1", "ptb0", "ptb1",
           "atmp", "atb", "rsum_sb", "rsum0_sb", "recips", "eshift")
    hT_sb = B.sbuf("hT_sb", [128, 8 * TOK], BF16)
    wout_sb = B.sbuf("wout_sb", [64, NH * 1024], BF16)
    attn_ld = [B.sbuf(f"attn_ld{i}", [64, NH * 512], BF16) for i in range(2)]
    xb2 = [B.sbuf(f"xb2_{i}", [128, D], F32) for i in range(2)]
    hpre = B.sbuf("hpre", [128, D], F32)
    hsq = B.sbuf("hsq", [128, D], F32)
    htile = B.sbuf("htile", [128, D], F32)

    for h in range(NH):
        B.op("gpsimd", lambda e, h=h: e.dma_start(
            out=wout_sb.ap()[:, 1024 * h:1024 * (h + 1)],
            in_=wout_in[64 * h:64 * (h + 1), :]),
            writes=["wout_sb"], stream="d_wout")

    def ln_stats(src, srcn, rstd_ap, negmr_ap, wr_r, wr_n):
        B.op("vector", lambda e: e.reduce_sum(sum1.ap(), src.ap(), axis=AX.X),
             reads=[srcn], writes=["sum1"])
        B.op("scalar", lambda e: e.activation(
            out=hsq.ap(), in_=src.ap(), func=AF.Square, accum_out=sumsq.ap()),
            reads=[srcn], writes=["hsq", "sumsq"])
        B.op("vector", lambda e: e.tensor_scalar(t1a.ap(), sum1.ap(), 1.0 / D, None, OP.mult),
             reads=["sum1"], writes=["t1a"])
        B.op("vector", lambda e: e.tensor_scalar(t1b.ap(), sumsq.ap(), 1.0 / D, None, OP.mult),
             reads=["sumsq"], writes=["t1b"])
        B.op("vector", lambda e: e.tensor_scalar(t1c.ap(), t1a.ap(), t1a.ap(), -1.0, OP.mult, OP.mult),
             reads=["t1a"], writes=["t1c"])
        B.op("vector", lambda e: e.tensor_add(t1d.ap(), t1b.ap(), t1c.ap()),
             reads=["t1b", "t1c"], writes=["t1d"])
        B.op("scalar", lambda e: e.activation(
            out=t1e.ap(), in_=t1d.ap(), func=AF.Sqrt, bias=epsc.ap()),
            reads=["t1d", "epsc"], writes=["t1e"])
        B.op("vector", lambda e: e.reciprocal(rstd_ap, t1e.ap()),
             reads=["t1e"], writes=[wr_r])
        B.op("vector", lambda e: e.tensor_scalar(
            negmr_ap, t1a.ap(), rstd_ap, -1.0, OP.mult, OP.mult),
            reads=["t1a", wr_r], writes=[wr_n])

    for tg in range(4):
        al, aln = attn_ld[tg % 2], f"attn_ld{tg % 2}"
        B.op("sync", lambda e, tg=tg, al=al: e.dma_start(
            out=al.ap(),
            in_=bass.AP(attn_spill.tensor, 512 * tg, [[NH * TOK, 64], [TOK, NH], [1, 512]])),
            reads=["attn_spill"], writes=[aln], stream="d_attnld")
        for tt in range(4):
            t = 4 * tg + tt
            xb, xbn = xb2[t % 2], f"xb2_{t % 2}"
            B.op("sync", lambda e, t=t, xb=xb: e.dma_start(
                out=xb.ap(), in_=x_in[128 * t:128 * (t + 1), :]),
                writes=[xbn], stream="d_x2")
            for n in range(2):
                pn = f"pb{n}"
                for h in range(NH):
                    B.op("tensor", lambda e, h=h, n=n, tt=tt, al=al: e.matmul(
                        pb[n].ap()[:, 0:512],
                        al.ap()[:, 512 * h + 128 * tt:512 * h + 128 * (tt + 1)],
                        wout_sb.ap()[:, 1024 * h + 512 * n:1024 * h + 512 * (n + 1)],
                        start=(h == 0), stop=(h == NH - 1)),
                        reads=[aln, "wout_sb"] + ([pn] if h == 0 else []),
                        writes=[pn])
            B.op("vector", lambda e, t=t, xb=xb: e.tensor_scalar(
                htile.ap(), xb.ap(), rstd1.ap()[:, t:t + 1], negmr1.ap()[:, t:t + 1],
                OP.mult, OP.add), reads=[xbn, "rstd1", "negmr1"], writes=["htile"])
            for n in range(2):
                B.op("vector", lambda e, n=n: e.tensor_add(
                    hpre.ap()[:, 512 * n:512 * (n + 1)], pb[n].ap()[:, 0:512],
                    htile.ap()[:, 512 * n:512 * (n + 1)]),
                    reads=[f"pb{n}", "htile"], writes=["hpre"])
            ln_stats(hpre, "hpre", rstd2.ap(), negmr2.ap(), "rstd2", "negmr2")
            B.op("scalar", lambda e: e.activation(
                out=htile.ap(), in_=hpre.ap(), func=AF.Identity,
                bias=negmr2.ap(), scale=rstd2.ap()),
                reads=["hpre", "rstd2", "negmr2"], writes=["htile"])
            B.op("sync", lambda e, t=t: e.dma_start(
                out=h_spill[128 * t:128 * (t + 1), :], in_=htile.ap()),
                reads=["htile"], writes=["h_spill"], stream="d_h")
            if debug_upto >= 4:
                B.op("sync", lambda e, t=t: e.dma_start(
                    out=dbg["h"][128 * t:128 * (t + 1), :], in_=htile.ap()),
                    reads=["htile"], stream="d_dbg2")
            for c in range(8):
                slot = c % 2
                pnn = f"pb{2 + slot}"
                B.op("tensor", lambda e, c=c, slot=slot: e.transpose(
                    pb[2 + slot].ap()[:, 0:128],
                    htile.ap()[:, 128 * c:128 * (c + 1)], identf.ap()),
                    reads=["htile", "identf", pnn], writes=[pnn])
                B.op("scalar", lambda e, c=c, t=t, slot=slot: e.activation(
                    out=hT_sb.ap()[:, c * TOK + 128 * t: c * TOK + 128 * (t + 1)],
                    in_=pb[2 + slot].ap()[:, 0:128], func=AF.Copy),
                    reads=[pnn], writes=["hT_sb"])

    B.free("wout_sb", "attn_ld0", "attn_ld1", "xb2_0", "xb2_1", "hpre", "hsq", "htile")
    w1_sb = B.sbuf("w1_sb", [128, 8, DFF], BF16)
    w2_sb = B.sbuf("w2_sb", [128, 32, D], BF16)
    zf = [B.sbuf(f"zf{i}", [128, 256], BF16) for i in range(2)]
    spre = B.sbuf("spre", [128, D], F32)
    hsq2 = B.sbuf("hsq2", [128, D], F32)
    hld = [B.sbuf(f"hld{i}", [128, D], F32) for i in range(2)]
    brtile = B.sbuf("brtile", [128, D], F32)

    for c in range(8):
        B.op("gpsimd", lambda e, c=c: e.dma_start(
            out=w1_sb.ap()[:, c, :], in_=w1_in[128 * c:128 * (c + 1), :]),
            writes=["w1_sb"], stream="d_w1")
    for f in range(32):
        B.op("gpsimd", lambda e, f=f: e.dma_start(
            out=w2_sb.ap()[:, f, :], in_=w2_in[128 * f:128 * (f + 1), :]),
            writes=["w2_sb"], stream="d_w2")

    def ln_stats2(src, srcn):
        B.op("vector", lambda e: e.reduce_sum(sum1.ap(), src.ap(), axis=AX.X),
             reads=[srcn], writes=["sum1"])
        B.op("scalar", lambda e: e.activation(
            out=hsq2.ap(), in_=src.ap(), func=AF.Square, accum_out=sumsq.ap()),
            reads=[srcn], writes=["hsq2", "sumsq"])
        B.op("vector", lambda e: e.tensor_scalar(t1a.ap(), sum1.ap(), 1.0 / D, None, OP.mult),
             reads=["sum1"], writes=["t1a"])
        B.op("vector", lambda e: e.tensor_scalar(t1b.ap(), sumsq.ap(), 1.0 / D, None, OP.mult),
             reads=["sumsq"], writes=["t1b"])
        B.op("vector", lambda e: e.tensor_scalar(t1c.ap(), t1a.ap(), t1a.ap(), -1.0, OP.mult, OP.mult),
             reads=["t1a"], writes=["t1c"])
        B.op("vector", lambda e: e.tensor_add(t1d.ap(), t1b.ap(), t1c.ap()),
             reads=["t1b", "t1c"], writes=["t1d"])
        B.op("scalar", lambda e: e.activation(
            out=t1e.ap(), in_=t1d.ap(), func=AF.Sqrt, bias=epsc.ap()),
            reads=["t1d", "epsc"], writes=["t1e"])
        B.op("vector", lambda e: e.reciprocal(rstd2.ap(), t1e.ap()),
             reads=["t1e"], writes=["rstd2"])
        B.op("vector", lambda e: e.tensor_scalar(
            negmr2.ap(), t1a.ap(), rstd2.ap(), -1.0, OP.mult, OP.mult),
            reads=["t1a", "rstd2"], writes=["negmr2"])

    for Bq in range(8):
        for f in range(32):
            pw = f % 2
            pwn = f"pb{pw}"
            for c in range(8):
                B.op("tensor", lambda e, c=c, f=f, pw=pw, Bq=Bq: e.matmul(
                    pb[pw].ap()[:, 0:256], w1_sb.ap()[:, c, 128 * f:128 * (f + 1)],
                    hT_sb.ap()[:, c * TOK + 256 * Bq:c * TOK + 256 * (Bq + 1)],
                    start=(c == 0), stop=(c == 7)),
                    reads=["w1_sb", "hT_sb"] + ([pwn] if c == 0 else []),
                    writes=[pwn])
            B.op("scalar", lambda e, f=f, pw=pw: e.activation(
                out=zf[pw].ap(), in_=pb[pw].ap()[:, 0:256], func=AF.Relu,
                bias=b1sb.ap()[:, f:f + 1]),
                reads=[pwn, "b1sb"], writes=[f"zf{pw}"])
            for tt in range(2):
                for n in range(2):
                    pa = f"pb{2 + 2 * tt + n}"
                    B.op("tensor", lambda e, f=f, pw=pw, tt=tt, n=n: e.matmul(
                        pb[2 + 2 * tt + n].ap()[:, 0:512],
                        zf[pw].ap()[:, 128 * tt:128 * (tt + 1)],
                        w2_sb.ap()[:, f, 512 * n:512 * (n + 1)],
                        start=(f == 0), stop=(f == 31)),
                        reads=[f"zf{pw}", "w2_sb"] + ([pa] if f == 0 else []),
                        writes=[pa])
        for tt in range(2):
            t = 2 * Bq + tt
            hb, hbn = hld[tt % 2], f"hld{tt % 2}"
            B.op("sync", lambda e, t=t, hb=hb: e.dma_start(
                out=hb.ap(), in_=h_spill[128 * t:128 * (t + 1), :]),
                reads=["h_spill"], writes=[hbn], stream="d_hld")
            for n in range(2):
                pa = f"pb{2 + 2 * tt + n}"
                B.op("vector", lambda e, tt=tt, n=n, hb=hb: e.tensor_add(
                    spre.ap()[:, 512 * n:512 * (n + 1)],
                    pb[2 + 2 * tt + n].ap()[:, 0:512],
                    hb.ap()[:, 512 * n:512 * (n + 1)]),
                    reads=[pa, hbn], writes=["spre"])
            ln_stats2(spre, "spre")
            B.op("scalar", lambda e: e.activation(
                out=brtile.ap(), in_=spre.ap(), func=AF.Identity,
                bias=negmr2.ap(), scale=rstd2.ap()),
                reads=["spre", "rstd2", "negmr2"], writes=["brtile"])
            B.op("sync", lambda e, t=t: e.dma_start(
                out=br_spill[128 * t:128 * (t + 1), :], in_=brtile.ap()),
                reads=["brtile"], writes=["br_spill"], stream="d_br")
            if debug_upto >= 4:
                B.op("sync", lambda e, t=t: e.dma_start(
                    out=dbg["br"][128 * t:128 * (t + 1), :], in_=brtile.ap()),
                    reads=["brtile"], stream="d_dbg2")

    if upto_phase <= 4:
        return finish(B, nc, dict(locals()))

    B.free("w1_sb", "w2_sb", "zf0", "zf1", "spre", "hsq2", "hld0", "hld1",
           "brtile", "hT_sb")
    gxall = B.sbuf("gxall", [128, 128], F32)
    xg = [B.sbuf(f"xg{i}", [128, D], BF16) for i in range(2)]
    dispbuf = B.sbuf("dispbuf", [128, D], BF16)
    dispT = B.sbuf("dispT", [128, 8, SLOTC], BF16)
    we1_sb = B.sbuf("we1_sb", [128, 8, DFF], BF16)
    we2f = [B.sbuf(f"we2f{i}", [128, D], BF16) for i in range(2)]
    h1T = B.sbuf("h1T", [128, 32, 256], BF16)
    eoev = [B.sbuf(f"eoev{i}", [128, 512], BF16) for i in range(2)]

    REGS = {}

    def _mkregs(e):
        REGS["bnd_scat"] = e.alloc_register("bnd_scat")
        e.reg_mov(REGS["bnd_scat"], SLOTC - 1)
        REGS["bnd_gath"] = e.alloc_register("bnd_gath")
        e.reg_mov(REGS["bnd_gath"], NSLOT - 1)
    B.raw("gpsimd", _mkregs)

    B.op("sync", lambda e: e.dma_start(out=gxall.ap(), in_=agout_gx),
         reads=["agout_gx"], writes=["gxall"], stream="d_gxld")
    B.op("tensor", lambda e: e.transpose(
        pb[0].ap()[:, 0:128], gxall.ap(), identf.ap()),
        reads=["gxall", "identf", "pb0"], writes=["pb0"])
    B.op("vector", lambda e: e.tensor_copy(idxall_f.ap(), pb[0].ap()[:, 0:128]),
         reads=["pb0"], writes=["idxall_f"])
    B.op("vector", lambda e: e.tensor_scalar(
        idxall_f.ap(), idxall_f.ap(), slotb.ap(), None, OP.subtract),
        reads=["idxall_f", "slotb"], writes=["idxall_f"])
    B.op("vector", lambda e: e.tensor_scalar(
        idxall_u.ap().bitcast(F32), idxall_f.ap(), 0.0, None, OP.is_lt),
        reads=["idxall_f"], writes=["idxall_u"])
    B.op("vector", lambda e: e.scalar_tensor_tensor(
        idxall_f.ap(), idxall_u.ap().bitcast(F32), 200000.0, idxall_f.ap(),
        OP.mult, OP.add),
        reads=["idxall_u", "idxall_f"], writes=["idxall_f"])
    B.op("vector", lambda e: e.tensor_copy(idxall_u.ap(), idxall_f.ap()),
         reads=["idxall_f"], writes=["idxall_u"])
    if debug_upto >= 5:
        B.op("sync", lambda e: e.dma_start(out=dbg["idx"], in_=idxall_f.ap()),
             reads=["idxall_f"], stream="d_dbg2")

    for sl in range(SLOTC // 128):
        B.op("sync", lambda e, sl=sl: e.dma_start(
            out=disp_dram[128 * sl:128 * (sl + 1), :], in_=zero_bf.ap()),
            reads=["zero_bf"], writes=["disp_dram"], stream="d_dz")
    for gt in range(S // 128):
        xgb, xgn = xg[gt % 2], f"xg{gt % 2}"
        B.op("sync", lambda e, gt=gt, xgb=xgb: e.dma_start(
            out=xgb.ap(), in_=agout_xn[128 * gt:128 * (gt + 1), :]),
            reads=["agout_xn"], writes=[xgn], stream="d_xg")
        B.op("gpsimd", lambda e, gt=gt, xgb=xgb: e.indirect_dma_start(
            out=disp_dram,
            out_offset=IndirectOffsetOnAxis(ap=idxall_u.ap()[:, gt:gt + 1], axis=0),
            in_=xgb.ap(), in_offset=None,
            bounds_check=REGS["bnd_scat"], oob_is_err=False),
            reads=[xgn, "idxall_u", "disp_dram"], writes=["disp_dram"],
            stream="d_scat")

    for sl in range(SLOTC // 128):
        B.op("sync", lambda e, sl=sl: e.dma_start(
            out=dispbuf.ap(), in_=disp_dram[128 * sl:128 * (sl + 1), :]),
            reads=["disp_dram"], writes=["dispbuf"], stream="d_dld")
        if debug_upto >= 5 and sl == 0:
            B.op("gpsimd", lambda e: e.dma_start(
                out=dbg["disp0"], in_=dispbuf.ap()),
                reads=["dispbuf"], stream="d_dbg")
        for c in range(8):
            slot = c % 2
            pnn = f"pb{slot}"
            B.op("tensor", lambda e, c=c, slot=slot: e.transpose(
                pb[slot].ap().bitcast(BF16)[:, 0:128],
                dispbuf.ap()[:, 128 * c:128 * (c + 1)], identb.ap()),
                reads=["dispbuf", "identb", pnn], writes=[pnn])
            B.op("scalar", lambda e, c=c, sl=sl, slot=slot: e.activation(
                out=dispT.ap()[:, c, 128 * sl:128 * (sl + 1)],
                in_=pb[slot].ap().bitcast(BF16)[:, 0:128], func=AF.Copy),
                reads=[pnn], writes=["dispT"])

    NSG = CAP // 256
    for ei in range(EPC):
        for c in range(8):
            B.op("gpsimd", lambda e, ei=ei, c=c: e.dma_start(
                out=we1_sb.ap()[:, c, :],
                in_=we1_in[ei, 128 * c:128 * (c + 1), :]),
                writes=["we1_sb"], stream="d_we1")
        for sg in range(NSG):
            soff = ei * CAP + 256 * sg
            for f in range(32):
                pw = f % 2
                pwn = f"pb{pw}"
                for c in range(8):
                    B.op("tensor", lambda e, c=c, f=f, pw=pw, soff=soff: e.matmul(
                        pb[pw].ap()[:, 0:256],
                        we1_sb.ap()[:, c, 128 * f:128 * (f + 1)],
                        dispT.ap()[:, c, soff:soff + 256],
                        start=(c == 0), stop=(c == 7)),
                        reads=["we1_sb", "dispT"] + ([pwn] if c == 0 else []),
                        writes=[pwn])
                B.op("scalar", lambda e, f=f, pw=pw: e.activation(
                    out=h1T.ap()[:, f, :], in_=pb[pw].ap()[:, 0:256],
                    func=AF.Gelu_apprx_tanh),
                    reads=[pwn], writes=["h1T"])
            for f in range(32):
                wf, wfn = we2f[f % 2], f"we2f{f % 2}"
                B.op("gpsimd", lambda e, ei=ei, f=f, wf=wf: e.dma_start(
                    out=wf.ap(), in_=we2_in[ei, 128 * f:128 * (f + 1), :]),
                    writes=[wfn], stream="d_we2")
                for tt in range(2):
                    for n in range(2):
                        pa = f"pb{2 + 2 * tt + n}"
                        B.op("tensor", lambda e, f=f, tt=tt, n=n, wf=wf: e.matmul(
                            pb[2 + 2 * tt + n].ap()[:, 0:512],
                            h1T.ap()[:, f, 128 * tt:128 * (tt + 1)],
                            wf.ap()[:, 512 * n:512 * (n + 1)],
                            start=(f == 0), stop=(f == 31)),
                            reads=["h1T", wfn] + ([pa] if f == 0 else []),
                            writes=[pa])
            for tt in range(2):
                for n in range(2):
                    pa = f"pb{2 + 2 * tt + n}"
                    ev, evn = eoev[n % 2], f"eoev{n % 2}"
                    B.op("scalar", lambda e, tt=tt, n=n, ev=ev: e.activation(
                        out=ev.ap(), in_=pb[2 + 2 * tt + n].ap()[:, 0:512],
                        func=AF.Copy), reads=[pa], writes=[evn])
                    B.op("sync", lambda e, soff=soff, tt=tt, n=n, ev=ev: e.dma_start(
                        out=agin_eo[soff + 128 * tt:soff + 128 * (tt + 1),
                                    512 * n:512 * (n + 1)],
                        in_=ev.ap()), reads=[evn], writes=["agin_eo"],
                        stream="d_eo")
                    if debug_upto >= 5 and sg == 0 and tt == 0:
                        B.op("gpsimd", lambda e, ei=ei, n=n, ev=ev: e.dma_start(
                            out=dbg["eo"][:, 1024 * ei + 512 * n:1024 * ei + 512 * (n + 1)],
                            in_=ev.ap()), reads=[evn], stream="d_dbg")
    B.op("gpsimd", lambda e: e.collective_compute(
        "AllGather", OP.bypass, replica_groups=[list(range(NCORES))],
        ins=[agin_eo.opt()], outs=[agout_eo.opt()]),
        reads=["agin_eo"], writes=["agout_eo"], sem=B.ccsem("scc5"))

    if upto_phase <= 5:
        return finish(B, nc, dict(locals()))

    B.free("gxall", "xg0", "xg1", "dispbuf", "dispT", "we1_sb", "we2f0", "we2f1",
           "h1T", "eoev0", "eoev1")
    gath = B.sbuf("gath", [128, D], BF16)
    xb3 = [B.sbuf(f"xb3_{i}", [128, D], F32) for i in range(2)]
    brb = [B.sbuf(f"brb{i}", [128, D], F32) for i in range(2)]
    sumt = B.sbuf("sumt", [128, D], F32)
    yb = B.sbuf("yb", [128, D], F32)

    B.op("vector", lambda e: e.memset(gath.ap(), 0.0), writes=["gath"])
    B.op("vector", lambda e: e.tensor_copy(gidxu.ap(), gidx_sb.ap()),
         reads=["gidx_sb"], writes=["gidxu"])
    for t in range(NT):
        xbb, xbn = xb3[t % 2], f"xb3_{t % 2}"
        bbb, bbn = brb[t % 2], f"brb{t % 2}"
        B.op("sync", lambda e, t=t, xbb=xbb: e.dma_start(
            out=xbb.ap(), in_=x_in[128 * t:128 * (t + 1), :]),
            writes=[xbn], stream="d_x3")
        B.op("sync", lambda e, t=t, bbb=bbb: e.dma_start(
            out=bbb.ap(), in_=br_spill[128 * t:128 * (t + 1), :]),
            reads=["br_spill"], writes=[bbn], stream="d_brl")
        B.op("gpsimd", lambda e, t=t: e.indirect_dma_start(
            out=gath.ap(), out_offset=None,
            in_=agout_eo,
            in_offset=IndirectOffsetOnAxis(ap=gidxu.ap()[:, t:t + 1], axis=0),
            bounds_check=REGS["bnd_gath"], oob_is_err=False),
            reads=["agout_eo", "gidxu", "gath"], writes=["gath"], stream="d_gath")
        B.op("vector", lambda e, xbb=xbb, bbb=bbb: e.tensor_add(
            sumt.ap(), xbb.ap(), bbb.ap()),
            reads=[xbn, bbn], writes=["sumt"])
        B.op("vector", lambda e, t=t: e.scalar_tensor_tensor(
            yb.ap(), gath.ap(), w_sb.ap()[:, t:t + 1], sumt.ap(), OP.mult, OP.add),
            reads=["gath", "w_sb", "sumt"], writes=["yb"])
        B.op("sync", lambda e, t=t: e.dma_start(
            out=y_out[128 * t:128 * (t + 1), :], in_=yb.ap()),
            reads=["yb"], writes=["y_out"], stream="d_y")

    return finish(B, nc, dict(locals()))


def finish(B, nc, state):
    for semname in list(B.sem):
        if B.cnt[semname] > 0:
            B.raw_wait("sync", semname, B.cnt[semname])
    nc_ = B.close()
    state["B"] = B
    return nc_, state


def make_inputs(full_inputs, core):
    x = np.asarray(full_inputs["L0_attn_x"]).reshape(S, D)
    bqkv = np.asarray(full_inputs["L0_attn_bqkv"])
    b1 = np.asarray(full_inputs["L0_attn_b1"])
    return {
        "x": np.ascontiguousarray(x[TOK * core:TOK * (core + 1)]),
        "wqkv": np.asarray(full_inputs["L0_attn_Wqkv"]),
        "bq64": np.ascontiguousarray(bqkv.reshape(48, 64).T),
        "wout": np.asarray(full_inputs["L0_attn_Wout"]),
        "w1": np.asarray(full_inputs["L0_attn_W1"]),
        "b1i": np.ascontiguousarray(b1.reshape(32, 128).T),
        "w2": np.asarray(full_inputs["L0_attn_W2"]),
        "gwT": np.ascontiguousarray(np.asarray(full_inputs["L0_ffn_gate_w"]).T),
        "we1": np.ascontiguousarray(np.asarray(full_inputs["L0_ffn_w1"])[EPC * core:EPC * (core + 1)]),
        "we2": np.ascontiguousarray(np.asarray(full_inputs["L0_ffn_w2"])[EPC * core:EPC * (core + 1)]),
        "identb": np.eye(128, dtype=ml_dtypes.bfloat16),
        "identf": np.eye(128, dtype=np.float32),
        "ltinc": np.triu(np.ones((128, 128), np.float32)),
        "oneshift": np.concatenate(
            [np.zeros((128, 127), np.float32), np.ones((128, 1), np.float32),
             np.zeros((128, 127), np.float32)], axis=1),
        "cpfx": np.concatenate(
            [np.tile((np.arange(128)[:, None] < t).astype(np.float32), (1, 128))
             for t in range(NT)], axis=1),
        "iota16": np.tile(np.arange(16, dtype=np.float32), (128, 1)),
        "iotarev": np.tile(16.0 - np.arange(16, dtype=np.float32), (128, 1)),
        "prevmask": (np.arange(8)[:, None] < core).astype(np.float32),
        "slotb": np.full((128, 1), SLOTC * core, np.float32),
    }


_NC_CACHE = {}


def kernel(**inputs):
    from concourse.bass_utils import run_bass_kernel_spmd
    if "nc" not in _NC_CACHE:
        nc, _state = build_nc(debug_upto=0, upto_phase=99)
        _NC_CACHE["nc"] = nc
    nc = _NC_CACHE["nc"]
    in_maps = [make_inputs(inputs, c) for c in range(NCORES)]
    res = run_bass_kernel_spmd(nc, in_maps, core_ids=list(range(NCORES)))
    y = np.concatenate([res.results[c]["y"] for c in range(NCORES)], axis=0)
    return y.reshape(1, S, D).astype(np.float32)


# version 02
# speedup vs baseline: 1.5488x; 1.5488x over previous
import numpy as np
import ml_dtypes
from contextlib import ExitStack

import concourse.bass as bass
import concourse.mybir as mybir
from concourse.bass_core_rust import IndirectOffsetOnAxis

F32 = mybir.dt.float32
BF16 = mybir.dt.bfloat16
U32 = mybir.dt.uint32
AF = mybir.ActivationFunctionType
OP = mybir.AluOpType
AX = mybir.AxisListType

S = 16384
D = 1024
NCORES = 8
TOK = S // NCORES
NT = TOK // 128
NH = 16
DH = 64
STRIDES = [1, 1, 2, 2, 2, 2, 4, 4, 4, 4, 8, 8, 8, 8, 16, 16]
NE = 16
CAP = 1280
EPC = 2
SLOTC = EPC * CAP
NSLOT = NE * CAP
DFF = 4096
LN_EPS = 1e-5
EXP_SCALE = 0.125
EXP_SHIFT = -16.0
BIGIDX = 99999.0

KV_OFF = []
_o = 0
for _st in STRIDES:
    KV_OFF.append(_o)
    _o += TOK // _st
KV_TOT = _o
VGROUPS = []
_h = 0
while _h < NH:
    _st = STRIDES[_h]
    _n = 1
    while _h + _n < NH and STRIDES[_h + _n] == _st:
        _n += 1
    VGROUPS.append((_h, _n, _st))
    _h += _n


class Build:
    ENGS = ("sync", "scalar", "vector", "tensor", "gpsimd")
    ESEM = {"scalar": "ss", "vector": "sv", "tensor": "st"}

    def __init__(self, nc, es):
        self.nc = nc
        self.es = es
        self.q = {e: [] for e in self.ENGS}
        self.sem = {}
        self.cnt = {}
        self.unord = {}
        self.inc = {}
        for name in ("ss", "sv", "st"):
            self._mksem(name, 1, False)
        self.lastw = {}
        self.readers = {}
        self.wm = {e: {} for e in self.ENGS}
        self._guards = {}
        self._free_deps = []

    def _mksem(self, name, inc, unord):
        self.sem[name] = self.es.enter_context(self.nc.semaphore(name))
        self.cnt[name] = 0
        self.inc[name] = inc
        self.unord[name] = unord

    def stream(self, name):
        if name not in self.sem:
            self._mksem(name, 16, True)
        return name

    def ccsem(self, name):
        if name not in self.sem:
            self._mksem(name, 1, False)
        return name

    def _wait(self, eng, sem, val):
        if val <= 0 or self.wm[eng].get(sem, 0) >= val:
            return
        self.wm[eng][sem] = val
        self.q[eng].append(lambda e, s=self.sem[sem], v=val: e.wait_ge(s, v))

    def _wait_dep(self, eng, dep):
        sem, val = dep
        if self.unord[sem]:
            val = self.cnt[sem]
        self._wait(eng, sem, val)

    def op(self, eng, fn, reads=(), writes=(), stream=None, sem=None):
        for r in reads:
            lw = self.lastw.get(r)
            if lw is not None:
                self._wait_dep(eng, lw)
        for w in writes:
            lw = self.lastw.get(w)
            if lw is not None:
                self._wait_dep(eng, lw)
            for dep in self.readers.get(w, ()):
                self._wait_dep(eng, dep)
        if stream is not None:
            semname = self.stream(stream)
        elif sem is not None:
            semname = sem
        else:
            semname = self.ESEM[eng]
        amt = self.inc[semname]
        self.cnt[semname] += amt
        val = self.cnt[semname]
        self.q[eng].append(
            lambda e, f=fn, s=self.sem[semname], a=amt: f(e).then_inc(s, a))
        for r in reads:
            self.readers.setdefault(r, []).append((semname, val))
        for w in writes:
            self.lastw[w] = (semname, val)
            self.readers[w] = []
        return val

    def raw_wait(self, eng, semname, val):
        self._wait(eng, semname, val)

    def raw(self, eng, fn):
        self.q[eng].append(fn)

    def sbuf(self, name, shape, dtype):
        g = self.nc.sbuf_tensor(name, shape, dtype)
        h = g.__enter__()
        self._guards[name] = g
        self._alloc_order = getattr(self, "_alloc_order", [])
        self._alloc_order.append(name)
        pool = list(getattr(self, "_free_deps", []))
        if pool:
            self.readers.setdefault(name, []).extend(pool)
        return h

    def free(self, *names):
        order = {n: i for i, n in enumerate(getattr(self, "_alloc_order", []))}
        names = sorted(names, key=lambda n: -order.get(n, 0))
        for name in names:
            lw = self.lastw.get(name)
            if lw is not None:
                self._free_deps.append(lw)
            self._free_deps.extend(self.readers.get(name, ()))
            g = self._guards.pop(name, None)
            if g is not None:
                g.__exit__(None, None, None)

    def close(self):
        nc = self.nc
        with nc.Block() as block:
            @block.sync
            def _(e):
                for f in self.q["sync"]:
                    f(e)

            @block.scalar
            def _(e):
                for f in self.q["scalar"]:
                    f(e)

            @block.vector
            def _(e):
                for f in self.q["vector"]:
                    f(e)

            @block.tensor
            def _(e):
                for f in self.q["tensor"]:
                    f(e)

            @block.gpsimd
            def _(e):
                for f in self.q["gpsimd"]:
                    f(e)
        return nc


def build_nc(debug_upto=0, upto_phase=99, nt_limit=NT, p1sec=6):
    nc = bass.Bass("TRN2", target_bir_lowering=False)
    es = ExitStack()
    B = Build(nc, es)

    def din(name, shape, dtype=F32):
        return nc.dram_tensor(name, shape, dtype, kind="ExternalInput").ap()

    def dint(name, shape, dtype):
        return nc.dram_tensor(name, shape, dtype).ap()

    def dout(name, shape, dtype=F32):
        return nc.dram_tensor(name, shape, dtype, kind="ExternalOutput").ap()

    x_in = din("x", [TOK, D])
    wqkv_in = din("wqkv", [D, 3 * D])
    bq64_in = din("bq64", [64, 48])
    wout_in = din("wout", [D, D])
    w1_in = din("w1", [D, DFF])
    b1_in = din("b1i", [128, 32])
    w2_in = din("w2", [DFF, D])
    gwT_in = din("gwT", [D, NE])
    we1_in = din("we1", [EPC, D, DFF])
    we2_in = din("we2", [EPC, DFF, D])
    identb_in = din("identb", [128, 128], BF16)
    identf_in = din("identf", [128, 128], F32)
    ltinc_in = din("ltinc", [128, 128])
    oneshift_in = din("oneshift", [128, 255])
    cpfx_in = din("cpfx", [128, NT * 128])
    iota16_in = din("iota16", [128, 16])
    iotarev_in = din("iotarev", [128, 16])
    prevmask_in = din("prevmask", [8, 1])
    slotb_in = din("slotb", [128, 1])

    y_out = dout("y", [TOK, D])

    agin_cnt = dint("agin_cnt", [128, NE], F32)
    agout_cnt = dint("agout_cnt", [8 * 128, NE], F32)
    agin_xn = dint("agin_xn", [TOK, D], BF16)
    agout_xn = dint("agout_xn", [S, D], BF16)
    agin_kt = dint("agin_kt", [64, KV_TOT], BF16)
    agout_kt = dint("agout_kt", [8 * 64, KV_TOT], BF16)
    agin_v = dint("agin_v", [KV_TOT, 64], BF16)
    agout_v = dint("agout_v", [8 * KV_TOT, 64], BF16)
    agin_gx = dint("agin_gx", [NT, 128], F32)
    agout_gx = dint("agout_gx", [8 * NT, 128], F32)
    agin_eo = dint("agin_eo", [SLOTC, D], BF16)
    agout_eo = dint("agout_eo", [NSLOT, D], BF16)

    qt_spill = dint("qt_spill", [64, NH * TOK], BF16)
    attn_spill = dint("attn_spill", [64, NH * TOK], BF16)
    h_spill = dint("h_spill", [TOK, D], F32)
    br_spill = dint("br_spill", [TOK, D], F32)
    disp_dram = dint("disp_dram", [SLOTC, D], BF16)

    dbg = {}
    if debug_upto >= 1:
        dbg["xn"] = dout("dbg_xn", [TOK, D])
        dbg["logit"] = dout("dbg_logit", [128, NT * 16])
        dbg["mask"] = dout("dbg_mask", [128, NT * 16])
        dbg["cnt"] = dout("dbg_cnt", [128, 16])
    if debug_upto >= 2:
        dbg["gidx"] = dout("dbg_gidx", [128, NT])
        dbg["w"] = dout("dbg_w", [128, NT])
        dbg["kt"] = dout("dbg_kt", [64, KV_TOT])
        dbg["v"] = dout("dbg_v", [KV_TOT, 64])
        dbg["qt"] = dout("dbg_qt", [64, NH * 128])
        dbg["base"] = dout("dbg_base", [1, NE])
        dbg["cnt8"] = dout("dbg_cnt8", [8, NE])
        dbg["rank0"] = dout("dbg_rank0", [128, 16])
        dbg["tot"] = dout("dbg_tot", [1, NE])
    if debug_upto >= 3:
        dbg["attn"] = dout("dbg_attn", [64, NH * 128])
    if debug_upto >= 4:
        dbg["h"] = dout("dbg_h", [TOK, D])
        dbg["br"] = dout("dbg_br", [TOK, D])
    if debug_upto >= 5:
        dbg["idx"] = dout("dbg_idx", [128, 128])
        dbg["disp0"] = dout("dbg_disp0", [128, D])
        dbg["eo"] = dout("dbg_eo", [128, 2 * D])

    sbA = nc.alloc_sbuf_tensor

    identb = sbA("identb_s", [128, 128], BF16)
    identf = sbA("identf_s", [128, 128], F32)
    ltinc = sbA("ltinc_s", [128, 128], F32)
    oneshift = sbA("oneshift_s", [128, 255], F32)
    cpfx = sbA("cpfx_s", [128, NT * 128], F32)
    iota16 = sbA("iota16_s", [128, 16], F32)
    iotarev = sbA("iotarev_s", [128, 16], F32)
    prevmask = sbA("prevmask_s", [8, 1], F32)
    slotb = sbA("slotb_s", [128, 1], F32)
    onescol = sbA("onescol", [128, 1], F32)
    onesrow = sbA("onesrow", [1, 128], F32)
    ones64 = sbA("ones64", [128, 64], F32)
    c9 = sbA("c9", [128, 1], F32)
    epsc = sbA("epsc", [128, 1], F32)
    bq64 = sbA("bq64_s", [64, 48], F32)
    b1sb = sbA("b1sb_s", [128, 32], F32)
    gwT = sbA("gwT_s", [128, 8, 16], F32)
    rstd1 = sbA("rstd1", [128, NT], F32)
    negmr1 = sbA("negmr1", [128, NT], F32)
    logit_sb = sbA("logit_sb", [128, NT * 16], F32)
    mask_sb = sbA("mask_sb", [128, NT * 16], F32)
    e_sb = sbA("e_sb", [128, NT], F32)
    wpre_sb = sbA("wpre_sb", [128, NT], F32)
    counts_sb = sbA("counts_sb", [128, 16], F32)
    total_sb = sbA("total_sb", [1, 16], F32)
    cnt8_sb = sbA("cnt8_sb", [8, NE], F32)
    base_sb = sbA("base_sb", [1, NE], F32)
    gidx_sb = sbA("gidx_sb", [128, NT], F32)
    w_sb = sbA("w_sb", [128, NT], F32)
    gidxu = sbA("gidxu", [128, NT], U32)
    gxT_sb = sbA("gxT_sb", [NT, 128], F32)
    idxall_f = sbA("idxall_f", [128, 128], F32)
    idxall_u = sbA("idxall_u", [128, 128], U32)
    zero_bf = sbA("zero_bf", [128, D], BF16)
    t16a = sbA("t16a", [128, 16], F32)
    t16b = sbA("t16b", [128, 16], F32)
    t1a = sbA("t1a", [128, 1], F32)
    t1b = sbA("t1b", [128, 1], F32)
    t1c = sbA("t1c", [128, 1], F32)
    t1d = sbA("t1d", [128, 1], F32)
    t1e = sbA("t1e", [128, 1], F32)
    sum1 = sbA("sum1", [128, 1], F32)
    sumsq = sbA("sumsq", [128, 1], F32)
    rstd2 = sbA("rstd2", [128, 1], F32)
    negmr2 = sbA("negmr2", [128, 1], F32)

    PA = nc.alloc_psum_tensor("PA", [128, 1024], F32)
    PB = nc.alloc_psum_tensor("PB", [128, 1024], F32)
    PC = nc.alloc_psum_tensor("PC", [128, 512], F32)
    PD = nc.alloc_psum_tensor("PD", [128, 512], F32)
    PE2 = nc.alloc_psum_tensor("PE2", [128, 512], F32)
    PF = nc.alloc_psum_tensor("PF", [128, 1024], BF16)

    for (dst, src, nm) in [
        (identb, identb_in, "identb"), (identf, identf_in, "identf"),
        (ltinc, ltinc_in, "ltinc"), (oneshift, oneshift_in, "oneshift"),
        (cpfx, cpfx_in, "cpfx"), (iota16, iota16_in, "iota16"),
        (iotarev, iotarev_in, "iotarev"), (prevmask, prevmask_in, "prevmask"),
        (slotb, slotb_in, "slotb"), (bq64, bq64_in, "bq64"), (b1sb, b1_in, "b1sb"),
    ]:
        B.op("sync", lambda e, d=dst, s=src: e.dma_start(out=d.ap(), in_=s),
             writes=[nm], stream="d_const")
    B.op("sync", lambda e: e.dma_start(
        out=gwT.ap(), in_=gwT_in.rearrange("(c p) e -> p c e", p=128)),
        writes=["gwT"], stream="d_const")
    B.op("vector", lambda e: e.memset(onescol.ap(), 1.0), writes=["onescol"])
    B.op("vector", lambda e: e.memset(onesrow.ap(), 1.0), writes=["onesrow"])
    B.op("vector", lambda e: e.memset(ones64.ap(), 1.0), writes=["ones64"])
    B.op("vector", lambda e: e.memset(c9.ap(), BIGIDX), writes=["c9"])
    B.op("vector", lambda e: e.memset(epsc.ap(), LN_EPS), writes=["epsc"])
    B.op("vector", lambda e: e.memset(zero_bf.ap(), 0.0), writes=["zero_bf"])

    if upto_phase <= 0:
        return finish(B, nc, dict(locals()))

    gxall = B.sbuf("gxall", [128, 128], F32)
    xg = [B.sbuf(f"xg{i}", [128, D], BF16) for i in range(2)]
    xbuf = [B.sbuf(f"xbuf{i}", [128, D], F32) for i in range(2)]
    xsq = B.sbuf("xsq", [128, D], F32)
    xn32 = B.sbuf("xn32", [128, D], F32)
    xnbf = B.sbuf("xnbf", [128, D], BF16)
    xnT32 = B.sbuf("xnT32", [128, D], F32)
    xnT_sb = B.sbuf("xnT_sb", [128, 8 * TOK], BF16)

    for t in range(nt_limit):
        xb, xbn = xbuf[t % 2], f"xbuf{t % 2}"
        B.op("sync", lambda e, t=t, xb=xb: e.dma_start(
            out=xb.ap(), in_=x_in[128 * t:128 * (t + 1), :]), writes=[xbn], stream="d_x")
        B.op("vector", lambda e, xb=xb: e.reduce_sum(sum1.ap(), xb.ap(), axis=AX.X),
             reads=[xbn], writes=["sum1"])
        B.op("scalar", lambda e, xb=xb: e.activation(
            out=xsq.ap(), in_=xb.ap(), func=AF.Square, accum_out=sumsq.ap()),
            reads=[xbn], writes=["xsq", "sumsq"])
        B.op("vector", lambda e: e.tensor_scalar(t1a.ap(), sum1.ap(), 1.0 / D, None, OP.mult),
             reads=["sum1"], writes=["t1a"])
        B.op("vector", lambda e: e.tensor_scalar(t1b.ap(), sumsq.ap(), 1.0 / D, None, OP.mult),
             reads=["sumsq"], writes=["t1b"])
        B.op("vector", lambda e: e.tensor_scalar(t1c.ap(), t1a.ap(), t1a.ap(), -1.0, OP.mult, OP.mult),
             reads=["t1a"], writes=["t1c"])
        B.op("vector", lambda e: e.tensor_add(t1d.ap(), t1b.ap(), t1c.ap()),
             reads=["t1b", "t1c"], writes=["t1d"])
        B.op("scalar", lambda e: e.activation(
            out=t1e.ap(), in_=t1d.ap(), func=AF.Sqrt, bias=epsc.ap()),
            reads=["t1d", "epsc"], writes=["t1e"])
        B.op("vector", lambda e, t=t: e.reciprocal(rstd1.ap()[:, t:t + 1], t1e.ap()),
             reads=["t1e"], writes=["rstd1"])
        B.op("vector", lambda e, t=t: e.tensor_scalar(
            negmr1.ap()[:, t:t + 1], t1a.ap(), rstd1.ap()[:, t:t + 1], -1.0,
            OP.mult, OP.mult), reads=["t1a", "rstd1"], writes=["negmr1"])
        B.op("vector", lambda e, xb=xb, t=t: e.tensor_scalar(
            xn32.ap(), xb.ap(), rstd1.ap()[:, t:t + 1], negmr1.ap()[:, t:t + 1],
            OP.mult, OP.add), reads=[xbn, "rstd1", "negmr1"], writes=["xn32"])
        B.op("scalar", lambda e, xb=xb, t=t: e.activation(
            out=xnbf.ap(), in_=xb.ap(), func=AF.Identity,
            bias=negmr1.ap()[:, t:t + 1], scale=rstd1.ap()[:, t:t + 1]),
            reads=[xbn, "rstd1", "negmr1"], writes=["xnbf"])
        B.op("sync", lambda e, t=t: e.dma_start(
            out=agin_xn[128 * t:128 * (t + 1), :], in_=xnbf.ap()),
            reads=["xnbf"], writes=["agin_xn"], stream="d_xn")
        if debug_upto >= 1:
            B.op("gpsimd", lambda e, t=t: e.dma_start(
                out=dbg["xn"][128 * t:128 * (t + 1), :], in_=xnbf.ap()),
                reads=["xnbf"], stream="d_dbg")
        if p1sec < 2:
            continue
        for c in range(8):
            slot = c % 2
            pn = "PC" if slot == 0 else "PD"
            ptile = PC if slot == 0 else PD
            B.op("tensor", lambda e, c=c, ptile=ptile: e.transpose(
                ptile.ap()[:, 0:128],
                xn32.ap()[:, 128 * c:128 * (c + 1)], identf.ap()),
                reads=["xn32", "identf"], writes=[pn])
            B.op("vector", lambda e, c=c, ptile=ptile: e.tensor_copy(
                xnT32.ap()[:, 128 * c:128 * (c + 1)],
                ptile.ap()[:, 0:128]),
                reads=[pn], writes=["xnT32"])
            B.op("scalar", lambda e, c=c, t=t: e.activation(
                out=xnT_sb.ap()[:, c * TOK + 128 * t: c * TOK + 128 * (t + 1)],
                in_=xnT32.ap()[:, 128 * c:128 * (c + 1)], func=AF.Copy),
                reads=["xnT32"], writes=["xnT_sb"])
        if (p1sec if p1sec < 20 else 2) < 3:
            continue
        for c in range(8):
            B.op("tensor", lambda e, c=c: e.matmul(
                PE2.ap()[:, 0:16], xnT32.ap()[:, 128 * c:128 * (c + 1)],
                gwT.ap()[:, c, :], start=(c == 0), stop=(c == 7)),
                reads=["xnT32", "gwT"] + (["PE2"] if c == 0 else []),
                writes=["PE2"])
        B.op("vector", lambda e, t=t: e.tensor_copy(
            logit_sb.ap()[:, 16 * t:16 * (t + 1)], PE2.ap()[:, 0:16]),
            reads=["PE2"], writes=["logit_sb"])
        if (p1sec if p1sec < 20 else 2) < 4:
            continue
        B.op("vector", lambda e, t=t: e.reduce_max(
            t1a.ap(), logit_sb.ap()[:, 16 * t:16 * (t + 1)], axis=AX.X),
            reads=["logit_sb"], writes=["t1a"])
        B.op("vector", lambda e, t=t: e.tensor_reduce(
            t1b.ap(), logit_sb.ap()[:, 16 * t:16 * (t + 1)], axis=AX.X,
            op=OP.max, negate=True), reads=["logit_sb"], writes=["t1b"])
        B.op("scalar", lambda e, t=t: e.activation(
            out=t16a.ap(), in_=logit_sb.ap()[:, 16 * t:16 * (t + 1)], func=AF.Exp,
            bias=t1b.ap(), accum_out=t1c.ap()),
            reads=["logit_sb", "t1b"], writes=["t16a", "t1c"])
        B.op("vector", lambda e, t=t: e.reciprocal(wpre_sb.ap()[:, t:t + 1], t1c.ap()),
             reads=["t1c"], writes=["wpre_sb"])
        B.op("vector", lambda e, t=t: e.tensor_scalar(
            t16b.ap(), logit_sb.ap()[:, 16 * t:16 * (t + 1)], t1a.ap(), None,
            OP.is_equal), reads=["logit_sb", "t1a"], writes=["t16b"])
        B.op("vector", lambda e: e.tensor_mul(t16b.ap(), t16b.ap(), iotarev.ap()),
             reads=["t16b", "iotarev"], writes=["t16b"])
        B.op("vector", lambda e: e.reduce_max(t1d.ap(), t16b.ap(), axis=AX.X),
             reads=["t16b"], writes=["t1d"])
        B.op("vector", lambda e, t=t: e.tensor_scalar(
            e_sb.ap()[:, t:t + 1], t1d.ap(), -1.0, 16.0, OP.mult, OP.add),
            reads=["t1d"], writes=["e_sb"])
        B.op("vector", lambda e, t=t: e.tensor_scalar(
            mask_sb.ap()[:, 16 * t:16 * (t + 1)], iota16.ap(), e_sb.ap()[:, t:t + 1],
            None, OP.is_equal), reads=["iota16", "e_sb"], writes=["mask_sb"])
        B.op("tensor", lambda e, t=t: e.matmul(
            PA.ap()[:, 0:16], oneshift.ap()[:, 127 - t:255 - t],
            mask_sb.ap()[:, 16 * t:16 * (t + 1)], start=(t == 0), stop=(t == nt_limit - 1)),
            reads=["mask_sb", "oneshift"] + (["PA0"] if t == 0 else []),
            writes=["PA0"])
        B.op("tensor", lambda e, t=t: e.matmul(
            PA.ap()[0:1, 512:528], onescol.ap(),
            mask_sb.ap()[:, 16 * t:16 * (t + 1)], start=(t == 0), stop=(t == nt_limit - 1)),
            reads=["mask_sb", "onescol"] + (["PA1"] if t == 0 else []),
            writes=["PA1"])

    if p1sec >= 5 and nt_limit == NT:
        B.op("vector", lambda e: e.tensor_copy(counts_sb.ap(), PA.ap()[:, 0:16]),
             reads=["PA0"], writes=["counts_sb"])
        B.op("vector", lambda e: e.tensor_copy(total_sb.ap(), PA.ap()[0:1, 512:528]),
             reads=["PA1"], writes=["total_sb"])
        B.op("vector", lambda e: e.memset(t16a.ap(), 7777.0), reads=[], writes=["t16a"])
        B.op("sync", lambda e: e.dma_start(out=agin_cnt[1:128, :], in_=t16a.ap()[1:128, :]),
             reads=["t16a"], writes=["agin_cnt"], stream="d_cnt")
        B.op("sync", lambda e: e.dma_start(out=agin_cnt[0:1, :], in_=total_sb.ap()),
             reads=["total_sb"], writes=["agin_cnt"], stream="d_cnt")
    if debug_upto >= 1 and p1sec >= 5 and nt_limit == NT:
        B.op("sync", lambda e: e.dma_start(out=dbg["cnt"], in_=counts_sb.ap()),
             reads=["counts_sb"], stream="d_dbg2")
        B.op("sync", lambda e: e.dma_start(out=dbg["logit"], in_=logit_sb.ap()),
             reads=["logit_sb"], stream="d_dbg2")
        B.op("sync", lambda e: e.dma_start(out=dbg["mask"], in_=mask_sb.ap()),
             reads=["mask_sb"], stream="d_dbg2")
    if p1sec >= 6 and nt_limit == NT:
        B.op("gpsimd", lambda e: e.collective_compute(
            "AllGather", OP.bypass, replica_groups=[list(range(NCORES))],
            ins=[agin_cnt.opt()], outs=[agout_cnt.opt()]),
            reads=["agin_cnt"], writes=["agout_cnt"], sem=B.ccsem("scc0"))

    if upto_phase <= 1:
        return finish(B, nc, dict(locals()))

    wqsg = [B.sbuf(f"wqsg{i}", [128, 8, 512], BF16) for i in range(2)]
    qth = [B.sbuf(f"qth{i}", [64, TOK], BF16) for i in range(2)]
    wkh = [B.sbuf(f"wkh{i}", [128, 8, 64], BF16) for i in range(2)]
    kevac = [B.sbuf(f"kevac{i}", [64, 512], BF16) for i in range(2)]
    wvg = [B.sbuf(f"wvg{i}", [128, 8, 256], BF16) for i in range(2)]
    vevac = [B.sbuf(f"vevac{i}", [128, 256], BF16) for i in range(2)]

    for sgi in range(2):
        wq, wqn = wqsg[sgi % 2], f"wqsg{sgi % 2}"
        for c in range(8):
            B.op("gpsimd", lambda e, sgi=sgi, c=c, wq=wq: e.dma_start(
                out=wq.ap()[:, c, :],
                in_=wqkv_in[128 * c:128 * (c + 1), 512 * sgi:512 * (sgi + 1)]),
                writes=[wqn], stream="d_wq")
        for hh in range(8):
            h = 8 * sgi + hh
            qb, qbn = qth[h % 2], f"qth{h % 2}"
            for n in range(4):
                ptile = PC if n % 2 == 0 else PD
                pn = "PC" if n % 2 == 0 else "PD"
                for c in range(8):
                    B.op("tensor", lambda e, c=c, hh=hh, n=n, wq=wq, ptile=ptile: e.matmul(
                        ptile.ap()[0:64, :],
                        wq.ap()[:, c, 64 * hh:64 * (hh + 1)],
                        xnT_sb.ap()[:, c * TOK + 512 * n:c * TOK + 512 * (n + 1)],
                        start=(c == 0), stop=(c == 7)),
                        reads=[wqn, "xnT_sb"] + ([pn] if c == 0 else []),
                        writes=[pn])
                B.op("scalar", lambda e, h=h, n=n, qb=qb, ptile=ptile: e.activation(
                    out=qb.ap()[:, 512 * n:512 * (n + 1)],
                    in_=ptile.ap()[0:64, :], func=AF.Identity,
                    bias=bq64.ap()[:, h:h + 1]),
                    reads=[pn, "bq64"], writes=[qbn])
            B.op("sync", lambda e, h=h, qb=qb: e.dma_start(
                out=qt_spill[:, TOK * h:TOK * (h + 1)], in_=qb.ap()),
                reads=[qbn], writes=["qt_spill"], stream="d_qt")
            if debug_upto >= 2:
                B.op("gpsimd", lambda e, h=h, qb=qb: e.dma_start(
                    out=dbg["qt"][:, 128 * h:128 * (h + 1)], in_=qb.ap()[:, 0:128]),
                    reads=[qbn], stream="d_dbg")

    for h in range(NH):
        stv = STRIDES[h]
        L = TOK // stv
        wk, wkn = wkh[h % 2], f"wkh{h % 2}"
        for c in range(8):
            B.op("gpsimd", lambda e, h=h, c=c, wk=wk: e.dma_start(
                out=wk.ap()[:, c, :],
                in_=wqkv_in[128 * c:128 * (c + 1), D + 64 * h:D + 64 * (h + 1)]),
                writes=[wkn], stream="d_wk")
        nt_ = (L + 511) // 512
        for n in range(nt_):
            cnt_ = min(512, L - 512 * n)
            kb, kbn = kevac[n % 2], f"kevac{n % 2}"
            pn = "PE2"
            for c in range(8):
                B.op("tensor", lambda e, c=c, n=n, cnt_=cnt_, stv=stv, wk=wk: e.matmul(
                    PE2.ap()[0:64, 0:cnt_], wk.ap()[:, c, :],
                    bass.AP(xnT_sb, c * TOK + 512 * n * stv, [[8 * TOK, 128], [stv, cnt_]]),
                    start=(c == 0), stop=(c == 7)),
                    reads=[wkn, "xnT_sb"] + ([pn] if c == 0 else []),
                    writes=[pn])
            B.op("scalar", lambda e, cnt_=cnt_, kb=kb: e.activation(
                out=kb.ap()[:, 0:cnt_], in_=PE2.ap()[0:64, 0:cnt_], func=AF.Copy),
                reads=[pn], writes=[kbn])
            B.op("sync", lambda e, h=h, n=n, cnt_=cnt_, kb=kb: e.dma_start(
                out=agin_kt[:, KV_OFF[h] + 512 * n:KV_OFF[h] + 512 * n + cnt_],
                in_=kb.ap()[:, 0:cnt_]),
                reads=[kbn], writes=["agin_kt"], stream="d_kt")
            if debug_upto >= 2:
                B.op("gpsimd", lambda e, h=h, n=n, cnt_=cnt_, kb=kb: e.dma_start(
                    out=dbg["kt"][:, KV_OFF[h] + 512 * n:KV_OFF[h] + 512 * n + cnt_],
                    in_=kb.ap()[:, 0:cnt_]), reads=[kbn], stream="d_dbg")

    for gi, (h0, gh, stv) in enumerate(VGROUPS):
        L = TOK // stv
        wv, wvn = wvg[gi % 2], f"wvg{gi % 2}"
        gn = 64 * gh
        for c in range(8):
            B.op("gpsimd", lambda e, h0=h0, gn=gn, c=c, wv=wv: e.dma_start(
                out=wv.ap()[:, c, 0:gn],
                in_=wqkv_in[128 * c:128 * (c + 1), 2 * D + 64 * h0:2 * D + 64 * h0 + gn]),
                writes=[wvn], stream="d_wv")
        for tl in range(L // 128):
            vb, vbn = vevac[tl % 2], f"vevac{tl % 2}"
            pn = "PF"
            for c in range(8):
                B.op("tensor", lambda e, c=c, tl=tl, stv=stv, gn=gn, wv=wv: e.matmul(
                    PF.bitcast(F32).ap()[:, 0:gn],
                    bass.AP(xnT_sb, c * TOK + 128 * tl * stv, [[8 * TOK, 128], [stv, 128]]),
                    wv.ap()[:, c, 0:gn], start=(c == 0), stop=(c == 7)),
                    reads=[wvn, "xnT_sb"] + ([pn] if c == 0 else []),
                    writes=[pn])
            B.op("scalar", lambda e, gn=gn, vb=vb: e.activation(
                out=vb.ap()[:, 0:gn], in_=PF.bitcast(F32).ap()[:, 0:gn], func=AF.Copy),
                reads=[pn], writes=[vbn])
            for hh in range(gh):
                h = h0 + hh
                B.op("sync", lambda e, h=h, hh=hh, tl=tl, vb=vb: e.dma_start(
                    out=agin_v[KV_OFF[h] + 128 * tl:KV_OFF[h] + 128 * (tl + 1), :],
                    in_=vb.ap()[:, 64 * hh:64 * (hh + 1)]),
                    reads=[vbn], writes=["agin_v"], stream="d_v")
                if debug_upto >= 2:
                    B.op("gpsimd", lambda e, h=h, hh=hh, tl=tl, vb=vb: e.dma_start(
                        out=dbg["v"][KV_OFF[h] + 128 * tl:KV_OFF[h] + 128 * (tl + 1), :],
                        in_=vb.ap()[:, 64 * hh:64 * (hh + 1)]), reads=[vbn], stream="d_dbg")

    B.op("sync", lambda e: e.dma_start(out=cnt8_sb.ap(), in_=bass.AP(agout_cnt.tensor, 0, [[128 * NE, 8], [1, NE]])),
         reads=["agout_cnt"], writes=["cnt8_sb"], stream="d_cnt")
    B.op("tensor", lambda e: e.matmul(
        PE2.ap()[0:1, 0:16], prevmask.ap(), cnt8_sb.ap(), start=True, stop=True),
        reads=["prevmask", "cnt8_sb", "PE2"], writes=["PE2"])
    B.op("vector", lambda e: e.tensor_copy(base_sb.ap(), PE2.ap()[0:1, 0:16]),
         reads=["PE2"], writes=["base_sb"])
    if debug_upto >= 2:
        B.op("sync", lambda e: e.dma_start(out=dbg["tot"], in_=total_sb.ap()),
             reads=["total_sb"], stream="d_dbg2")
        B.op("sync", lambda e: e.dma_start(out=dbg["base"], in_=base_sb.ap()),
             reads=["base_sb"], stream="d_dbg2")
        B.op("sync", lambda e: e.dma_start(out=dbg["cnt8"], in_=cnt8_sb.ap()),
             reads=["cnt8_sb"], stream="d_dbg2")
    for t in range(NT):
        pn = "PB0"
        B.op("tensor", lambda e, t=t: e.matmul(
            PB.ap()[:, 0:16], ltinc.ap(), mask_sb.ap()[:, 16 * t:16 * (t + 1)],
            start=True, stop=False),
            reads=["ltinc", "mask_sb", pn], writes=[pn])
        B.op("tensor", lambda e, t=t: e.matmul(
            PB.ap()[:, 0:16], cpfx.ap()[:, 128 * t:128 * (t + 1)], counts_sb.ap(),
            start=False, stop=False), reads=["cpfx", "counts_sb"], writes=[pn])
        B.op("tensor", lambda e: e.matmul(
            PB.ap()[:, 0:16], onesrow.ap(), base_sb.ap(),
            start=False, stop=True), reads=["onesrow", "base_sb"], writes=[pn])
        if debug_upto >= 2 and t == 0:
            B.op("vector", lambda e: e.tensor_copy(t16b.ap(), pb[2].ap()[:, 0:16]),
                 reads=[pn], writes=["t16b"])
            B.op("sync", lambda e: e.dma_start(out=dbg["rank0"], in_=t16b.ap()),
                 reads=["t16b"], stream="d_dbg2")
        B.op("vector", lambda e: e.tensor_scalar(
            t16a.ap(), PB.ap()[:, 0:16], CAP + 0.5, None, OP.is_le),
            reads=[pn], writes=["t16a"])
        B.op("vector", lambda e, t=t: e.tensor_mul(
            t16b.ap(), mask_sb.ap()[:, 16 * t:16 * (t + 1)], t16a.ap()),
            reads=["mask_sb", "t16a"], writes=["t16b"])
        B.op("vector", lambda e: e.scalar_tensor_tensor(
            t16a.ap(), t16b.ap(), 1.0, PB.ap()[:, 0:16], OP.mult, OP.mult,
            accum_out=t1a.ap()),
            reads=["t16b", pn], writes=["t16a", "t1a"])
        B.op("vector", lambda e: e.reduce_sum(t1b.ap(), t16b.ap(), axis=AX.X),
             reads=["t16b"], writes=["t1b"])
        B.op("vector", lambda e: e.tensor_sub(t1c.ap(), t1a.ap(), t1b.ap()),
             reads=["t1a", "t1b"], writes=["t1c"])
        B.op("vector", lambda e, t=t: e.tensor_scalar(
            t1d.ap(), e_sb.ap()[:, t:t + 1], CAP * 1.0, None, OP.mult),
            reads=["e_sb"], writes=["t1d"])
        B.op("vector", lambda e: e.tensor_add(t1d.ap(), t1d.ap(), t1c.ap()),
             reads=["t1c"], writes=["t1d"])
        B.op("vector", lambda e: e.tensor_scalar(
            t1e.ap(), t1d.ap(), BIGIDX, None, OP.subtract),
            reads=["t1d"], writes=["t1e"])
        B.op("vector", lambda e, t=t: e.scalar_tensor_tensor(
            gidx_sb.ap()[:, t:t + 1], t1e.ap(), t1b.ap(), c9.ap(), OP.mult, OP.add),
            reads=["t1e", "t1b", "c9"], writes=["gidx_sb"])
        B.op("vector", lambda e, t=t: e.tensor_mul(
            w_sb.ap()[:, t:t + 1], wpre_sb.ap()[:, t:t + 1], t1b.ap()),
            reads=["wpre_sb", "t1b"], writes=["w_sb"])
    B.op("tensor", lambda e: e.transpose(
        PB.ap()[0:16, 0:128], gidx_sb.ap(), identf.ap()),
        reads=["gidx_sb", "identf", "PB0"], writes=["PB0"])
    B.op("vector", lambda e: e.tensor_copy(gxT_sb.ap(), PB.ap()[0:16, 0:128]),
         reads=["PB0"], writes=["gxT_sb"])
    B.op("sync", lambda e: e.dma_start(out=agin_gx, in_=gxT_sb.ap()),
         reads=["gxT_sb"], writes=["agin_gx"], stream="d_gx")
    if debug_upto >= 2:
        B.op("sync", lambda e: e.dma_start(out=dbg["gidx"], in_=gidx_sb.ap()),
             reads=["gidx_sb"], stream="d_dbg2")
        B.op("sync", lambda e: e.dma_start(out=dbg["w"], in_=w_sb.ap()),
             reads=["w_sb"], stream="d_dbg2")
    for i, (ain, aout, ainn, aoutn) in enumerate([
            (agin_kt, agout_kt, "agin_kt", "agout_kt"),
            (agin_v, agout_v, "agin_v", "agout_v"),
            (agin_gx, agout_gx, "agin_gx", "agout_gx"),
            (agin_xn, agout_xn, "agin_xn", "agout_xn")]):
        B.op("gpsimd", lambda e, ain=ain, aout=aout: e.collective_compute(
            "AllGather", OP.bypass, replica_groups=[list(range(NCORES))],
            ins=[ain.opt()], outs=[aout.opt()]),
            reads=[ainn], writes=[aoutn], sem=B.ccsem(f"scc{i + 1}"))

    REGS = {}

    def _mkregs(e):
        REGS["bnd_scat"] = e.alloc_register("bnd_scat")
        e.reg_mov(REGS["bnd_scat"], SLOTC - 1)
        REGS["bnd_gath"] = e.alloc_register("bnd_gath")
        e.reg_mov(REGS["bnd_gath"], NSLOT - 1)
    B.raw("gpsimd", _mkregs)

    B.op("sync", lambda e: e.dma_start(out=gxall.ap(), in_=agout_gx),
         reads=["agout_gx"], writes=["gxall"], stream="d_gxld")
    B.op("tensor", lambda e: e.transpose(
        PC.ap()[:, 0:128], gxall.ap(), identf.ap()),
        reads=["gxall", "identf", "PC"], writes=["PC"])
    B.op("vector", lambda e: e.tensor_copy(idxall_f.ap(), PC.ap()[:, 0:128]),
         reads=["PC"], writes=["idxall_f"])
    B.op("vector", lambda e: e.tensor_scalar(
        idxall_f.ap(), idxall_f.ap(), slotb.ap(), None, OP.subtract),
        reads=["idxall_f", "slotb"], writes=["idxall_f"])
    B.op("vector", lambda e: e.tensor_scalar(
        idxall_u.ap().bitcast(F32), idxall_f.ap(), 0.0, None, OP.is_lt),
        reads=["idxall_f"], writes=["idxall_u"])
    B.op("vector", lambda e: e.scalar_tensor_tensor(
        idxall_f.ap(), idxall_u.ap().bitcast(F32), 200000.0, idxall_f.ap(),
        OP.mult, OP.add),
        reads=["idxall_u", "idxall_f"], writes=["idxall_f"])
    B.op("vector", lambda e: e.tensor_copy(idxall_u.ap(), idxall_f.ap()),
         reads=["idxall_f"], writes=["idxall_u"])
    if debug_upto >= 5:
        B.op("sync", lambda e: e.dma_start(out=dbg["idx"], in_=idxall_f.ap()),
             reads=["idxall_f"], stream="d_dbg2")

    for sl in range(SLOTC // 128):
        B.op("sync", lambda e, sl=sl: e.dma_start(
            out=disp_dram[128 * sl:128 * (sl + 1), :], in_=zero_bf.ap()),
            reads=["zero_bf"], writes=["disp_dram"], stream="d_dz")
    for gt in range(S // 128):
        xgb, xgn = xg[gt % 2], f"xg{gt % 2}"
        B.op("sync", lambda e, gt=gt, xgb=xgb: e.dma_start(
            out=xgb.ap(), in_=agout_xn[128 * gt:128 * (gt + 1), :]),
            reads=["agout_xn"], writes=[xgn], stream="d_xg")
        B.op("gpsimd", lambda e, gt=gt, xgb=xgb: e.indirect_dma_start(
            out=disp_dram,
            out_offset=IndirectOffsetOnAxis(ap=idxall_u.ap()[:, gt:gt + 1], axis=0),
            in_=xgb.ap(), in_offset=None,
            bounds_check=REGS["bnd_scat"], oob_is_err=False),
            reads=[xgn, "idxall_u", "disp_dram"], writes=["disp_dram"],
            stream="d_scat")


    if upto_phase <= 2:
        return finish(B, nc, dict(locals()))

    B.free("xbuf0", "xbuf1", "xsq", "xn32", "xnbf", "xnT32", "xnT_sb",
           "wqsg0", "wqsg1", "qth0", "qth1", "wkh0", "wkh1", "kevac0", "kevac1",
           "wvg0", "wvg1", "vevac0", "vevac1")
    for nm in ["xbuf0", "xbuf1", "xsq", "xn32", "xnbf", "xnT32", "xnT_sb",
               "wqsg0", "wqsg1", "qth0", "qth1", "wkh0", "wkh1", "kevac0",
               "kevac1", "wvg0", "wvg1", "vevac0", "vevac1"]:
        pass

    ktb = [B.sbuf(f"ktb{i}", [64, S], BF16) for i in range(2)]
    vaugb = [B.sbuf(f"vaugb{i}", [128, 128 * 65], BF16) for i in range(2)]
    qtb = [B.sbuf(f"qtb{i}", [64, TOK], BF16) for i in range(2)]
    ptb = [B.sbuf(f"ptb{i}", [128, 1024], BF16) for i in range(2)]
    atmp = B.sbuf("atmp", [64, 512], BF16)
    atb = B.sbuf("atb", [64, TOK], BF16)
    rsum_sb = B.sbuf("rsum_sb", [128, TOK], F32)
    rsum0_sb = B.sbuf("rsum0_sb", [1, TOK], F32)
    recips = B.sbuf("recips", [1, TOK], F32)
    eshift = B.sbuf("eshift", [128, 1], F32)
    B.op("vector", lambda e: e.memset(eshift.ap(), EXP_SHIFT), writes=["eshift"])
    for i in range(2):
        B.op("vector", lambda e, i=i: e.memset(
            bass.AP(vaugb[i], 64, [[128 * 65, 128], [65, 128]]), 1.0),
            writes=[f"vaugb{i}_ones"])

    for h in range(NH):
        stv = STRIDES[h]
        L = S // stv
        Lloc = TOK // stv
        kt, ktn = ktb[h % 2], f"ktb{h % 2}"
        va, van = vaugb[h % 2], f"vaugb{h % 2}"
        qb, qbn = qtb[h % 2], f"qtb{h % 2}"
        B.op("sync", lambda e, h=h, kt=kt, Lloc=Lloc: e.dma_start(
            out=kt.ap()[:, 0:8 * Lloc],
            in_=bass.AP(agout_kt.tensor, KV_OFF[h],
                        [[KV_TOT, 64], [64 * KV_TOT, 8], [1, Lloc]])),
            reads=["agout_kt"], writes=[ktn], stream="d_kld")
        nck = Lloc // 128
        for rblk in range(8):
            B.op("sync", lambda e, h=h, va=va, nck=nck, rblk=rblk: e.dma_start(
                out=bass.AP(va, rblk * nck * 65, [[128 * 65, 128], [65, nck], [1, 64]]),
                in_=bass.AP(agout_v.tensor, 64 * (KV_OFF[h] + KV_TOT * rblk),
                            [[64, 128], [64 * 128, nck], [1, 64]])),
                reads=["agout_v", f"vaugb{h % 2}_ones"], writes=[van], stream="d_vld")
        B.op("sync", lambda e, h=h, qb=qb: e.dma_start(
            out=qb.ap(), in_=qt_spill[:, TOK * h:TOK * (h + 1)]),
            reads=["qt_spill"], writes=[qbn], stream="d_qld")
        NKC = L // 128
        NP = NKC // 2
        for qt in range(4):
            ab = PC if qt % 2 == 0 else PD
            abn = "PC" if qt % 2 == 0 else "PD"

            def emit_st(p):
                pr = PA if p % 2 == 0 else PB
                prn0 = ("PA0", "PA1") if p % 2 == 0 else ("PB0", "PB1")
                for half in range(2):
                    kc = 2 * p + half
                    B.op("tensor", lambda e, kc=kc, half=half, pr=pr, kt=kt, qb=qb, qt=qt: e.matmul(
                        pr.ap()[:, 512 * half:512 * (half + 1)],
                        kt.ap()[:, 128 * kc:128 * (kc + 1)],
                        qb.ap()[:, 512 * qt:512 * (qt + 1)], start=True, stop=True),
                        reads=[ktn, qbn, prn0[half]], writes=[prn0[half]])

            def emit_exp(p):
                pr = PA if p % 2 == 0 else PB
                prn = ("PA0", "PA1") if p % 2 == 0 else ("PB0", "PB1")
                B.op("scalar", lambda e, p=p, pr=pr: e.activation(
                    out=ptb[p % 2].ap(), in_=pr.ap(), func=AF.Exp,
                    bias=eshift.ap(), scale=EXP_SCALE),
                    reads=[prn[0], prn[1], "eshift"], writes=[f"ptb{p % 2}"])

            def emit_pv(p):
                for half in range(2):
                    kc = 2 * p + half
                    B.op("tensor", lambda e, kc=kc, half=half, p=p, va=va, ab=ab, NP=NP: e.matmul(
                        ab.ap()[0:65, 0:512], va.ap()[:, 65 * kc:65 * (kc + 1)],
                        ptb[p % 2].ap()[:, 512 * half:512 * (half + 1)],
                        start=(kc == 0), stop=(kc == 2 * NP - 1)),
                        reads=[van, f"ptb{p % 2}"] + ([abn] if kc == 0 else []),
                        writes=[abn])

            for p in range(NP + 1):
                if p < NP:
                    emit_st(p)
                    emit_exp(p)
                if p >= 1:
                    emit_pv(p - 1)
            B.op("vector", lambda e, qt=qt, ab=ab: e.tensor_copy(
                rsum_sb.ap()[64:65, 512 * qt:512 * (qt + 1)],
                ab.ap()[64:65, 0:512]),
                reads=[abn], writes=["rsum_sb"])
            B.op("sync", lambda e, qt=qt: e.dma_start(
                out=rsum0_sb.ap()[:, 512 * qt:512 * (qt + 1)],
                in_=rsum_sb.ap()[64:65, 512 * qt:512 * (qt + 1)]),
                reads=["rsum_sb"], writes=["rsum0_sb"], stream="d_rs")
            B.op("vector", lambda e, qt=qt: e.reciprocal(
                recips.ap()[:, 512 * qt:512 * (qt + 1)],
                rsum0_sb.ap()[:, 512 * qt:512 * (qt + 1)]),
                reads=["rsum0_sb"], writes=["recips"])
            B.op("tensor", lambda e, qt=qt: e.matmul(
                PE2.ap()[0:64, 0:512], onesrow.ap()[:, 0:64],
                recips.ap()[:, 512 * qt:512 * (qt + 1)], start=True, stop=True),
                reads=["recips", "onesrow", "PE2"], writes=["PE2"])
            B.op("vector", lambda e, qt=qt, ab=ab: e.tensor_copy(
                atmp.ap(), ab.ap()[0:64, 0:512]),
                reads=[abn], writes=["atmp"])
            B.op("vector", lambda e, qt=qt: e.tensor_mul(
                atb.ap()[:, 512 * qt:512 * (qt + 1)], atmp.ap(), PE2.ap()[0:64, 0:512]),
                reads=["atmp", "PE2"], writes=["atb"])
        B.op("sync", lambda e, h=h: e.dma_start(
            out=attn_spill[:, TOK * h:TOK * (h + 1)], in_=atb.ap()),
            reads=["atb"], writes=["attn_spill"], stream="d_attnst")
        if debug_upto >= 3:
            B.op("gpsimd", lambda e, h=h: e.dma_start(
                out=dbg["attn"][:, 128 * h:128 * (h + 1)], in_=atb.ap()[:, 0:128]),
                reads=["atb"], stream="d_dbg")

    if upto_phase <= 3:
        return finish(B, nc, dict(locals()))

    B.free("ktb0", "ktb1", "vaugb0", "vaugb1", "qtb0", "qtb1", "ptb0", "ptb1",
           "atmp", "atb", "rsum_sb", "rsum0_sb", "recips", "eshift")
    hT_sb = B.sbuf("hT_sb", [128, 8 * TOK], BF16)
    wout_sb = B.sbuf("wout_sb", [64, NH * 1024], BF16)
    attn_ld = [B.sbuf(f"attn_ld{i}", [64, NH * 512], BF16) for i in range(2)]
    xb2 = [B.sbuf(f"xb2_{i}", [128, D], F32) for i in range(2)]
    hpre = B.sbuf("hpre", [128, D], F32)
    hsq = B.sbuf("hsq", [128, D], F32)
    htile = B.sbuf("htile", [128, D], F32)

    for h in range(NH):
        B.op("gpsimd", lambda e, h=h: e.dma_start(
            out=wout_sb.ap()[:, 1024 * h:1024 * (h + 1)],
            in_=wout_in[64 * h:64 * (h + 1), :]),
            writes=["wout_sb"], stream="d_wout")

    def ln_stats(src, srcn, rstd_ap, negmr_ap, wr_r, wr_n):
        B.op("vector", lambda e: e.reduce_sum(sum1.ap(), src.ap(), axis=AX.X),
             reads=[srcn], writes=["sum1"])
        B.op("scalar", lambda e: e.activation(
            out=hsq.ap(), in_=src.ap(), func=AF.Square, accum_out=sumsq.ap()),
            reads=[srcn], writes=["hsq", "sumsq"])
        B.op("vector", lambda e: e.tensor_scalar(t1a.ap(), sum1.ap(), 1.0 / D, None, OP.mult),
             reads=["sum1"], writes=["t1a"])
        B.op("vector", lambda e: e.tensor_scalar(t1b.ap(), sumsq.ap(), 1.0 / D, None, OP.mult),
             reads=["sumsq"], writes=["t1b"])
        B.op("vector", lambda e: e.tensor_scalar(t1c.ap(), t1a.ap(), t1a.ap(), -1.0, OP.mult, OP.mult),
             reads=["t1a"], writes=["t1c"])
        B.op("vector", lambda e: e.tensor_add(t1d.ap(), t1b.ap(), t1c.ap()),
             reads=["t1b", "t1c"], writes=["t1d"])
        B.op("scalar", lambda e: e.activation(
            out=t1e.ap(), in_=t1d.ap(), func=AF.Sqrt, bias=epsc.ap()),
            reads=["t1d", "epsc"], writes=["t1e"])
        B.op("vector", lambda e: e.reciprocal(rstd_ap, t1e.ap()),
             reads=["t1e"], writes=[wr_r])
        B.op("vector", lambda e: e.tensor_scalar(
            negmr_ap, t1a.ap(), rstd_ap, -1.0, OP.mult, OP.mult),
            reads=["t1a", wr_r], writes=[wr_n])

    for tg in range(4):
        al, aln = attn_ld[tg % 2], f"attn_ld{tg % 2}"
        B.op("sync", lambda e, tg=tg, al=al: e.dma_start(
            out=al.ap(),
            in_=bass.AP(attn_spill.tensor, 512 * tg, [[NH * TOK, 64], [TOK, NH], [1, 512]])),
            reads=["attn_spill"], writes=[aln], stream="d_attnld")
        for tt in range(4):
            t = 4 * tg + tt
            xb, xbn = xb2[t % 2], f"xb2_{t % 2}"
            B.op("sync", lambda e, t=t, xb=xb: e.dma_start(
                out=xb.ap(), in_=x_in[128 * t:128 * (t + 1), :]),
                writes=[xbn], stream="d_x2")
            wp = PA if t % 2 == 0 else PB
            wpn = ("PA0", "PA1") if t % 2 == 0 else ("PB0", "PB1")
            for n in range(2):
                pn = wpn[n]
                for h in range(NH):
                    B.op("tensor", lambda e, h=h, n=n, tt=tt, al=al, wp=wp: e.matmul(
                        wp.ap()[:, 512 * n:512 * (n + 1)],
                        al.ap()[:, 512 * h + 128 * tt:512 * h + 128 * (tt + 1)],
                        wout_sb.ap()[:, 1024 * h + 512 * n:1024 * h + 512 * (n + 1)],
                        start=(h == 0), stop=(h == NH - 1)),
                        reads=[aln, "wout_sb"] + ([pn] if h == 0 else []),
                        writes=[pn])
            B.op("vector", lambda e, t=t, xb=xb: e.tensor_scalar(
                htile.ap(), xb.ap(), rstd1.ap()[:, t:t + 1], negmr1.ap()[:, t:t + 1],
                OP.mult, OP.add), reads=[xbn, "rstd1", "negmr1"], writes=["htile"])
            for n in range(2):
                B.op("vector", lambda e, n=n, wp=wp: e.tensor_add(
                    hpre.ap()[:, 512 * n:512 * (n + 1)], wp.ap()[:, 512 * n:512 * (n + 1)],
                    htile.ap()[:, 512 * n:512 * (n + 1)]),
                    reads=[wpn[n], "htile"], writes=["hpre"])
            ln_stats(hpre, "hpre", rstd2.ap(), negmr2.ap(), "rstd2", "negmr2")
            B.op("scalar", lambda e: e.activation(
                out=htile.ap(), in_=hpre.ap(), func=AF.Identity,
                bias=negmr2.ap(), scale=rstd2.ap()),
                reads=["hpre", "rstd2", "negmr2"], writes=["htile"])
            B.op("sync", lambda e, t=t: e.dma_start(
                out=h_spill[128 * t:128 * (t + 1), :], in_=htile.ap()),
                reads=["htile"], writes=["h_spill"], stream="d_h")
            if debug_upto >= 4:
                B.op("sync", lambda e, t=t: e.dma_start(
                    out=dbg["h"][128 * t:128 * (t + 1), :], in_=htile.ap()),
                    reads=["htile"], stream="d_dbg2")
            for c in range(8):
                ptile = PC if c % 2 == 0 else PD
                pnn = "PC" if c % 2 == 0 else "PD"
                B.op("tensor", lambda e, c=c, ptile=ptile: e.transpose(
                    ptile.ap()[:, 0:128],
                    htile.ap()[:, 128 * c:128 * (c + 1)], identf.ap()),
                    reads=["htile", "identf", pnn], writes=[pnn])
                B.op("scalar", lambda e, c=c, t=t, ptile=ptile: e.activation(
                    out=hT_sb.ap()[:, c * TOK + 128 * t: c * TOK + 128 * (t + 1)],
                    in_=ptile.ap()[:, 0:128], func=AF.Copy),
                    reads=[pnn], writes=["hT_sb"])

    B.free("wout_sb", "attn_ld0", "attn_ld1", "xb2_0", "xb2_1", "hpre", "hsq", "htile")
    w1_sb = B.sbuf("w1_sb", [128, 8, DFF], BF16)
    w2_sb = B.sbuf("w2_sb", [128, 32, D], BF16)
    zf = [B.sbuf(f"zf{i}", [128, 256], BF16) for i in range(2)]
    spre = B.sbuf("spre", [128, D], F32)
    hsq2 = B.sbuf("hsq2", [128, D], F32)
    hld = [B.sbuf(f"hld{i}", [128, D], F32) for i in range(2)]
    brtile = B.sbuf("brtile", [128, D], F32)

    for c in range(8):
        B.op("gpsimd", lambda e, c=c: e.dma_start(
            out=w1_sb.ap()[:, c, :], in_=w1_in[128 * c:128 * (c + 1), :]),
            writes=["w1_sb"], stream="d_w1")
    for f in range(32):
        B.op("gpsimd", lambda e, f=f: e.dma_start(
            out=w2_sb.ap()[:, f, :], in_=w2_in[128 * f:128 * (f + 1), :]),
            writes=["w2_sb"], stream="d_w2")

    def ln_stats2(src, srcn):
        B.op("vector", lambda e: e.reduce_sum(sum1.ap(), src.ap(), axis=AX.X),
             reads=[srcn], writes=["sum1"])
        B.op("scalar", lambda e: e.activation(
            out=hsq2.ap(), in_=src.ap(), func=AF.Square, accum_out=sumsq.ap()),
            reads=[srcn], writes=["hsq2", "sumsq"])
        B.op("vector", lambda e: e.tensor_scalar(t1a.ap(), sum1.ap(), 1.0 / D, None, OP.mult),
             reads=["sum1"], writes=["t1a"])
        B.op("vector", lambda e: e.tensor_scalar(t1b.ap(), sumsq.ap(), 1.0 / D, None, OP.mult),
             reads=["sumsq"], writes=["t1b"])
        B.op("vector", lambda e: e.tensor_scalar(t1c.ap(), t1a.ap(), t1a.ap(), -1.0, OP.mult, OP.mult),
             reads=["t1a"], writes=["t1c"])
        B.op("vector", lambda e: e.tensor_add(t1d.ap(), t1b.ap(), t1c.ap()),
             reads=["t1b", "t1c"], writes=["t1d"])
        B.op("scalar", lambda e: e.activation(
            out=t1e.ap(), in_=t1d.ap(), func=AF.Sqrt, bias=epsc.ap()),
            reads=["t1d", "epsc"], writes=["t1e"])
        B.op("vector", lambda e: e.reciprocal(rstd2.ap(), t1e.ap()),
             reads=["t1e"], writes=["rstd2"])
        B.op("vector", lambda e: e.tensor_scalar(
            negmr2.ap(), t1a.ap(), rstd2.ap(), -1.0, OP.mult, OP.mult),
            reads=["t1a", "rstd2"], writes=["negmr2"])

    for Bq in range(8):
        def emit_w1(f):
            pw = PE2 if f % 2 == 0 else PF.bitcast(F32)
            pwn = "PE2" if f % 2 == 0 else "PF"
            for c in range(8):
                B.op("tensor", lambda e, c=c, f=f, pw=pw, Bq=Bq: e.matmul(
                    pw.ap()[:, 0:256], w1_sb.ap()[:, c, 128 * f:128 * (f + 1)],
                    hT_sb.ap()[:, c * TOK + 256 * Bq:c * TOK + 256 * (Bq + 1)],
                    start=(c == 0), stop=(c == 7)),
                    reads=["w1_sb", "hT_sb"] + ([pwn] if c == 0 else []),
                    writes=[pwn])

        def emit_relu(f):
            pw = PE2 if f % 2 == 0 else PF.bitcast(F32)
            pwn = "PE2" if f % 2 == 0 else "PF"
            B.op("scalar", lambda e, f=f, pw=pw: e.activation(
                out=zf[f % 2].ap(), in_=pw.ap()[:, 0:256], func=AF.Relu,
                bias=b1sb.ap()[:, f:f + 1]),
                reads=[pwn, "b1sb"], writes=[f"zf{f % 2}"])

        def emit_w2(f):
            for tt in range(2):
                for n in range(2):
                    acc = PA if tt == 0 else PB
                    pa = ("PA0", "PA1")[n] if tt == 0 else ("PB0", "PB1")[n]
                    B.op("tensor", lambda e, f=f, tt=tt, n=n, acc=acc: e.matmul(
                        acc.ap()[:, 512 * n:512 * (n + 1)],
                        zf[f % 2].ap()[:, 128 * tt:128 * (tt + 1)],
                        w2_sb.ap()[:, f, 512 * n:512 * (n + 1)],
                        start=(f == 0), stop=(f == 31)),
                        reads=[f"zf{f % 2}", "w2_sb"] + ([pa] if f == 0 else []),
                        writes=[pa])

        for f in range(33):
            if f < 32:
                emit_w1(f)
                emit_relu(f)
            if f >= 1:
                emit_w2(f - 1)
        for tt in range(2):
            t = 2 * Bq + tt
            hb, hbn = hld[tt % 2], f"hld{tt % 2}"
            B.op("sync", lambda e, t=t, hb=hb: e.dma_start(
                out=hb.ap(), in_=h_spill[128 * t:128 * (t + 1), :]),
                reads=["h_spill"], writes=[hbn], stream="d_hld")
            for n in range(2):
                acc = PA if tt == 0 else PB
                pa = ("PA0", "PA1")[n] if tt == 0 else ("PB0", "PB1")[n]
                B.op("vector", lambda e, tt=tt, n=n, hb=hb, acc=acc: e.tensor_add(
                    spre.ap()[:, 512 * n:512 * (n + 1)],
                    acc.ap()[:, 512 * n:512 * (n + 1)],
                    hb.ap()[:, 512 * n:512 * (n + 1)]),
                    reads=[pa, hbn], writes=["spre"])
            ln_stats2(spre, "spre")
            B.op("scalar", lambda e: e.activation(
                out=brtile.ap(), in_=spre.ap(), func=AF.Identity,
                bias=negmr2.ap(), scale=rstd2.ap()),
                reads=["spre", "rstd2", "negmr2"], writes=["brtile"])
            B.op("sync", lambda e, t=t: e.dma_start(
                out=br_spill[128 * t:128 * (t + 1), :], in_=brtile.ap()),
                reads=["brtile"], writes=["br_spill"], stream="d_br")
            if debug_upto >= 4:
                B.op("sync", lambda e, t=t: e.dma_start(
                    out=dbg["br"][128 * t:128 * (t + 1), :], in_=brtile.ap()),
                    reads=["brtile"], stream="d_dbg2")

    if upto_phase <= 4:
        return finish(B, nc, dict(locals()))

    B.free("w1_sb", "w2_sb", "zf0", "zf1", "spre", "hsq2", "hld0", "hld1",
           "brtile", "hT_sb")
    dispbuf = B.sbuf("dispbuf", [128, D], BF16)
    dispT = B.sbuf("dispT", [128, 8, SLOTC], BF16)
    we1_sb = B.sbuf("we1_sb", [128, 8, DFF], BF16)
    we2f = [B.sbuf(f"we2f{i}", [128, D], BF16) for i in range(2)]
    h1T = B.sbuf("h1T", [128, 32, 256], BF16)
    eoev = [B.sbuf(f"eoev{i}", [128, 512], BF16) for i in range(2)]

    for sl in range(SLOTC // 128):
        B.op("sync", lambda e, sl=sl: e.dma_start(
            out=dispbuf.ap(), in_=disp_dram[128 * sl:128 * (sl + 1), :]),
            reads=["disp_dram"], writes=["dispbuf"], stream="d_dld")
        if debug_upto >= 5 and sl == 0:
            B.op("gpsimd", lambda e: e.dma_start(
                out=dbg["disp0"], in_=dispbuf.ap()),
                reads=["dispbuf"], stream="d_dbg")
        for c in range(8):
            ptile = PC if c % 2 == 0 else PD
            pnn = "PC" if c % 2 == 0 else "PD"
            B.op("tensor", lambda e, c=c, ptile=ptile: e.transpose(
                ptile.bitcast(BF16).ap()[:, 0:128],
                dispbuf.ap()[:, 128 * c:128 * (c + 1)], identb.ap()),
                reads=["dispbuf", "identb", pnn], writes=[pnn])
            B.op("scalar", lambda e, c=c, sl=sl, ptile=ptile: e.activation(
                out=dispT.ap()[:, c, 128 * sl:128 * (sl + 1)],
                in_=ptile.bitcast(BF16).ap()[:, 0:128], func=AF.Copy),
                reads=[pnn], writes=["dispT"])

    NSG = CAP // 256
    for ei in range(EPC):
        for c in range(8):
            B.op("gpsimd", lambda e, ei=ei, c=c: e.dma_start(
                out=we1_sb.ap()[:, c, :],
                in_=we1_in[ei, 128 * c:128 * (c + 1), :]),
                writes=["we1_sb"], stream="d_we1")
        for sg in range(NSG):
            soff = ei * CAP + 256 * sg

            def emit_ew1(f):
                pw = PE2 if f % 2 == 0 else PF.bitcast(F32)
                pwn = "PE2" if f % 2 == 0 else "PF"
                for c in range(8):
                    B.op("tensor", lambda e, c=c, f=f, pw=pw, soff=soff: e.matmul(
                        pw.ap()[:, 0:256],
                        we1_sb.ap()[:, c, 128 * f:128 * (f + 1)],
                        dispT.ap()[:, c, soff:soff + 256],
                        start=(c == 0), stop=(c == 7)),
                        reads=["we1_sb", "dispT"] + ([pwn] if c == 0 else []),
                        writes=[pwn])

            def emit_gelu(f):
                pw = PE2 if f % 2 == 0 else PF.bitcast(F32)
                pwn = "PE2" if f % 2 == 0 else "PF"
                B.op("scalar", lambda e, f=f, pw=pw: e.activation(
                    out=h1T.ap()[:, f, :], in_=pw.ap()[:, 0:256],
                    func=AF.Gelu_apprx_tanh),
                    reads=[pwn], writes=["h1T"])

            def emit_ew2(f):
                wf, wfn = we2f[f % 2], f"we2f{f % 2}"
                for tt in range(2):
                    for n in range(2):
                        acc = PA if tt == 0 else PB
                        pa = ("PA0", "PA1")[n] if tt == 0 else ("PB0", "PB1")[n]
                        B.op("tensor", lambda e, f=f, tt=tt, n=n, wf=wf, acc=acc: e.matmul(
                            acc.ap()[:, 512 * n:512 * (n + 1)],
                            h1T.ap()[:, f, 128 * tt:128 * (tt + 1)],
                            wf.ap()[:, 512 * n:512 * (n + 1)],
                            start=(f == 0), stop=(f == 31)),
                            reads=["h1T", wfn] + ([pa] if f == 0 else []),
                            writes=[pa])

            for f in range(32):
                B.op("gpsimd", lambda e, ei=ei, f=f, wf=we2f[f % 2]: e.dma_start(
                    out=wf.ap(), in_=we2_in[ei, 128 * f:128 * (f + 1), :]),
                    writes=[f"we2f{f % 2}"], stream="d_we2")
                emit_ew1(f)
                emit_gelu(f)
                if f >= 1:
                    emit_ew2(f - 1)
            emit_ew2(31)
            for tt in range(2):
                for n in range(2):
                    acc = PA if tt == 0 else PB
                    pa = ("PA0", "PA1")[n] if tt == 0 else ("PB0", "PB1")[n]
                    ev, evn = eoev[n % 2], f"eoev{n % 2}"
                    B.op("scalar", lambda e, tt=tt, n=n, ev=ev, acc=acc: e.activation(
                        out=ev.ap(), in_=acc.ap()[:, 512 * n:512 * (n + 1)],
                        func=AF.Copy), reads=[pa], writes=[evn])
                    B.op("sync", lambda e, soff=soff, tt=tt, n=n, ev=ev: e.dma_start(
                        out=agin_eo[soff + 128 * tt:soff + 128 * (tt + 1),
                                    512 * n:512 * (n + 1)],
                        in_=ev.ap()), reads=[evn], writes=["agin_eo"],
                        stream="d_eo")
                    if debug_upto >= 5 and sg == 0 and tt == 0:
                        B.op("gpsimd", lambda e, ei=ei, n=n, ev=ev: e.dma_start(
                            out=dbg["eo"][:, 1024 * ei + 512 * n:1024 * ei + 512 * (n + 1)],
                            in_=ev.ap()), reads=[evn], stream="d_dbg")
    B.op("gpsimd", lambda e: e.collective_compute(
        "AllGather", OP.bypass, replica_groups=[list(range(NCORES))],
        ins=[agin_eo.opt()], outs=[agout_eo.opt()]),
        reads=["agin_eo"], writes=["agout_eo"], sem=B.ccsem("scc5"))

    if upto_phase <= 5:
        return finish(B, nc, dict(locals()))

    B.free("gxall", "xg0", "xg1", "dispbuf", "dispT", "we1_sb", "we2f0", "we2f1",
           "h1T", "eoev0", "eoev1")
    gath = B.sbuf("gath", [128, D], BF16)
    xb3 = [B.sbuf(f"xb3_{i}", [128, D], F32) for i in range(2)]
    brb = [B.sbuf(f"brb{i}", [128, D], F32) for i in range(2)]
    sumt = B.sbuf("sumt", [128, D], F32)
    yb = B.sbuf("yb", [128, D], F32)

    B.op("vector", lambda e: e.memset(gath.ap(), 0.0), writes=["gath"])
    B.op("vector", lambda e: e.tensor_copy(gidxu.ap(), gidx_sb.ap()),
         reads=["gidx_sb"], writes=["gidxu"])
    for t in range(NT):
        xbb, xbn = xb3[t % 2], f"xb3_{t % 2}"
        bbb, bbn = brb[t % 2], f"brb{t % 2}"
        B.op("sync", lambda e, t=t, xbb=xbb: e.dma_start(
            out=xbb.ap(), in_=x_in[128 * t:128 * (t + 1), :]),
            writes=[xbn], stream="d_x3")
        B.op("sync", lambda e, t=t, bbb=bbb: e.dma_start(
            out=bbb.ap(), in_=br_spill[128 * t:128 * (t + 1), :]),
            reads=["br_spill"], writes=[bbn], stream="d_brl")
        B.op("gpsimd", lambda e, t=t: e.indirect_dma_start(
            out=gath.ap(), out_offset=None,
            in_=agout_eo,
            in_offset=IndirectOffsetOnAxis(ap=gidxu.ap()[:, t:t + 1], axis=0),
            bounds_check=REGS["bnd_gath"], oob_is_err=False),
            reads=["agout_eo", "gidxu", "gath"], writes=["gath"], stream="d_gath")
        B.op("vector", lambda e, xbb=xbb, bbb=bbb: e.tensor_add(
            sumt.ap(), xbb.ap(), bbb.ap()),
            reads=[xbn, bbn], writes=["sumt"])
        B.op("vector", lambda e, t=t: e.scalar_tensor_tensor(
            yb.ap(), gath.ap(), w_sb.ap()[:, t:t + 1], sumt.ap(), OP.mult, OP.add),
            reads=["gath", "w_sb", "sumt"], writes=["yb"])
        B.op("sync", lambda e, t=t: e.dma_start(
            out=y_out[128 * t:128 * (t + 1), :], in_=yb.ap()),
            reads=["yb"], writes=["y_out"], stream="d_y")

    return finish(B, nc, dict(locals()))


def finish(B, nc, state):
    for semname in list(B.sem):
        if B.cnt[semname] > 0:
            B.raw_wait("sync", semname, B.cnt[semname])
    nc_ = B.close()
    state["B"] = B
    return nc_, state


def make_inputs(full_inputs, core):
    x = np.asarray(full_inputs["L0_attn_x"]).reshape(S, D)
    bqkv = np.asarray(full_inputs["L0_attn_bqkv"])
    b1 = np.asarray(full_inputs["L0_attn_b1"])
    return {
        "x": np.ascontiguousarray(x[TOK * core:TOK * (core + 1)]),
        "wqkv": np.asarray(full_inputs["L0_attn_Wqkv"]),
        "bq64": np.ascontiguousarray(bqkv.reshape(48, 64).T),
        "wout": np.asarray(full_inputs["L0_attn_Wout"]),
        "w1": np.asarray(full_inputs["L0_attn_W1"]),
        "b1i": np.ascontiguousarray(b1.reshape(32, 128).T),
        "w2": np.asarray(full_inputs["L0_attn_W2"]),
        "gwT": np.ascontiguousarray(np.asarray(full_inputs["L0_ffn_gate_w"]).T),
        "we1": np.ascontiguousarray(np.asarray(full_inputs["L0_ffn_w1"])[EPC * core:EPC * (core + 1)]),
        "we2": np.ascontiguousarray(np.asarray(full_inputs["L0_ffn_w2"])[EPC * core:EPC * (core + 1)]),
        "identb": np.eye(128, dtype=ml_dtypes.bfloat16),
        "identf": np.eye(128, dtype=np.float32),
        "ltinc": np.triu(np.ones((128, 128), np.float32)),
        "oneshift": np.concatenate(
            [np.zeros((128, 127), np.float32), np.ones((128, 1), np.float32),
             np.zeros((128, 127), np.float32)], axis=1),
        "cpfx": np.concatenate(
            [np.tile((np.arange(128)[:, None] < t).astype(np.float32), (1, 128))
             for t in range(NT)], axis=1),
        "iota16": np.tile(np.arange(16, dtype=np.float32), (128, 1)),
        "iotarev": np.tile(16.0 - np.arange(16, dtype=np.float32), (128, 1)),
        "prevmask": (np.arange(8)[:, None] < core).astype(np.float32),
        "slotb": np.full((128, 1), SLOTC * core, np.float32),
    }


_NC_CACHE = {}


def kernel(**inputs):
    from concourse.bass_utils import run_bass_kernel_spmd
    if "nc" not in _NC_CACHE:
        nc, _state = build_nc(debug_upto=0, upto_phase=99)
        _NC_CACHE["nc"] = nc
    nc = _NC_CACHE["nc"]
    in_maps = [make_inputs(inputs, c) for c in range(NCORES)]
    res = run_bass_kernel_spmd(nc, in_maps, core_ids=list(range(NCORES)))
    y = np.concatenate([res.results[c]["y"] for c in range(NCORES)], axis=0)
    return y.reshape(1, S, D).astype(np.float32)


# version 04 (final)
# speedup vs baseline: 1.5519x; 1.0020x over previous
import numpy as np
import ml_dtypes
from contextlib import ExitStack

import concourse.bass as bass
import concourse.mybir as mybir
from concourse.bass_core_rust import IndirectOffsetOnAxis

F32 = mybir.dt.float32
BF16 = mybir.dt.bfloat16
U32 = mybir.dt.uint32
AF = mybir.ActivationFunctionType
OP = mybir.AluOpType
AX = mybir.AxisListType

S = 16384
D = 1024
NCORES = 8
TOK = S // NCORES
NT = TOK // 128
NH = 16
DH = 64
STRIDES = [1, 1, 2, 2, 2, 2, 4, 4, 4, 4, 8, 8, 8, 8, 16, 16]
NE = 16
CAP = 1280
EPC = 2
SLOTC = EPC * CAP
NSLOT = NE * CAP
DFF = 4096
LN_EPS = 1e-5
EXP_SCALE = 0.125
EXP_SHIFT = -16.0
BIGIDX = 99999.0

KV_OFF = []
_o = 0
for _st in STRIDES:
    KV_OFF.append(_o)
    _o += TOK // _st
KV_TOT = _o
VGROUPS = []
_h = 0
while _h < NH:
    _st = STRIDES[_h]
    _n = 1
    while _h + _n < NH and STRIDES[_h + _n] == _st:
        _n += 1
    VGROUPS.append((_h, _n, _st))
    _h += _n


class Build:
    ENGS = ("sync", "scalar", "vector", "tensor", "gpsimd")
    ESEM = {"scalar": "ss", "vector": "sv", "tensor": "st"}

    def __init__(self, nc, es):
        self.nc = nc
        self.es = es
        self.q = {e: [] for e in self.ENGS}
        self.sem = {}
        self.cnt = {}
        self.unord = {}
        self.inc = {}
        for name in ("ss", "sv", "st"):
            self._mksem(name, 1, False)
        self.lastw = {}
        self.readers = {}
        self.wm = {e: {} for e in self.ENGS}
        self._guards = {}
        self._free_deps = []

    def _mksem(self, name, inc, unord):
        self.sem[name] = self.es.enter_context(self.nc.semaphore(name))
        self.cnt[name] = 0
        self.inc[name] = inc
        self.unord[name] = unord

    def stream(self, name):
        if name not in self.sem:
            self._mksem(name, 16, True)
        return name

    def ccsem(self, name):
        if name not in self.sem:
            self._mksem(name, 1, False)
        return name

    def _wait(self, eng, sem, val):
        if val <= 0 or self.wm[eng].get(sem, 0) >= val:
            return
        self.wm[eng][sem] = val
        self.q[eng].append(lambda e, s=self.sem[sem], v=val: e.wait_ge(s, v))

    def _wait_dep(self, eng, dep):
        sem, val = dep
        if self.unord[sem]:
            val = self.cnt[sem]
        self._wait(eng, sem, val)

    def op(self, eng, fn, reads=(), writes=(), stream=None, sem=None):
        for r in reads:
            lw = self.lastw.get(r)
            if lw is not None:
                self._wait_dep(eng, lw)
        for w in writes:
            lw = self.lastw.get(w)
            if lw is not None:
                self._wait_dep(eng, lw)
            for dep in self.readers.get(w, ()):
                self._wait_dep(eng, dep)
        if stream is not None:
            semname = self.stream(stream)
            self._wait(eng, semname, self.cnt[semname])
        elif sem is not None:
            semname = sem
        else:
            semname = self.ESEM[eng]
        amt = self.inc[semname]
        self.cnt[semname] += amt
        val = self.cnt[semname]
        self.q[eng].append(
            lambda e, f=fn, s=self.sem[semname], a=amt: f(e).then_inc(s, a))
        for r in reads:
            self.readers.setdefault(r, []).append((semname, val))
        for w in writes:
            self.lastw[w] = (semname, val)
            self.readers[w] = []
        return val

    def raw_wait(self, eng, semname, val):
        self._wait(eng, semname, val)

    def raw(self, eng, fn):
        self.q[eng].append(fn)

    def sbuf(self, name, shape, dtype):
        g = self.nc.sbuf_tensor(name, shape, dtype)
        h = g.__enter__()
        self._guards[name] = g
        self._alloc_order = getattr(self, "_alloc_order", [])
        self._alloc_order.append(name)
        pool = list(getattr(self, "_free_deps", []))
        if pool:
            self.readers.setdefault(name, []).extend(pool)
        return h

    def free(self, *names):
        order = {n: i for i, n in enumerate(getattr(self, "_alloc_order", []))}
        names = sorted(names, key=lambda n: -order.get(n, 0))
        for name in names:
            lw = self.lastw.get(name)
            if lw is not None:
                self._free_deps.append(lw)
            self._free_deps.extend(self.readers.get(name, ()))
            g = self._guards.pop(name, None)
            if g is not None:
                g.__exit__(None, None, None)

    def close(self):
        nc = self.nc
        with nc.Block() as block:
            @block.sync
            def _(e):
                for f in self.q["sync"]:
                    f(e)

            @block.scalar
            def _(e):
                for f in self.q["scalar"]:
                    f(e)

            @block.vector
            def _(e):
                for f in self.q["vector"]:
                    f(e)

            @block.tensor
            def _(e):
                for f in self.q["tensor"]:
                    f(e)

            @block.gpsimd
            def _(e):
                for f in self.q["gpsimd"]:
                    f(e)
        return nc


def build_nc(debug_upto=0, upto_phase=99, nt_limit=NT, p1sec=6):
    nc = bass.Bass("TRN2", target_bir_lowering=False)
    es = ExitStack()
    B = Build(nc, es)

    def din(name, shape, dtype=F32):
        return nc.dram_tensor(name, shape, dtype, kind="ExternalInput").ap()

    def dint(name, shape, dtype):
        return nc.dram_tensor(name, shape, dtype).ap()

    def dout(name, shape, dtype=F32):
        return nc.dram_tensor(name, shape, dtype, kind="ExternalOutput").ap()

    x_in = din("x", [TOK, D])
    wqkv_in = din("wqkv", [D, 3 * D])
    bq64_in = din("bq64", [64, 48])
    wout_in = din("wout", [D, D])
    w1_in = din("w1", [D, DFF])
    b1_in = din("b1i", [128, 32])
    w2_in = din("w2", [DFF, D])
    gwT_in = din("gwT", [D, NE])
    we1_in = din("we1", [EPC, D, DFF])
    we2_in = din("we2", [EPC, DFF, D])
    identb_in = din("identb", [128, 128], BF16)
    identf_in = din("identf", [128, 128], F32)
    ltinc_in = din("ltinc", [128, 128])
    oneshift_in = din("oneshift", [128, 255])
    cpfx_in = din("cpfx", [128, NT * 128])
    iota16_in = din("iota16", [128, 16])
    iotarev_in = din("iotarev", [128, 16])
    prevmask_in = din("prevmask", [8, 1])
    slotb_in = din("slotb", [128, 1])

    y_out = dout("y", [TOK, D])

    agin_cnt = dint("agin_cnt", [128, NE], F32)
    agout_cnt = dint("agout_cnt", [8 * 128, NE], F32)
    agin_xn = dint("agin_xn", [TOK, D], BF16)
    agout_xn = dint("agout_xn", [S, D], BF16)
    agin_kt = dint("agin_kt", [64, KV_TOT], BF16)
    agout_kt = dint("agout_kt", [8 * 64, KV_TOT], BF16)
    agin_v = dint("agin_v", [KV_TOT, 64], BF16)
    agout_v = dint("agout_v", [8 * KV_TOT, 64], BF16)
    agin_gx = dint("agin_gx", [NT, 128], F32)
    agout_gx = dint("agout_gx", [8 * NT, 128], F32)
    agin_eo = dint("agin_eo", [SLOTC, D], BF16)
    agout_eo = dint("agout_eo", [NSLOT, D], BF16)

    qt_spill = dint("qt_spill", [64, NH * TOK], BF16)
    attn_spill = dint("attn_spill", [64, NH * TOK], BF16)
    h_spill = dint("h_spill", [TOK, D], F32)
    br_spill = dint("br_spill", [TOK, D], F32)
    disp_dram = dint("disp_dram", [SLOTC, D], BF16)

    dbg = {}
    if debug_upto >= 1:
        dbg["xn"] = dout("dbg_xn", [TOK, D])
        dbg["logit"] = dout("dbg_logit", [128, NT * 16])
        dbg["mask"] = dout("dbg_mask", [128, NT * 16])
        dbg["cnt"] = dout("dbg_cnt", [128, 16])
    if debug_upto >= 2:
        dbg["gidx"] = dout("dbg_gidx", [128, NT])
        dbg["w"] = dout("dbg_w", [128, NT])
        dbg["kt"] = dout("dbg_kt", [64, KV_TOT])
        dbg["v"] = dout("dbg_v", [KV_TOT, 64])
        dbg["qt"] = dout("dbg_qt", [64, NH * 128])
        dbg["base"] = dout("dbg_base", [1, NE])
        dbg["cnt8"] = dout("dbg_cnt8", [8, NE])
        dbg["rank0"] = dout("dbg_rank0", [128, 16])
        dbg["tot"] = dout("dbg_tot", [1, NE])
    if debug_upto >= 3:
        dbg["attn"] = dout("dbg_attn", [64, NH * 128])
    if debug_upto >= 4:
        dbg["h"] = dout("dbg_h", [TOK, D])
        dbg["br"] = dout("dbg_br", [TOK, D])
    if debug_upto >= 5:
        dbg["idx"] = dout("dbg_idx", [128, 128])
        dbg["disp0"] = dout("dbg_disp0", [128, D])
        dbg["eo"] = dout("dbg_eo", [128, 2 * D])

    sbA = nc.alloc_sbuf_tensor

    identb = sbA("identb_s", [128, 128], BF16)
    identf = sbA("identf_s", [128, 128], F32)
    ltinc = sbA("ltinc_s", [128, 128], F32)
    oneshift = sbA("oneshift_s", [128, 255], F32)
    cpfx = sbA("cpfx_s", [128, NT * 128], F32)
    iota16 = sbA("iota16_s", [128, 16], F32)
    iotarev = sbA("iotarev_s", [128, 16], F32)
    prevmask = sbA("prevmask_s", [8, 1], F32)
    slotb = sbA("slotb_s", [128, 1], F32)
    onescol = sbA("onescol", [128, 1], F32)
    onesrow = sbA("onesrow", [1, 128], F32)
    ones64 = sbA("ones64", [128, 64], F32)
    c9 = sbA("c9", [128, 1], F32)
    epsc = sbA("epsc", [128, 1], F32)
    bq64 = sbA("bq64_s", [64, 48], F32)
    b1sb = sbA("b1sb_s", [128, 32], F32)
    gwT = sbA("gwT_s", [128, 8, 16], F32)
    rstd1 = sbA("rstd1", [128, NT], F32)
    negmr1 = sbA("negmr1", [128, NT], F32)
    logit_sb = sbA("logit_sb", [128, NT * 16], F32)
    mask_sb = sbA("mask_sb", [128, NT * 16], F32)
    e_sb = sbA("e_sb", [128, NT], F32)
    wpre_sb = sbA("wpre_sb", [128, NT], F32)
    counts_sb = sbA("counts_sb", [128, 16], F32)
    total_sb = sbA("total_sb", [1, 16], F32)
    cnt8_sb = sbA("cnt8_sb", [8, NE], F32)
    base_sb = sbA("base_sb", [1, NE], F32)
    gidx_sb = sbA("gidx_sb", [128, NT], F32)
    w_sb = sbA("w_sb", [128, NT], F32)
    gidxu = sbA("gidxu", [128, NT], U32)
    gxT_sb = sbA("gxT_sb", [NT, 128], F32)
    idxall_f = sbA("idxall_f", [128, 128], F32)
    idxall_u = sbA("idxall_u", [128, 128], U32)
    zero_bf = sbA("zero_bf", [128, D], BF16)
    t16a = sbA("t16a", [128, 16], F32)
    t16b = sbA("t16b", [128, 16], F32)
    t1a = sbA("t1a", [128, 1], F32)
    t1b = sbA("t1b", [128, 1], F32)
    t1c = sbA("t1c", [128, 1], F32)
    t1d = sbA("t1d", [128, 1], F32)
    t1e = sbA("t1e", [128, 1], F32)
    sum1 = sbA("sum1", [128, 1], F32)
    sumsq = sbA("sumsq", [128, 1], F32)
    rstd2 = sbA("rstd2", [128, 1], F32)
    negmr2 = sbA("negmr2", [128, 1], F32)

    PA = nc.alloc_psum_tensor("PA", [128, 1024], F32)
    PB = nc.alloc_psum_tensor("PB", [128, 1024], F32)
    PC = nc.alloc_psum_tensor("PC", [128, 512], F32)
    PD = nc.alloc_psum_tensor("PD", [128, 512], F32)
    PE2 = nc.alloc_psum_tensor("PE2", [128, 512], F32)
    PF = nc.alloc_psum_tensor("PF", [128, 1024], BF16)

    for (dst, src, nm) in [
        (identb, identb_in, "identb"), (identf, identf_in, "identf"),
        (ltinc, ltinc_in, "ltinc"), (oneshift, oneshift_in, "oneshift"),
        (cpfx, cpfx_in, "cpfx"), (iota16, iota16_in, "iota16"),
        (iotarev, iotarev_in, "iotarev"), (prevmask, prevmask_in, "prevmask"),
        (slotb, slotb_in, "slotb"), (bq64, bq64_in, "bq64"), (b1sb, b1_in, "b1sb"),
    ]:
        B.op("sync", lambda e, d=dst, s=src: e.dma_start(out=d.ap(), in_=s),
             writes=[nm], stream="d_const")
    B.op("sync", lambda e: e.dma_start(
        out=gwT.ap(), in_=gwT_in.rearrange("(c p) e -> p c e", p=128)),
        writes=["gwT"], stream="d_const")
    B.op("vector", lambda e: e.memset(onescol.ap(), 1.0), writes=["onescol"])
    B.op("vector", lambda e: e.memset(onesrow.ap(), 1.0), writes=["onesrow"])
    B.op("vector", lambda e: e.memset(ones64.ap(), 1.0), writes=["ones64"])
    B.op("vector", lambda e: e.memset(c9.ap(), BIGIDX), writes=["c9"])
    B.op("vector", lambda e: e.memset(epsc.ap(), LN_EPS), writes=["epsc"])
    B.op("vector", lambda e: e.memset(zero_bf.ap(), 0.0), writes=["zero_bf"])

    if upto_phase <= 0:
        return finish(B, nc, dict(locals()))

    gxall = B.sbuf("gxall", [128, 128], F32)
    xg = [B.sbuf(f"xg{i}", [128, D], BF16) for i in range(2)]
    xbuf = [B.sbuf(f"xbuf{i}", [128, D], F32) for i in range(2)]
    xsq = B.sbuf("xsq", [128, D], F32)
    xn32 = B.sbuf("xn32", [128, D], F32)
    xnbf = B.sbuf("xnbf", [128, D], BF16)
    xnT32 = B.sbuf("xnT32", [128, D], F32)
    xnT_sb = B.sbuf("xnT_sb", [128, 8 * TOK], BF16)

    for t in range(nt_limit):
        xb, xbn = xbuf[t % 2], f"xbuf{t % 2}"
        B.op("sync", lambda e, t=t, xb=xb: e.dma_start(
            out=xb.ap(), in_=x_in[128 * t:128 * (t + 1), :]), writes=[xbn], stream="d_x")
        B.op("vector", lambda e, xb=xb: e.reduce_sum(sum1.ap(), xb.ap(), axis=AX.X),
             reads=[xbn], writes=["sum1"])
        B.op("scalar", lambda e, xb=xb: e.activation(
            out=xsq.ap(), in_=xb.ap(), func=AF.Square, accum_out=sumsq.ap()),
            reads=[xbn], writes=["xsq", "sumsq"])
        B.op("vector", lambda e: e.tensor_scalar(t1a.ap(), sum1.ap(), 1.0 / D, None, OP.mult),
             reads=["sum1"], writes=["t1a"])
        B.op("vector", lambda e: e.tensor_scalar(t1b.ap(), sumsq.ap(), 1.0 / D, None, OP.mult),
             reads=["sumsq"], writes=["t1b"])
        B.op("vector", lambda e: e.tensor_scalar(t1c.ap(), t1a.ap(), t1a.ap(), -1.0, OP.mult, OP.mult),
             reads=["t1a"], writes=["t1c"])
        B.op("vector", lambda e: e.tensor_add(t1d.ap(), t1b.ap(), t1c.ap()),
             reads=["t1b", "t1c"], writes=["t1d"])
        B.op("scalar", lambda e: e.activation(
            out=t1e.ap(), in_=t1d.ap(), func=AF.Sqrt, bias=epsc.ap()),
            reads=["t1d", "epsc"], writes=["t1e"])
        B.op("vector", lambda e, t=t: e.reciprocal(rstd1.ap()[:, t:t + 1], t1e.ap()),
             reads=["t1e"], writes=["rstd1"])
        B.op("vector", lambda e, t=t: e.tensor_scalar(
            negmr1.ap()[:, t:t + 1], t1a.ap(), rstd1.ap()[:, t:t + 1], -1.0,
            OP.mult, OP.mult), reads=["t1a", "rstd1"], writes=["negmr1"])
        B.op("vector", lambda e, xb=xb, t=t: e.tensor_scalar(
            xn32.ap(), xb.ap(), rstd1.ap()[:, t:t + 1], negmr1.ap()[:, t:t + 1],
            OP.mult, OP.add), reads=[xbn, "rstd1", "negmr1"], writes=["xn32"])
        B.op("scalar", lambda e, xb=xb, t=t: e.activation(
            out=xnbf.ap(), in_=xb.ap(), func=AF.Identity,
            bias=negmr1.ap()[:, t:t + 1], scale=rstd1.ap()[:, t:t + 1]),
            reads=[xbn, "rstd1", "negmr1"], writes=["xnbf"])
        B.op("sync", lambda e, t=t: e.dma_start(
            out=agin_xn[128 * t:128 * (t + 1), :], in_=xnbf.ap()),
            reads=["xnbf"], writes=["agin_xn"], stream="d_xn")
        if debug_upto >= 1:
            B.op("gpsimd", lambda e, t=t: e.dma_start(
                out=dbg["xn"][128 * t:128 * (t + 1), :], in_=xnbf.ap()),
                reads=["xnbf"], stream="d_dbg")
        if p1sec < 2:
            continue
        for c in range(8):
            slot = c % 2
            pn = "PC" if slot == 0 else "PD"
            ptile = PC if slot == 0 else PD
            B.op("tensor", lambda e, c=c, ptile=ptile: e.transpose(
                ptile.ap()[:, 0:128],
                xn32.ap()[:, 128 * c:128 * (c + 1)], identf.ap()),
                reads=["xn32", "identf"], writes=[pn])
            B.op("vector", lambda e, c=c, ptile=ptile: e.tensor_copy(
                xnT32.ap()[:, 128 * c:128 * (c + 1)],
                ptile.ap()[:, 0:128]),
                reads=[pn], writes=["xnT32"])
            B.op("scalar", lambda e, c=c, t=t: e.activation(
                out=xnT_sb.ap()[:, c * TOK + 128 * t: c * TOK + 128 * (t + 1)],
                in_=xnT32.ap()[:, 128 * c:128 * (c + 1)], func=AF.Copy),
                reads=["xnT32"], writes=["xnT_sb"])
        if (p1sec if p1sec < 20 else 2) < 3:
            continue
        for c in range(8):
            B.op("tensor", lambda e, c=c: e.matmul(
                PE2.ap()[:, 0:16], xnT32.ap()[:, 128 * c:128 * (c + 1)],
                gwT.ap()[:, c, :], start=(c == 0), stop=(c == 7)),
                reads=["xnT32", "gwT"] + (["PE2"] if c == 0 else []),
                writes=["PE2"])
        B.op("vector", lambda e, t=t: e.tensor_copy(
            logit_sb.ap()[:, 16 * t:16 * (t + 1)], PE2.ap()[:, 0:16]),
            reads=["PE2"], writes=["logit_sb"])
        if (p1sec if p1sec < 20 else 2) < 4:
            continue
        B.op("vector", lambda e, t=t: e.reduce_max(
            t1a.ap(), logit_sb.ap()[:, 16 * t:16 * (t + 1)], axis=AX.X),
            reads=["logit_sb"], writes=["t1a"])
        B.op("vector", lambda e, t=t: e.tensor_reduce(
            t1b.ap(), logit_sb.ap()[:, 16 * t:16 * (t + 1)], axis=AX.X,
            op=OP.max, negate=True), reads=["logit_sb"], writes=["t1b"])
        B.op("scalar", lambda e, t=t: e.activation(
            out=t16a.ap(), in_=logit_sb.ap()[:, 16 * t:16 * (t + 1)], func=AF.Exp,
            bias=t1b.ap(), accum_out=t1c.ap()),
            reads=["logit_sb", "t1b"], writes=["t16a", "t1c"])
        B.op("vector", lambda e, t=t: e.reciprocal(wpre_sb.ap()[:, t:t + 1], t1c.ap()),
             reads=["t1c"], writes=["wpre_sb"])
        B.op("vector", lambda e, t=t: e.tensor_scalar(
            t16b.ap(), logit_sb.ap()[:, 16 * t:16 * (t + 1)], t1a.ap(), None,
            OP.is_equal), reads=["logit_sb", "t1a"], writes=["t16b"])
        B.op("vector", lambda e: e.tensor_mul(t16b.ap(), t16b.ap(), iotarev.ap()),
             reads=["t16b", "iotarev"], writes=["t16b"])
        B.op("vector", lambda e: e.reduce_max(t1d.ap(), t16b.ap(), axis=AX.X),
             reads=["t16b"], writes=["t1d"])
        B.op("vector", lambda e, t=t: e.tensor_scalar(
            e_sb.ap()[:, t:t + 1], t1d.ap(), -1.0, 16.0, OP.mult, OP.add),
            reads=["t1d"], writes=["e_sb"])
        B.op("vector", lambda e, t=t: e.tensor_scalar(
            mask_sb.ap()[:, 16 * t:16 * (t + 1)], iota16.ap(), e_sb.ap()[:, t:t + 1],
            None, OP.is_equal), reads=["iota16", "e_sb"], writes=["mask_sb"])
        B.op("tensor", lambda e, t=t: e.matmul(
            PA.ap()[:, 0:16], oneshift.ap()[:, 127 - t:255 - t],
            mask_sb.ap()[:, 16 * t:16 * (t + 1)], start=(t == 0), stop=(t == nt_limit - 1)),
            reads=["mask_sb", "oneshift"] + (["PA0"] if t == 0 else []),
            writes=["PA0"])
        B.op("tensor", lambda e, t=t: e.matmul(
            PA.ap()[0:1, 512:528], onescol.ap(),
            mask_sb.ap()[:, 16 * t:16 * (t + 1)], start=(t == 0), stop=(t == nt_limit - 1)),
            reads=["mask_sb", "onescol"] + (["PA1"] if t == 0 else []),
            writes=["PA1"])

    if p1sec >= 5 and nt_limit == NT:
        B.op("vector", lambda e: e.tensor_copy(counts_sb.ap(), PA.ap()[:, 0:16]),
             reads=["PA0"], writes=["counts_sb"])
        B.op("vector", lambda e: e.tensor_copy(total_sb.ap(), PA.ap()[0:1, 512:528]),
             reads=["PA1"], writes=["total_sb"])
        B.op("vector", lambda e: e.memset(t16a.ap(), 7777.0), reads=[], writes=["t16a"])
        B.op("sync", lambda e: e.dma_start(out=agin_cnt[1:128, :], in_=t16a.ap()[1:128, :]),
             reads=["t16a"], writes=["agin_cnt"], stream="d_cnt")
        B.op("sync", lambda e: e.dma_start(out=agin_cnt[0:1, :], in_=total_sb.ap()),
             reads=["total_sb"], writes=["agin_cnt"], stream="d_cnt")
    if debug_upto >= 1 and p1sec >= 5 and nt_limit == NT:
        B.op("sync", lambda e: e.dma_start(out=dbg["cnt"], in_=counts_sb.ap()),
             reads=["counts_sb"], stream="d_dbg2")
        B.op("sync", lambda e: e.dma_start(out=dbg["logit"], in_=logit_sb.ap()),
             reads=["logit_sb"], stream="d_dbg2")
        B.op("sync", lambda e: e.dma_start(out=dbg["mask"], in_=mask_sb.ap()),
             reads=["mask_sb"], stream="d_dbg2")
    if p1sec >= 6 and nt_limit == NT:
        B.op("gpsimd", lambda e: e.collective_compute(
            "AllGather", OP.bypass, replica_groups=[list(range(NCORES))],
            ins=[agin_cnt.opt()], outs=[agout_cnt.opt()]),
            reads=["agin_cnt"], writes=["agout_cnt"], sem=B.ccsem("scc0"))
        B.op("gpsimd", lambda e: e.collective_compute(
            "AllGather", OP.bypass, replica_groups=[list(range(NCORES))],
            ins=[agin_xn.opt()], outs=[agout_xn.opt()]),
            reads=["agin_xn"], writes=["agout_xn"], sem=B.ccsem("scc4"))

    if upto_phase <= 1:
        return finish(B, nc, dict(locals()))

    wqsg = [B.sbuf(f"wqsg{i}", [128, 8, 512], BF16) for i in range(2)]
    qth = [B.sbuf(f"qth{i}", [64, TOK], BF16) for i in range(2)]
    wkh = [B.sbuf(f"wkh{i}", [128, 8, 64], BF16) for i in range(2)]
    kevac = [B.sbuf(f"kevac{i}", [64, 512], BF16) for i in range(2)]
    wvg = [B.sbuf(f"wvg{i}", [128, 8, 256], BF16) for i in range(2)]
    vevac = [B.sbuf(f"vevac{i}", [128, 256], BF16) for i in range(2)]

    for h in range(NH):
        stv = STRIDES[h]
        L = TOK // stv
        wk, wkn = wkh[h % 2], f"wkh{h % 2}"
        for c in range(8):
            B.op("gpsimd", lambda e, h=h, c=c, wk=wk: e.dma_start(
                out=wk.ap()[:, c, :],
                in_=wqkv_in[128 * c:128 * (c + 1), D + 64 * h:D + 64 * (h + 1)]),
                writes=[wkn], stream="d_wk")
        nt_ = (L + 511) // 512
        for n in range(nt_):
            cnt_ = min(512, L - 512 * n)
            kb, kbn = kevac[n % 2], f"kevac{n % 2}"
            pn = "PE2"
            for c in range(8):
                B.op("tensor", lambda e, c=c, n=n, cnt_=cnt_, stv=stv, wk=wk: e.matmul(
                    PE2.ap()[0:64, 0:cnt_], wk.ap()[:, c, :],
                    bass.AP(xnT_sb, c * TOK + 512 * n * stv, [[8 * TOK, 128], [stv, cnt_]]),
                    start=(c == 0), stop=(c == 7)),
                    reads=[wkn, "xnT_sb"] + ([pn] if c == 0 else []),
                    writes=[pn])
            B.op("scalar", lambda e, cnt_=cnt_, kb=kb: e.activation(
                out=kb.ap()[:, 0:cnt_], in_=PE2.ap()[0:64, 0:cnt_], func=AF.Copy),
                reads=[pn], writes=[kbn])
            B.op("sync", lambda e, h=h, n=n, cnt_=cnt_, kb=kb: e.dma_start(
                out=agin_kt[:, KV_OFF[h] + 512 * n:KV_OFF[h] + 512 * n + cnt_],
                in_=kb.ap()[:, 0:cnt_]),
                reads=[kbn], writes=["agin_kt"], stream="d_kt")
            if debug_upto >= 2:
                B.op("gpsimd", lambda e, h=h, n=n, cnt_=cnt_, kb=kb: e.dma_start(
                    out=dbg["kt"][:, KV_OFF[h] + 512 * n:KV_OFF[h] + 512 * n + cnt_],
                    in_=kb.ap()[:, 0:cnt_]), reads=[kbn], stream="d_dbg")

    for gi, (h0, gh, stv) in enumerate(VGROUPS):
        L = TOK // stv
        wv, wvn = wvg[gi % 2], f"wvg{gi % 2}"
        gn = 64 * gh
        for c in range(8):
            B.op("gpsimd", lambda e, h0=h0, gn=gn, c=c, wv=wv: e.dma_start(
                out=wv.ap()[:, c, 0:gn],
                in_=wqkv_in[128 * c:128 * (c + 1), 2 * D + 64 * h0:2 * D + 64 * h0 + gn]),
                writes=[wvn], stream="d_wv")
        for tl in range(L // 128):
            vb, vbn = vevac[tl % 2], f"vevac{tl % 2}"
            pn = "PF"
            for c in range(8):
                B.op("tensor", lambda e, c=c, tl=tl, stv=stv, gn=gn, wv=wv: e.matmul(
                    PF.bitcast(F32).ap()[:, 0:gn],
                    bass.AP(xnT_sb, c * TOK + 128 * tl * stv, [[8 * TOK, 128], [stv, 128]]),
                    wv.ap()[:, c, 0:gn], start=(c == 0), stop=(c == 7)),
                    reads=[wvn, "xnT_sb"] + ([pn] if c == 0 else []),
                    writes=[pn])
            B.op("scalar", lambda e, gn=gn, vb=vb: e.activation(
                out=vb.ap()[:, 0:gn], in_=PF.bitcast(F32).ap()[:, 0:gn], func=AF.Copy),
                reads=[pn], writes=[vbn])
            for hh in range(gh):
                h = h0 + hh
                B.op("sync", lambda e, h=h, hh=hh, tl=tl, vb=vb: e.dma_start(
                    out=agin_v[KV_OFF[h] + 128 * tl:KV_OFF[h] + 128 * (tl + 1), :],
                    in_=vb.ap()[:, 64 * hh:64 * (hh + 1)]),
                    reads=[vbn], writes=["agin_v"], stream="d_v")
                if debug_upto >= 2:
                    B.op("gpsimd", lambda e, h=h, hh=hh, tl=tl, vb=vb: e.dma_start(
                        out=dbg["v"][KV_OFF[h] + 128 * tl:KV_OFF[h] + 128 * (tl + 1), :],
                        in_=vb.ap()[:, 64 * hh:64 * (hh + 1)]), reads=[vbn], stream="d_dbg")

    B.op("gpsimd", lambda e: e.collective_compute(
        "AllGather", OP.bypass, replica_groups=[list(range(NCORES))],
        ins=[agin_kt.opt()], outs=[agout_kt.opt()]),
        reads=["agin_kt"], writes=["agout_kt"], sem=B.ccsem("scc1"))
    B.op("gpsimd", lambda e: e.collective_compute(
        "AllGather", OP.bypass, replica_groups=[list(range(NCORES))],
        ins=[agin_v.opt()], outs=[agout_v.opt()]),
        reads=["agin_v"], writes=["agout_v"], sem=B.ccsem("scc2"))
    for sgi in range(2):
        wq, wqn = wqsg[sgi % 2], f"wqsg{sgi % 2}"
        for c in range(8):
            B.op("gpsimd", lambda e, sgi=sgi, c=c, wq=wq: e.dma_start(
                out=wq.ap()[:, c, :],
                in_=wqkv_in[128 * c:128 * (c + 1), 512 * sgi:512 * (sgi + 1)]),
                writes=[wqn], stream="d_wq")
        for hh in range(8):
            h = 8 * sgi + hh
            qb, qbn = qth[h % 2], f"qth{h % 2}"
            for n in range(4):
                ptile = PC if n % 2 == 0 else PD
                pn = "PC" if n % 2 == 0 else "PD"
                for c in range(8):
                    B.op("tensor", lambda e, c=c, hh=hh, n=n, wq=wq, ptile=ptile: e.matmul(
                        ptile.ap()[0:64, :],
                        wq.ap()[:, c, 64 * hh:64 * (hh + 1)],
                        xnT_sb.ap()[:, c * TOK + 512 * n:c * TOK + 512 * (n + 1)],
                        start=(c == 0), stop=(c == 7)),
                        reads=[wqn, "xnT_sb"] + ([pn] if c == 0 else []),
                        writes=[pn])
                B.op("scalar", lambda e, h=h, n=n, qb=qb, ptile=ptile: e.activation(
                    out=qb.ap()[:, 512 * n:512 * (n + 1)],
                    in_=ptile.ap()[0:64, :], func=AF.Identity,
                    bias=bq64.ap()[:, h:h + 1]),
                    reads=[pn, "bq64"], writes=[qbn])
            B.op("sync", lambda e, h=h, qb=qb: e.dma_start(
                out=qt_spill[:, TOK * h:TOK * (h + 1)], in_=qb.ap()),
                reads=[qbn], writes=["qt_spill"], stream="d_qt")
            if debug_upto >= 2:
                B.op("gpsimd", lambda e, h=h, qb=qb: e.dma_start(
                    out=dbg["qt"][:, 128 * h:128 * (h + 1)], in_=qb.ap()[:, 0:128]),
                    reads=[qbn], stream="d_dbg")

    B.op("sync", lambda e: e.dma_start(out=cnt8_sb.ap(), in_=bass.AP(agout_cnt.tensor, 0, [[128 * NE, 8], [1, NE]])),
         reads=["agout_cnt"], writes=["cnt8_sb"], stream="d_cnt")
    B.op("tensor", lambda e: e.matmul(
        PE2.ap()[0:1, 0:16], prevmask.ap(), cnt8_sb.ap(), start=True, stop=True),
        reads=["prevmask", "cnt8_sb", "PE2"], writes=["PE2"])
    B.op("vector", lambda e: e.tensor_copy(base_sb.ap(), PE2.ap()[0:1, 0:16]),
         reads=["PE2"], writes=["base_sb"])
    if debug_upto >= 2:
        B.op("sync", lambda e: e.dma_start(out=dbg["tot"], in_=total_sb.ap()),
             reads=["total_sb"], stream="d_dbg2")
        B.op("sync", lambda e: e.dma_start(out=dbg["base"], in_=base_sb.ap()),
             reads=["base_sb"], stream="d_dbg2")
        B.op("sync", lambda e: e.dma_start(out=dbg["cnt8"], in_=cnt8_sb.ap()),
             reads=["cnt8_sb"], stream="d_dbg2")
    for t in range(NT):
        pn = "PB0"
        B.op("tensor", lambda e, t=t: e.matmul(
            PB.ap()[:, 0:16], ltinc.ap(), mask_sb.ap()[:, 16 * t:16 * (t + 1)],
            start=True, stop=False),
            reads=["ltinc", "mask_sb", pn], writes=[pn])
        B.op("tensor", lambda e, t=t: e.matmul(
            PB.ap()[:, 0:16], cpfx.ap()[:, 128 * t:128 * (t + 1)], counts_sb.ap(),
            start=False, stop=False), reads=["cpfx", "counts_sb"], writes=[pn])
        B.op("tensor", lambda e: e.matmul(
            PB.ap()[:, 0:16], onesrow.ap(), base_sb.ap(),
            start=False, stop=True), reads=["onesrow", "base_sb"], writes=[pn])
        if debug_upto >= 2 and t == 0:
            B.op("vector", lambda e: e.tensor_copy(t16b.ap(), PB.ap()[:, 0:16]),
                 reads=[pn], writes=["t16b"])
            B.op("sync", lambda e: e.dma_start(out=dbg["rank0"], in_=t16b.ap()),
                 reads=["t16b"], stream="d_dbg2")
        B.op("vector", lambda e: e.tensor_scalar(
            t16a.ap(), PB.ap()[:, 0:16], CAP + 0.5, None, OP.is_le),
            reads=[pn], writes=["t16a"])
        B.op("vector", lambda e, t=t: e.tensor_mul(
            t16b.ap(), mask_sb.ap()[:, 16 * t:16 * (t + 1)], t16a.ap()),
            reads=["mask_sb", "t16a"], writes=["t16b"])
        B.op("vector", lambda e: e.scalar_tensor_tensor(
            t16a.ap(), t16b.ap(), 1.0, PB.ap()[:, 0:16], OP.mult, OP.mult,
            accum_out=t1a.ap()),
            reads=["t16b", pn], writes=["t16a", "t1a"])
        B.op("vector", lambda e: e.reduce_sum(t1b.ap(), t16b.ap(), axis=AX.X),
             reads=["t16b"], writes=["t1b"])
        B.op("vector", lambda e: e.tensor_sub(t1c.ap(), t1a.ap(), t1b.ap()),
             reads=["t1a", "t1b"], writes=["t1c"])
        B.op("vector", lambda e, t=t: e.tensor_scalar(
            t1d.ap(), e_sb.ap()[:, t:t + 1], CAP * 1.0, None, OP.mult),
            reads=["e_sb"], writes=["t1d"])
        B.op("vector", lambda e: e.tensor_add(t1d.ap(), t1d.ap(), t1c.ap()),
             reads=["t1c"], writes=["t1d"])
        B.op("vector", lambda e: e.tensor_scalar(
            t1e.ap(), t1d.ap(), BIGIDX, None, OP.subtract),
            reads=["t1d"], writes=["t1e"])
        B.op("vector", lambda e, t=t: e.scalar_tensor_tensor(
            gidx_sb.ap()[:, t:t + 1], t1e.ap(), t1b.ap(), c9.ap(), OP.mult, OP.add),
            reads=["t1e", "t1b", "c9"], writes=["gidx_sb"])
        B.op("vector", lambda e, t=t: e.tensor_mul(
            w_sb.ap()[:, t:t + 1], wpre_sb.ap()[:, t:t + 1], t1b.ap()),
            reads=["wpre_sb", "t1b"], writes=["w_sb"])
    B.op("tensor", lambda e: e.transpose(
        PB.ap()[0:16, 0:128], gidx_sb.ap(), identf.ap()),
        reads=["gidx_sb", "identf", "PB0"], writes=["PB0"])
    B.op("vector", lambda e: e.tensor_copy(gxT_sb.ap(), PB.ap()[0:16, 0:128]),
         reads=["PB0"], writes=["gxT_sb"])
    B.op("sync", lambda e: e.dma_start(out=agin_gx, in_=gxT_sb.ap()),
         reads=["gxT_sb"], writes=["agin_gx"], stream="d_gx")
    if debug_upto >= 2:
        B.op("sync", lambda e: e.dma_start(out=dbg["gidx"], in_=gidx_sb.ap()),
             reads=["gidx_sb"], stream="d_dbg2")
        B.op("sync", lambda e: e.dma_start(out=dbg["w"], in_=w_sb.ap()),
             reads=["w_sb"], stream="d_dbg2")
    B.op("gpsimd", lambda e: e.collective_compute(
        "AllGather", OP.bypass, replica_groups=[list(range(NCORES))],
        ins=[agin_gx.opt()], outs=[agout_gx.opt()]),
        reads=["agin_gx"], writes=["agout_gx"], sem=B.ccsem("scc3"))

    REGS = {}

    def _mkregs(e):
        REGS["bnd_scat"] = e.alloc_register("bnd_scat")
        e.reg_mov(REGS["bnd_scat"], SLOTC - 1)
        REGS["bnd_gath"] = e.alloc_register("bnd_gath")
        e.reg_mov(REGS["bnd_gath"], NSLOT - 1)
    B.raw("gpsimd", _mkregs)

    B.op("sync", lambda e: e.dma_start(out=gxall.ap(), in_=agout_gx),
         reads=["agout_gx"], writes=["gxall"], stream="d_gxld")
    B.op("tensor", lambda e: e.transpose(
        PC.ap()[:, 0:128], gxall.ap(), identf.ap()),
        reads=["gxall", "identf", "PC"], writes=["PC"])
    B.op("vector", lambda e: e.tensor_copy(idxall_f.ap(), PC.ap()[:, 0:128]),
         reads=["PC"], writes=["idxall_f"])
    B.op("vector", lambda e: e.tensor_scalar(
        idxall_f.ap(), idxall_f.ap(), slotb.ap(), None, OP.subtract),
        reads=["idxall_f", "slotb"], writes=["idxall_f"])
    B.op("vector", lambda e: e.tensor_scalar(
        idxall_u.ap().bitcast(F32), idxall_f.ap(), 0.0, None, OP.is_lt),
        reads=["idxall_f"], writes=["idxall_u"])
    B.op("vector", lambda e: e.scalar_tensor_tensor(
        idxall_f.ap(), idxall_u.ap().bitcast(F32), 200000.0, idxall_f.ap(),
        OP.mult, OP.add),
        reads=["idxall_u", "idxall_f"], writes=["idxall_f"])
    B.op("vector", lambda e: e.tensor_copy(idxall_u.ap(), idxall_f.ap()),
         reads=["idxall_f"], writes=["idxall_u"])
    if debug_upto >= 5:
        B.op("sync", lambda e: e.dma_start(out=dbg["idx"], in_=idxall_f.ap()),
             reads=["idxall_f"], stream="d_dbg2")

    for sl in range(SLOTC // 128):
        B.op("sync", lambda e, sl=sl: e.dma_start(
            out=disp_dram[128 * sl:128 * (sl + 1), :], in_=zero_bf.ap()),
            reads=["zero_bf"], writes=["disp_dram"], stream="d_dz")
    for gt in range(S // 128):
        xgb, xgn = xg[gt % 2], f"xg{gt % 2}"
        B.op("sync", lambda e, gt=gt, xgb=xgb: e.dma_start(
            out=xgb.ap(), in_=agout_xn[128 * gt:128 * (gt + 1), :]),
            reads=["agout_xn"], writes=[xgn], stream="d_xg")
        B.op("gpsimd", lambda e, gt=gt, xgb=xgb: e.indirect_dma_start(
            out=disp_dram,
            out_offset=IndirectOffsetOnAxis(ap=idxall_u.ap()[:, gt:gt + 1], axis=0),
            in_=xgb.ap(), in_offset=None,
            bounds_check=REGS["bnd_scat"], oob_is_err=False),
            reads=[xgn, "idxall_u", "disp_dram"], writes=["disp_dram"],
            stream="d_scat")


    if upto_phase <= 2:
        return finish(B, nc, dict(locals()))

    B.free("xbuf0", "xbuf1", "xsq", "xn32", "xnbf", "xnT32", "xnT_sb",
           "wqsg0", "wqsg1", "qth0", "qth1", "wkh0", "wkh1", "kevac0", "kevac1",
           "wvg0", "wvg1", "vevac0", "vevac1")
    for nm in ["xbuf0", "xbuf1", "xsq", "xn32", "xnbf", "xnT32", "xnT_sb",
               "wqsg0", "wqsg1", "qth0", "qth1", "wkh0", "wkh1", "kevac0",
               "kevac1", "wvg0", "wvg1", "vevac0", "vevac1"]:
        pass

    ktb = [B.sbuf(f"ktb{i}", [64, S], BF16) for i in range(2)]
    vaugb = [B.sbuf(f"vaugb{i}", [128, 128 * 65], BF16) for i in range(2)]
    qtb = [B.sbuf(f"qtb{i}", [64, TOK], BF16) for i in range(2)]
    ptb = [B.sbuf(f"ptb{i}", [128, 1024], BF16) for i in range(3)]
    atmp = B.sbuf("atmp", [64, 512], BF16)
    atb = B.sbuf("atb", [64, TOK], BF16)
    rsum_sb = B.sbuf("rsum_sb", [128, TOK], F32)
    rsum0_sb = B.sbuf("rsum0_sb", [1, TOK], F32)
    recips = B.sbuf("recips", [1, TOK], F32)
    eshift = B.sbuf("eshift", [128, 1], F32)
    B.op("vector", lambda e: e.memset(eshift.ap(), EXP_SHIFT), writes=["eshift"])
    for i in range(2):
        B.op("vector", lambda e, i=i: e.memset(
            bass.AP(vaugb[i], 64, [[128 * 65, 128], [65, 128]]), 1.0),
            writes=[f"vaugb{i}_ones"])

    for h in range(NH):
        stv = STRIDES[h]
        L = S // stv
        Lloc = TOK // stv
        kt, ktn = ktb[h % 2], f"ktb{h % 2}"
        va, van = vaugb[h % 2], f"vaugb{h % 2}"
        qb, qbn = qtb[h % 2], f"qtb{h % 2}"
        B.op("sync", lambda e, h=h, kt=kt, Lloc=Lloc: e.dma_start(
            out=kt.ap()[:, 0:8 * Lloc],
            in_=bass.AP(agout_kt.tensor, KV_OFF[h],
                        [[KV_TOT, 64], [64 * KV_TOT, 8], [1, Lloc]])),
            reads=["agout_kt"], writes=[ktn], stream="d_kld")
        nck = Lloc // 128
        for rblk in range(8):
            B.op("sync", lambda e, h=h, va=va, nck=nck, rblk=rblk: e.dma_start(
                out=bass.AP(va, rblk * nck * 65, [[128 * 65, 128], [65, nck], [1, 64]]),
                in_=bass.AP(agout_v.tensor, 64 * (KV_OFF[h] + KV_TOT * rblk),
                            [[64, 128], [64 * 128, nck], [1, 64]])),
                reads=["agout_v", f"vaugb{h % 2}_ones"], writes=[van], stream="d_vld")
        B.op("sync", lambda e, h=h, qb=qb: e.dma_start(
            out=qb.ap(), in_=qt_spill[:, TOK * h:TOK * (h + 1)]),
            reads=["qt_spill"], writes=[qbn], stream="d_qld")
        NKC = L // 128
        NP = NKC // 2
        for qt in range(4):
            ab = PC if qt % 2 == 0 else PD
            abn = "PC" if qt % 2 == 0 else "PD"

            def emit_st(p):
                pr = PA if p % 2 == 0 else PB
                prn0 = ("PA0", "PA1") if p % 2 == 0 else ("PB0", "PB1")
                for half in range(2):
                    kc = 2 * p + half
                    B.op("tensor", lambda e, kc=kc, half=half, pr=pr, kt=kt, qb=qb, qt=qt: e.matmul(
                        pr.ap()[:, 512 * half:512 * (half + 1)],
                        kt.ap()[:, 128 * kc:128 * (kc + 1)],
                        qb.ap()[:, 512 * qt:512 * (qt + 1)], start=True, stop=True),
                        reads=[ktn, qbn, prn0[half]], writes=[prn0[half]])

            def emit_exp(p):
                pr = PA if p % 2 == 0 else PB
                prn = ("PA0", "PA1") if p % 2 == 0 else ("PB0", "PB1")
                B.op("scalar", lambda e, p=p, pr=pr: e.activation(
                    out=ptb[p % 3].ap(), in_=pr.ap(), func=AF.Exp,
                    bias=eshift.ap(), scale=EXP_SCALE),
                    reads=[prn[0], prn[1], "eshift"], writes=[f"ptb{p % 3}"])

            def emit_pv(p):
                for half in range(2):
                    kc = 2 * p + half
                    B.op("tensor", lambda e, kc=kc, half=half, p=p, va=va, ab=ab, NP=NP: e.matmul(
                        ab.ap()[0:65, 0:512], va.ap()[:, 65 * kc:65 * (kc + 1)],
                        ptb[p % 3].ap()[:, 512 * half:512 * (half + 1)],
                        start=(kc == 0), stop=(kc == 2 * NP - 1)),
                        reads=[van, f"ptb{p % 3}"] + ([abn] if kc == 0 else []),
                        writes=[abn])

            for p in range(NP + 2):
                if p < NP:
                    emit_st(p)
                    emit_exp(p)
                if p >= 2:
                    emit_pv(p - 2)
            B.op("vector", lambda e, qt=qt, ab=ab: e.tensor_copy(
                rsum_sb.ap()[64:65, 512 * qt:512 * (qt + 1)],
                ab.ap()[64:65, 0:512]),
                reads=[abn], writes=["rsum_sb"])
            B.op("sync", lambda e, qt=qt: e.dma_start(
                out=rsum0_sb.ap()[:, 512 * qt:512 * (qt + 1)],
                in_=rsum_sb.ap()[64:65, 512 * qt:512 * (qt + 1)]),
                reads=["rsum_sb"], writes=["rsum0_sb"], stream="d_rs")
            B.op("vector", lambda e, qt=qt: e.reciprocal(
                recips.ap()[:, 512 * qt:512 * (qt + 1)],
                rsum0_sb.ap()[:, 512 * qt:512 * (qt + 1)]),
                reads=["rsum0_sb"], writes=["recips"])
            B.op("tensor", lambda e, qt=qt: e.matmul(
                PE2.ap()[0:64, 0:512], onesrow.ap()[:, 0:64],
                recips.ap()[:, 512 * qt:512 * (qt + 1)], start=True, stop=True),
                reads=["recips", "onesrow", "PE2"], writes=["PE2"])
            B.op("vector", lambda e, qt=qt, ab=ab: e.tensor_copy(
                atmp.ap(), ab.ap()[0:64, 0:512]),
                reads=[abn], writes=["atmp"])
            B.op("vector", lambda e, qt=qt: e.tensor_mul(
                atb.ap()[:, 512 * qt:512 * (qt + 1)], atmp.ap(), PE2.ap()[0:64, 0:512]),
                reads=["atmp", "PE2"], writes=["atb"])
        B.op("sync", lambda e, h=h: e.dma_start(
            out=attn_spill[:, TOK * h:TOK * (h + 1)], in_=atb.ap()),
            reads=["atb"], writes=["attn_spill"], stream="d_attnst")
        if debug_upto >= 3:
            B.op("gpsimd", lambda e, h=h: e.dma_start(
                out=dbg["attn"][:, 128 * h:128 * (h + 1)], in_=atb.ap()[:, 0:128]),
                reads=["atb"], stream="d_dbg")

    if upto_phase <= 3:
        return finish(B, nc, dict(locals()))

    B.free("ktb0", "ktb1", "vaugb0", "vaugb1", "qtb0", "qtb1", "ptb0", "ptb1",
           "ptb2", "atmp", "atb", "rsum_sb", "rsum0_sb", "recips", "eshift")
    hT_sb = B.sbuf("hT_sb", [128, 8 * TOK], BF16)
    wout_sb = B.sbuf("wout_sb", [64, NH * 1024], BF16)
    attn_ld = [B.sbuf(f"attn_ld{i}", [64, NH * 512], BF16) for i in range(2)]
    xb2 = [B.sbuf(f"xb2_{i}", [128, D], F32) for i in range(2)]
    hpre = B.sbuf("hpre", [128, D], F32)
    hsq = B.sbuf("hsq", [128, D], F32)
    htile = B.sbuf("htile", [128, D], F32)

    for h in range(NH):
        B.op("gpsimd", lambda e, h=h: e.dma_start(
            out=wout_sb.ap()[:, 1024 * h:1024 * (h + 1)],
            in_=wout_in[64 * h:64 * (h + 1), :]),
            writes=["wout_sb"], stream="d_wout")

    def ln_stats(src, srcn, rstd_ap, negmr_ap, wr_r, wr_n):
        B.op("vector", lambda e: e.reduce_sum(sum1.ap(), src.ap(), axis=AX.X),
             reads=[srcn], writes=["sum1"])
        B.op("scalar", lambda e: e.activation(
            out=hsq.ap(), in_=src.ap(), func=AF.Square, accum_out=sumsq.ap()),
            reads=[srcn], writes=["hsq", "sumsq"])
        B.op("vector", lambda e: e.tensor_scalar(t1a.ap(), sum1.ap(), 1.0 / D, None, OP.mult),
             reads=["sum1"], writes=["t1a"])
        B.op("vector", lambda e: e.tensor_scalar(t1b.ap(), sumsq.ap(), 1.0 / D, None, OP.mult),
             reads=["sumsq"], writes=["t1b"])
        B.op("vector", lambda e: e.tensor_scalar(t1c.ap(), t1a.ap(), t1a.ap(), -1.0, OP.mult, OP.mult),
             reads=["t1a"], writes=["t1c"])
        B.op("vector", lambda e: e.tensor_add(t1d.ap(), t1b.ap(), t1c.ap()),
             reads=["t1b", "t1c"], writes=["t1d"])
        B.op("scalar", lambda e: e.activation(
            out=t1e.ap(), in_=t1d.ap(), func=AF.Sqrt, bias=epsc.ap()),
            reads=["t1d", "epsc"], writes=["t1e"])
        B.op("vector", lambda e: e.reciprocal(rstd_ap, t1e.ap()),
             reads=["t1e"], writes=[wr_r])
        B.op("vector", lambda e: e.tensor_scalar(
            negmr_ap, t1a.ap(), rstd_ap, -1.0, OP.mult, OP.mult),
            reads=["t1a", wr_r], writes=[wr_n])

    for tg in range(4):
        al, aln = attn_ld[tg % 2], f"attn_ld{tg % 2}"
        B.op("sync", lambda e, tg=tg, al=al: e.dma_start(
            out=al.ap(),
            in_=bass.AP(attn_spill.tensor, 512 * tg, [[NH * TOK, 64], [TOK, NH], [1, 512]])),
            reads=["attn_spill"], writes=[aln], stream="d_attnld")
        for tt in range(4):
            t = 4 * tg + tt
            xb, xbn = xb2[t % 2], f"xb2_{t % 2}"
            B.op("sync", lambda e, t=t, xb=xb: e.dma_start(
                out=xb.ap(), in_=x_in[128 * t:128 * (t + 1), :]),
                writes=[xbn], stream="d_x2")
            wp = PA if t % 2 == 0 else PB
            wpn = ("PA0", "PA1") if t % 2 == 0 else ("PB0", "PB1")
            for n in range(2):
                pn = wpn[n]
                for h in range(NH):
                    B.op("tensor", lambda e, h=h, n=n, tt=tt, al=al, wp=wp: e.matmul(
                        wp.ap()[:, 512 * n:512 * (n + 1)],
                        al.ap()[:, 512 * h + 128 * tt:512 * h + 128 * (tt + 1)],
                        wout_sb.ap()[:, 1024 * h + 512 * n:1024 * h + 512 * (n + 1)],
                        start=(h == 0), stop=(h == NH - 1)),
                        reads=[aln, "wout_sb"] + ([pn] if h == 0 else []),
                        writes=[pn])
            B.op("vector", lambda e, t=t, xb=xb: e.tensor_scalar(
                htile.ap(), xb.ap(), rstd1.ap()[:, t:t + 1], negmr1.ap()[:, t:t + 1],
                OP.mult, OP.add), reads=[xbn, "rstd1", "negmr1"], writes=["htile"])
            for n in range(2):
                B.op("vector", lambda e, n=n, wp=wp: e.tensor_add(
                    hpre.ap()[:, 512 * n:512 * (n + 1)], wp.ap()[:, 512 * n:512 * (n + 1)],
                    htile.ap()[:, 512 * n:512 * (n + 1)]),
                    reads=[wpn[n], "htile"], writes=["hpre"])
            ln_stats(hpre, "hpre", rstd2.ap(), negmr2.ap(), "rstd2", "negmr2")
            B.op("scalar", lambda e: e.activation(
                out=htile.ap(), in_=hpre.ap(), func=AF.Identity,
                bias=negmr2.ap(), scale=rstd2.ap()),
                reads=["hpre", "rstd2", "negmr2"], writes=["htile"])
            B.op("sync", lambda e, t=t: e.dma_start(
                out=h_spill[128 * t:128 * (t + 1), :], in_=htile.ap()),
                reads=["htile"], writes=["h_spill"], stream="d_h")
            if debug_upto >= 4:
                B.op("sync", lambda e, t=t: e.dma_start(
                    out=dbg["h"][128 * t:128 * (t + 1), :], in_=htile.ap()),
                    reads=["htile"], stream="d_dbg2")
            for c in range(8):
                ptile = PC if c % 2 == 0 else PD
                pnn = "PC" if c % 2 == 0 else "PD"
                B.op("tensor", lambda e, c=c, ptile=ptile: e.transpose(
                    ptile.ap()[:, 0:128],
                    htile.ap()[:, 128 * c:128 * (c + 1)], identf.ap()),
                    reads=["htile", "identf", pnn], writes=[pnn])
                B.op("scalar", lambda e, c=c, t=t, ptile=ptile: e.activation(
                    out=hT_sb.ap()[:, c * TOK + 128 * t: c * TOK + 128 * (t + 1)],
                    in_=ptile.ap()[:, 0:128], func=AF.Copy),
                    reads=[pnn], writes=["hT_sb"])

    B.free("wout_sb", "attn_ld0", "attn_ld1", "xb2_0", "xb2_1", "hpre", "hsq", "htile")
    w1_sb = B.sbuf("w1_sb", [128, 8, DFF], BF16)
    w2_sb = B.sbuf("w2_sb", [128, 32, D], BF16)
    zf = [B.sbuf(f"zf{i}", [128, 256], BF16) for i in range(2)]
    spre = B.sbuf("spre", [128, D], F32)
    hsq2 = B.sbuf("hsq2", [128, D], F32)
    hld = [B.sbuf(f"hld{i}", [128, D], F32) for i in range(2)]
    brtile = B.sbuf("brtile", [128, D], F32)

    for c in range(8):
        B.op("gpsimd", lambda e, c=c: e.dma_start(
            out=w1_sb.ap()[:, c, :], in_=w1_in[128 * c:128 * (c + 1), :]),
            writes=["w1_sb"], stream="d_w1")
    for f in range(32):
        B.op("gpsimd", lambda e, f=f: e.dma_start(
            out=w2_sb.ap()[:, f, :], in_=w2_in[128 * f:128 * (f + 1), :]),
            writes=["w2_sb"], stream="d_w2")

    def ln_stats2(src, srcn):
        B.op("vector", lambda e: e.reduce_sum(sum1.ap(), src.ap(), axis=AX.X),
             reads=[srcn], writes=["sum1"])
        B.op("scalar", lambda e: e.activation(
            out=hsq2.ap(), in_=src.ap(), func=AF.Square, accum_out=sumsq.ap()),
            reads=[srcn], writes=["hsq2", "sumsq"])
        B.op("vector", lambda e: e.tensor_scalar(t1a.ap(), sum1.ap(), 1.0 / D, None, OP.mult),
             reads=["sum1"], writes=["t1a"])
        B.op("vector", lambda e: e.tensor_scalar(t1b.ap(), sumsq.ap(), 1.0 / D, None, OP.mult),
             reads=["sumsq"], writes=["t1b"])
        B.op("vector", lambda e: e.tensor_scalar(t1c.ap(), t1a.ap(), t1a.ap(), -1.0, OP.mult, OP.mult),
             reads=["t1a"], writes=["t1c"])
        B.op("vector", lambda e: e.tensor_add(t1d.ap(), t1b.ap(), t1c.ap()),
             reads=["t1b", "t1c"], writes=["t1d"])
        B.op("scalar", lambda e: e.activation(
            out=t1e.ap(), in_=t1d.ap(), func=AF.Sqrt, bias=epsc.ap()),
            reads=["t1d", "epsc"], writes=["t1e"])
        B.op("vector", lambda e: e.reciprocal(rstd2.ap(), t1e.ap()),
             reads=["t1e"], writes=["rstd2"])
        B.op("vector", lambda e: e.tensor_scalar(
            negmr2.ap(), t1a.ap(), rstd2.ap(), -1.0, OP.mult, OP.mult),
            reads=["t1a", "rstd2"], writes=["negmr2"])

    for Bq in range(8):
        def emit_w1(f):
            pw = PE2 if f % 2 == 0 else PF.bitcast(F32)
            pwn = "PE2" if f % 2 == 0 else "PF"
            for c in range(8):
                B.op("tensor", lambda e, c=c, f=f, pw=pw, Bq=Bq: e.matmul(
                    pw.ap()[:, 0:256], w1_sb.ap()[:, c, 128 * f:128 * (f + 1)],
                    hT_sb.ap()[:, c * TOK + 256 * Bq:c * TOK + 256 * (Bq + 1)],
                    start=(c == 0), stop=(c == 7)),
                    reads=["w1_sb", "hT_sb"] + ([pwn] if c == 0 else []),
                    writes=[pwn])

        def emit_relu(f):
            pw = PE2 if f % 2 == 0 else PF.bitcast(F32)
            pwn = "PE2" if f % 2 == 0 else "PF"
            B.op("scalar", lambda e, f=f, pw=pw: e.activation(
                out=zf[f % 2].ap(), in_=pw.ap()[:, 0:256], func=AF.Relu,
                bias=b1sb.ap()[:, f:f + 1]),
                reads=[pwn, "b1sb"], writes=[f"zf{f % 2}"])

        def emit_w2(f):
            for tt in range(2):
                for n in range(2):
                    acc = PA if tt == 0 else PB
                    pa = ("PA0", "PA1")[n] if tt == 0 else ("PB0", "PB1")[n]
                    B.op("tensor", lambda e, f=f, tt=tt, n=n, acc=acc: e.matmul(
                        acc.ap()[:, 512 * n:512 * (n + 1)],
                        zf[f % 2].ap()[:, 128 * tt:128 * (tt + 1)],
                        w2_sb.ap()[:, f, 512 * n:512 * (n + 1)],
                        start=(f == 0), stop=(f == 31)),
                        reads=[f"zf{f % 2}", "w2_sb"] + ([pa] if f == 0 else []),
                        writes=[pa])

        for f in range(33):
            if f < 32:
                emit_w1(f)
                emit_relu(f)
            if f >= 1:
                emit_w2(f - 1)
        for tt in range(2):
            t = 2 * Bq + tt
            hb, hbn = hld[tt % 2], f"hld{tt % 2}"
            B.op("sync", lambda e, t=t, hb=hb: e.dma_start(
                out=hb.ap(), in_=h_spill[128 * t:128 * (t + 1), :]),
                reads=["h_spill"], writes=[hbn], stream="d_hld")
            for n in range(2):
                acc = PA if tt == 0 else PB
                pa = ("PA0", "PA1")[n] if tt == 0 else ("PB0", "PB1")[n]
                B.op("vector", lambda e, tt=tt, n=n, hb=hb, acc=acc: e.tensor_add(
                    spre.ap()[:, 512 * n:512 * (n + 1)],
                    acc.ap()[:, 512 * n:512 * (n + 1)],
                    hb.ap()[:, 512 * n:512 * (n + 1)]),
                    reads=[pa, hbn], writes=["spre"])
            ln_stats2(spre, "spre")
            B.op("scalar", lambda e: e.activation(
                out=brtile.ap(), in_=spre.ap(), func=AF.Identity,
                bias=negmr2.ap(), scale=rstd2.ap()),
                reads=["spre", "rstd2", "negmr2"], writes=["brtile"])
            B.op("sync", lambda e, t=t: e.dma_start(
                out=br_spill[128 * t:128 * (t + 1), :], in_=brtile.ap()),
                reads=["brtile"], writes=["br_spill"], stream="d_br")
            if debug_upto >= 4:
                B.op("sync", lambda e, t=t: e.dma_start(
                    out=dbg["br"][128 * t:128 * (t + 1), :], in_=brtile.ap()),
                    reads=["brtile"], stream="d_dbg2")

    if upto_phase <= 4:
        return finish(B, nc, dict(locals()))

    B.free("w1_sb", "w2_sb", "zf0", "zf1", "spre", "hsq2", "hld0", "hld1",
           "brtile", "hT_sb")
    dispbuf = B.sbuf("dispbuf", [128, D], BF16)
    dispT = B.sbuf("dispT", [128, 8, SLOTC], BF16)
    we1_sb = B.sbuf("we1_sb", [128, 8, DFF], BF16)
    we2f = [B.sbuf(f"we2f{i}", [128, D], BF16) for i in range(2)]
    h1T = B.sbuf("h1T", [128, 32, 256], BF16)
    eoev = [B.sbuf(f"eoev{i}", [128, 512], BF16) for i in range(2)]

    for sl in range(SLOTC // 128):
        B.op("sync", lambda e, sl=sl: e.dma_start(
            out=dispbuf.ap(), in_=disp_dram[128 * sl:128 * (sl + 1), :]),
            reads=["disp_dram"], writes=["dispbuf"], stream="d_dld")
        if debug_upto >= 5 and sl == 0:
            B.op("gpsimd", lambda e: e.dma_start(
                out=dbg["disp0"], in_=dispbuf.ap()),
                reads=["dispbuf"], stream="d_dbg")
        for c in range(8):
            ptile = PC if c % 2 == 0 else PD
            pnn = "PC" if c % 2 == 0 else "PD"
            B.op("tensor", lambda e, c=c, ptile=ptile: e.transpose(
                ptile.bitcast(BF16).ap()[:, 0:128],
                dispbuf.ap()[:, 128 * c:128 * (c + 1)], identb.ap()),
                reads=["dispbuf", "identb", pnn], writes=[pnn])
            B.op("scalar", lambda e, c=c, sl=sl, ptile=ptile: e.activation(
                out=dispT.ap()[:, c, 128 * sl:128 * (sl + 1)],
                in_=ptile.bitcast(BF16).ap()[:, 0:128], func=AF.Copy),
                reads=[pnn], writes=["dispT"])

    NSG = CAP // 256
    for ei in range(EPC):
        for c in range(8):
            B.op("gpsimd", lambda e, ei=ei, c=c: e.dma_start(
                out=we1_sb.ap()[:, c, :],
                in_=we1_in[ei, 128 * c:128 * (c + 1), :]),
                writes=["we1_sb"], stream="d_we1")
        for sg in range(NSG):
            soff = ei * CAP + 256 * sg

            def emit_ew1(f):
                pw = PE2 if f % 2 == 0 else PF.bitcast(F32)
                pwn = "PE2" if f % 2 == 0 else "PF"
                for c in range(8):
                    B.op("tensor", lambda e, c=c, f=f, pw=pw, soff=soff: e.matmul(
                        pw.ap()[:, 0:256],
                        we1_sb.ap()[:, c, 128 * f:128 * (f + 1)],
                        dispT.ap()[:, c, soff:soff + 256],
                        start=(c == 0), stop=(c == 7)),
                        reads=["we1_sb", "dispT"] + ([pwn] if c == 0 else []),
                        writes=[pwn])

            def emit_gelu(f):
                pw = PE2 if f % 2 == 0 else PF.bitcast(F32)
                pwn = "PE2" if f % 2 == 0 else "PF"
                B.op("scalar", lambda e, f=f, pw=pw: e.activation(
                    out=h1T.ap()[:, f, :], in_=pw.ap()[:, 0:256],
                    func=AF.Gelu_apprx_tanh),
                    reads=[pwn], writes=["h1T"])

            def emit_ew2(f):
                wf, wfn = we2f[f % 2], f"we2f{f % 2}"
                for tt in range(2):
                    for n in range(2):
                        acc = PA if tt == 0 else PB
                        pa = ("PA0", "PA1")[n] if tt == 0 else ("PB0", "PB1")[n]
                        B.op("tensor", lambda e, f=f, tt=tt, n=n, wf=wf, acc=acc: e.matmul(
                            acc.ap()[:, 512 * n:512 * (n + 1)],
                            h1T.ap()[:, f, 128 * tt:128 * (tt + 1)],
                            wf.ap()[:, 512 * n:512 * (n + 1)],
                            start=(f == 0), stop=(f == 31)),
                            reads=["h1T", wfn] + ([pa] if f == 0 else []),
                            writes=[pa])

            for f in range(32):
                B.op("gpsimd", lambda e, ei=ei, f=f, wf=we2f[f % 2]: e.dma_start(
                    out=wf.ap(), in_=we2_in[ei, 128 * f:128 * (f + 1), :]),
                    writes=[f"we2f{f % 2}"], stream="d_we2")
                emit_ew1(f)
                emit_gelu(f)
                if f >= 1:
                    emit_ew2(f - 1)
            emit_ew2(31)
            for tt in range(2):
                for n in range(2):
                    acc = PA if tt == 0 else PB
                    pa = ("PA0", "PA1")[n] if tt == 0 else ("PB0", "PB1")[n]
                    ev, evn = eoev[n % 2], f"eoev{n % 2}"
                    B.op("scalar", lambda e, tt=tt, n=n, ev=ev, acc=acc: e.activation(
                        out=ev.ap(), in_=acc.ap()[:, 512 * n:512 * (n + 1)],
                        func=AF.Copy), reads=[pa], writes=[evn])
                    B.op("sync", lambda e, soff=soff, tt=tt, n=n, ev=ev: e.dma_start(
                        out=agin_eo[soff + 128 * tt:soff + 128 * (tt + 1),
                                    512 * n:512 * (n + 1)],
                        in_=ev.ap()), reads=[evn], writes=["agin_eo"],
                        stream="d_eo")
                    if debug_upto >= 5 and sg == 0 and tt == 0:
                        B.op("gpsimd", lambda e, ei=ei, n=n, ev=ev: e.dma_start(
                            out=dbg["eo"][:, 1024 * ei + 512 * n:1024 * ei + 512 * (n + 1)],
                            in_=ev.ap()), reads=[evn], stream="d_dbg")
    B.op("gpsimd", lambda e: e.collective_compute(
        "AllGather", OP.bypass, replica_groups=[list(range(NCORES))],
        ins=[agin_eo.opt()], outs=[agout_eo.opt()]),
        reads=["agin_eo"], writes=["agout_eo"], sem=B.ccsem("scc5"))

    if upto_phase <= 5:
        return finish(B, nc, dict(locals()))

    B.free("gxall", "xg0", "xg1", "dispbuf", "dispT", "we1_sb", "we2f0", "we2f1",
           "h1T", "eoev0", "eoev1")
    gath = B.sbuf("gath", [128, D], BF16)
    xb3 = [B.sbuf(f"xb3_{i}", [128, D], F32) for i in range(2)]
    brb = [B.sbuf(f"brb{i}", [128, D], F32) for i in range(2)]
    sumt = B.sbuf("sumt", [128, D], F32)
    yb = B.sbuf("yb", [128, D], F32)

    B.op("vector", lambda e: e.memset(gath.ap(), 0.0), writes=["gath"])
    B.op("vector", lambda e: e.tensor_copy(gidxu.ap(), gidx_sb.ap()),
         reads=["gidx_sb"], writes=["gidxu"])
    for t in range(NT):
        xbb, xbn = xb3[t % 2], f"xb3_{t % 2}"
        bbb, bbn = brb[t % 2], f"brb{t % 2}"
        B.op("sync", lambda e, t=t, xbb=xbb: e.dma_start(
            out=xbb.ap(), in_=x_in[128 * t:128 * (t + 1), :]),
            writes=[xbn], stream="d_x3")
        B.op("sync", lambda e, t=t, bbb=bbb: e.dma_start(
            out=bbb.ap(), in_=br_spill[128 * t:128 * (t + 1), :]),
            reads=["br_spill"], writes=[bbn], stream="d_brl")
        B.op("gpsimd", lambda e, t=t: e.indirect_dma_start(
            out=gath.ap(), out_offset=None,
            in_=agout_eo,
            in_offset=IndirectOffsetOnAxis(ap=gidxu.ap()[:, t:t + 1], axis=0),
            bounds_check=REGS["bnd_gath"], oob_is_err=False),
            reads=["agout_eo", "gidxu", "gath"], writes=["gath"], stream="d_gath")
        B.op("vector", lambda e, xbb=xbb, bbb=bbb: e.tensor_add(
            sumt.ap(), xbb.ap(), bbb.ap()),
            reads=[xbn, bbn], writes=["sumt"])
        B.op("vector", lambda e, t=t: e.scalar_tensor_tensor(
            yb.ap(), gath.ap(), w_sb.ap()[:, t:t + 1], sumt.ap(), OP.mult, OP.add),
            reads=["gath", "w_sb", "sumt"], writes=["yb"])
        B.op("sync", lambda e, t=t: e.dma_start(
            out=y_out[128 * t:128 * (t + 1), :], in_=yb.ap()),
            reads=["yb"], writes=["y_out"], stream="d_y")

    return finish(B, nc, dict(locals()))


def finish(B, nc, state):
    for semname in list(B.sem):
        if B.cnt[semname] > 0:
            B.raw_wait("sync", semname, B.cnt[semname])
    nc_ = B.close()
    state["B"] = B
    return nc_, state


def make_inputs(full_inputs, core):
    x = np.asarray(full_inputs["L0_attn_x"]).reshape(S, D)
    bqkv = np.asarray(full_inputs["L0_attn_bqkv"])
    b1 = np.asarray(full_inputs["L0_attn_b1"])
    return {
        "x": np.ascontiguousarray(x[TOK * core:TOK * (core + 1)]),
        "wqkv": np.asarray(full_inputs["L0_attn_Wqkv"]),
        "bq64": np.ascontiguousarray(bqkv.reshape(48, 64).T),
        "wout": np.asarray(full_inputs["L0_attn_Wout"]),
        "w1": np.asarray(full_inputs["L0_attn_W1"]),
        "b1i": np.ascontiguousarray(b1.reshape(32, 128).T),
        "w2": np.asarray(full_inputs["L0_attn_W2"]),
        "gwT": np.ascontiguousarray(np.asarray(full_inputs["L0_ffn_gate_w"]).T),
        "we1": np.ascontiguousarray(np.asarray(full_inputs["L0_ffn_w1"])[EPC * core:EPC * (core + 1)]),
        "we2": np.ascontiguousarray(np.asarray(full_inputs["L0_ffn_w2"])[EPC * core:EPC * (core + 1)]),
        "identb": np.eye(128, dtype=ml_dtypes.bfloat16),
        "identf": np.eye(128, dtype=np.float32),
        "ltinc": np.triu(np.ones((128, 128), np.float32)),
        "oneshift": np.concatenate(
            [np.zeros((128, 127), np.float32), np.ones((128, 1), np.float32),
             np.zeros((128, 127), np.float32)], axis=1),
        "cpfx": np.concatenate(
            [np.tile((np.arange(128)[:, None] < t).astype(np.float32), (1, 128))
             for t in range(NT)], axis=1),
        "iota16": np.tile(np.arange(16, dtype=np.float32), (128, 1)),
        "iotarev": np.tile(16.0 - np.arange(16, dtype=np.float32), (128, 1)),
        "prevmask": (np.arange(8)[:, None] < core).astype(np.float32),
        "slotb": np.full((128, 1), SLOTC * core, np.float32),
    }


_NC_CACHE = {}


def kernel(**inputs):
    from concourse.bass_utils import run_bass_kernel_spmd
    if "nc" not in _NC_CACHE:
        nc, _state = build_nc(debug_upto=0, upto_phase=99)
        _NC_CACHE["nc"] = nc
    nc = _NC_CACHE["nc"]
    in_maps = [make_inputs(inputs, c) for c in range(NCORES)]
    res = run_bass_kernel_spmd(nc, in_maps, core_ids=list(range(NCORES)))
    y = np.concatenate([res.results[c]["y"] for c in range(NCORES)], axis=0)
    return y.reshape(1, S, D).astype(np.float32)
